# version 02
# baseline (speedup 1.0000x reference)
import math
from contextlib import ExitStack

import numpy as np
import ml_dtypes

import concourse.bass as bass
import concourse.mybir as mybir
from concourse.bass_utils import run_bass_kernel_spmd

F32 = mybir.dt.float32
BF16 = mybir.dt.bfloat16
AF = mybir.ActivationFunctionType
ALU = mybir.AluOpType
AX = mybir.AxisListType

NCORES = 8
D = 1024
T = 2048
NCH = D // 128
NEG = -1.0e30

ENGS = ("sync", "scalar", "vector", "gpsimd", "tensor")


class Sem:
    __slots__ = ("h", "n")

    def __init__(self, h):
        self.h = h
        self.n = 0


class KB:

    def __init__(self, nc, stack):
        self.nc = nc
        self.stack = stack
        self.prog = {e: [] for e in ENGS}
        self._n = 0
        self.pe = self.sem("m_pe")
        self.v = self.sem("m_v")
        self.s = self.sem("m_s")
        self.g = self.sem("m_g")
        self.ld = self.sem("q_ld")
        self.sx = self.sem("q_sx")
        self.dt = self.sem("q_dt")
        self.ms = {"tensor": self.pe, "vector": self.v, "scalar": self.s,
                   "gpsimd": self.g}

    def sem(self, name=None):
        if name is None:
            name = f"s{self._n}"
            self._n += 1
        return Sem(self.stack.enter_context(self.nc.semaphore(name)))

    def sbuf(self, name, shape, dt):
        return self.stack.enter_context(self.nc.sbuf_tensor(name, shape, dt))

    def psum(self, name, shape, dt=F32):
        return self.stack.enter_context(self.nc.psum_tensor(name, shape, dt))

    def rec(self, eng, fn):
        self.prog[eng].append(fn)

    def wait(self, eng, sem, val=None):
        v = sem.n if val is None else val
        if v > 0:
            self.rec(eng, lambda e, h=sem.h, v=v: e.wait_ge(h, v))

    def op(self, eng, fn, inc=None, inc_n=1):
        if inc is not None:
            inc.n += inc_n
            self.rec(eng, lambda e, fn=fn, h=inc.h, k=inc_n: fn(e).then_inc(h, k))
        else:
            self.rec(eng, fn)

    def mark(self, eng):
        m = self.ms[eng]
        self.op(eng, lambda e: e.nop(), inc=m)
        return m.n

    def dma(self, eng, out, in_, sem=None, **kw):
        sem = sem or {"gpsimd": self.ld, "sync": self.sx, "scalar": self.dt}[eng]
        sem.n += 16
        self.rec(eng, lambda e, o=out, i=in_, h=sem.h, kw=dict(kw):
                 e.dma_start(out=o, in_=i, **kw).then_inc(h, 16))
        return sem.n

    def dmaT(self, eng, out, in_, sem=None):
        sem = sem or {"sync": self.sx, "scalar": self.dt}[eng]
        sem.n += 16
        self.rec(eng, lambda e, o=out, i=in_, h=sem.h:
                 e.dma_start_transpose(o, i).then_inc(h, 16))
        return sem.n

    def finalize(self, out_waits):
        for s, v in out_waits:
            self.wait("sync", s, v)
        with self.nc.Block() as block:
            for eng in ENGS:
                ops = self.prog[eng]
                if ops:
                    getattr(block, eng)(lambda e, ops=ops: [f(e) for f in ops])


class PsTile:
    def __init__(self, kb, name, shape, dt=F32):
        self.kb = kb
        self.t = kb.psum(name, shape, dt)
        self.sem = kb.sem(name + "_f")
        self.uses = 0


class PsPool:
    def __init__(self, kb, n, shape=(128, 512), dt=F32, tag="ps"):
        self.tiles = [PsTile(kb, f"{tag}{i}", list(shape), dt) for i in range(n)]
        self.i = 0

    def next(self):
        t = self.tiles[self.i % len(self.tiles)]
        self.i += 1
        return t


def mm_evac(kb, pool, mms, evac_eng, evac_fn, extra_mm_waits=(), evac_inc=None):
    t = pool.next()
    if t.uses > 0:
        kb.wait("tensor", t.sem, t.uses)
    t.uses += 1
    for s, v in extra_mm_waits:
        kb.wait("tensor", s, v)
    for i, fn in enumerate(mms):
        kb.op("tensor", lambda e, fn=fn, t=t: fn(e, t.t),
              inc=kb.pe if i == len(mms) - 1 else None)
    kb.wait(evac_eng, kb.pe)
    m = kb.ms[evac_eng]
    m.n += 1
    kb.rec(evac_eng, lambda e, fn=evac_fn, t=t, h=m.h: fn(e, t.t).then_inc(h, 1))
    t.sem.n += 1
    kb.rec(evac_eng, lambda e, h=t.sem.h: e.nop().then_inc(h, 1))
    return t


class Rot:

    def __init__(self, views):
        self.views = list(views)
        self.pend = [[] for _ in self.views]
        self.i = 0

    def next(self):
        j = self.i % len(self.views)
        self.i += 1
        w = self.pend[j]
        self.pend[j] = []
        return self.views[j], j, w

    def done(self, j, sem, val):
        self.pend[j].append((sem, val))


def host_causal_masks():
    m = np.zeros((128, 640), np.float32)
    r = np.arange(128)
    m[:, 512:640] = np.where(r[None, :] > r[:, None], NEG, 0.0)
    return m


def host_window_masks():
    m = np.zeros((2, 128, 512), np.float32)
    m0 = np.zeros((2, 128, 512), np.float32)
    for p in range(2):
        for r in range(128):
            lo = 128 * p + r
            hi = lo + 256
            c = np.arange(512)
            bad = (c < lo) | (c > hi)
            m[p, r, bad] = NEG
            bad0 = bad | (c < 256)
            m0[p, r, bad0] = NEG
    return m, m0


def host_rotary_tables():
    n, dh, win = T, 64, 256
    half = np.arange(0, dh, 2, dtype=np.float64)
    inv_freq = 1.0 / 10000.0 ** (half / dh)
    t = np.arange(n, dtype=np.float64)
    freqs = t[:, None] * inv_freq[None, :]
    freqs = np.concatenate([freqs, freqs], axis=1)
    base = (half + 0.4 * dh) / (1.4 * dh)
    power = (t - n // 2) / (win // 2)
    scale = base[None, :] ** power[:, None]
    scale = np.concatenate([scale, scale], axis=1)
    cs, sn = np.cos(freqs), np.sin(freqs)
    A = (cs * scale).T
    B = (sn * scale).T
    Ai = (cs / scale).T
    Bi = (sn / scale).T
    out = [np.tile(x, (2, 1)).astype(np.float32) for x in (A, B, Ai, Bi)]
    return out


def host_rot_perm():
    P = np.zeros((128, 128), np.float32)
    for blk in range(2):
        o = blk * 64
        for dd in range(32):
            P[o + dd, o + dd + 32] = -1.0
        for dd in range(32, 64):
            P[o + dd, o + dd - 32] = 1.0
    return np.ascontiguousarray(P.T).astype(ml_dtypes.bfloat16)


def host_ehead():
    E = np.zeros((8, 16, 128), np.float32)
    for c in range(8):
        E[c, 2 * c, 0:64] = 1.0
        E[c, 2 * c + 1, 64:128] = 1.0
    return E


def host_latent_masks():
    m = np.zeros((16, 128, 64), np.float32)
    for qt in range(16):
        for r in range(128):
            q = 128 * qt + r
            m[qt, r, q // 32 + 1:] = NEG
    return m


def make_ctx(kb, n_f32scr=12):
    cx = type("Ctx", (), {})()
    cx.kb = kb
    cx.xbuf = kb.sbuf("xbuf", [128, NCH, T], F32)
    cx.bigA = kb.sbuf("bigA", [128, NCH, T], BF16)
    cx.bigB = kb.sbuf("bigB", [128, NCH, T], BF16)
    cx.bigC = kb.sbuf("bigC", [128, NCH, T], BF16)
    cx.f32scr = kb.sbuf("f32scr", [128, n_f32scr, 512], F32)
    cx.rows = kb.sbuf("lnrows", [2, 4 * 512], F32)
    cx.stt = kb.sbuf("stt", [128, 16], F32)
    cx.wrot = Rot([kb.sbuf(f"w{i}", [128, 8 * 128], BF16) for i in range(2)])
    cx.wbig = Rot([kb.sbuf("wbig", [128, 22 * 128], BF16)])
    cx.ones1_f32 = kb.sbuf("ones1", [128, 1], F32)
    cx.ones1_bf = kb.sbuf("ones1b", [128, 1], BF16)
    cx.cvals = kb.sbuf("cvals", [128, 4], F32)
    cx.ones_row_f32 = kb.sbuf("onesr", [1, 128], F32)
    cx.kdup = cx.f32scr[:, 0:2, :].bitcast(BF16).rearrange("p a t -> p (a t)").unsqueeze(1).transpose([0, 2, 1]) if False else \
        cx.f32scr[:, 0:2, :].bitcast(BF16).rearrange("p a t -> p (a t)").rearrange("p (a t) -> p a t", a=1)
    cx.vtm = cx.f32scr[:, 2, :].bitcast(BF16).rearrange("p (a b) -> p a b", b=64)
    cx.wv_sb = cx.f32scr[:, 3, 0:128].bitcast(BF16).rearrange("p (a b) -> p a b", b=64)
    cx.cmask = cx.f32scr[:, 6, 0:320].bitcast(BF16)
    cx.pp = PsPool(kb, 4, (128, 512), F32, "pp")
    cx.pps = PsPool(kb, 2, (1, 512), F32, "pst")
    cx.ppbc = PsPool(kb, 2, (128, 512), F32, "pbc")
    return cx


def emit_const_init(kb, cx):
    kb.op("vector", lambda e: e.memset(cx.ones1_f32[:, :], 1.0))
    kb.op("vector", lambda e: e.memset(cx.ones_row_f32[:, :], 1.0))
    kb.op("vector", lambda e: e.memset(cx.ones1_bf[:, :], 1.0))
    kb.op("vector", lambda e: e.memset(cx.cvals[:, 0:1], 1e-5), inc=kb.v)


def emit_ln(kb, cx, src3, t_lo, t_hi, dst3=None, dst_off=0, src_off=0,
            gamma_col=None, beta_col=None, resid3=None, n_ch=NCH, eps=1e-5,
            src_tl0=None, src_bf=False, resid_src3=None, rs_tl0=None):
    rows = cx.rows
    rv = lambda i: rows[0:1, 512 * i:512 * i + 512]
    mean, m2, msq, var = (rv(i) for i in range(4))
    std = msq
    a_ = var
    b_ = m2
    D_ = 128.0 * n_ch
    for tl in range(t_lo, t_hi, 512):
        w = 512
        sl = tl if src_tl0 is None else src_tl0 + (tl - t_lo)
        kb.wait("scalar", kb.g)
        kb.wait("scalar", kb.v)
        ts = cx.pps.next()
        ts2 = cx.pps.next()
        for t_ in (ts, ts2):
            if t_.uses > 0:
                kb.wait("tensor", t_.sem, t_.uses)
            t_.uses += 1
        pe_after = []
        for c in range(n_ch):
            sq = cx.f32scr[:, 8 + (c % 2), 0:w]
            if c >= 2:
                kb.wait("scalar", kb.pe, pe_after[c - 2])
            kb.op("scalar", lambda e, o=sq, i=src3[:, src_off + c, sl:sl + w]:
                  e.activation(o, i, AF.Square), inc=kb.s)
            kb.wait("tensor", kb.s)
            ones_l = cx.ones1_bf if src_bf else cx.ones1_f32
            kb.op("tensor", lambda e, o=ts.t[:, 0:w], l=ones_l[:, 0:1],
                  r=src3[:, src_off + c, sl:sl + w], s=(c == 0), p=(c == n_ch - 1):
                  e.matmul(o, l, r, start=s, stop=p))
            kb.op("tensor", lambda e, o=ts2.t[:, 0:w], l=cx.ones1_f32[:, 0:1],
                  r=sq, s=(c == 0), p=(c == n_ch - 1):
                  e.matmul(o, l, r, start=s, stop=p), inc=kb.pe)
            pe_after.append(kb.pe.n)
        kb.wait("vector", kb.pe)
        kb.op("vector", lambda e, o=mean, i=ts.t[0:1, 0:w]: e.tensor_scalar_mul(o, i, 1.0 / D_))
        kb.op("vector", lambda e, o=m2, i=ts2.t[0:1, 0:w]: e.tensor_scalar_mul(o, i, 1.0 / D_),
              inc=kb.v)
        ts.sem.n += 1
        ts2.sem.n += 1
        kb.rec("vector", lambda e, h=ts.sem.h: e.nop().then_inc(h, 1))
        kb.rec("vector", lambda e, h=ts2.sem.h: e.nop().then_inc(h, 1))
        kb.op("vector", lambda e, o=msq, i=mean: e.tensor_tensor(o, i, i, ALU.mult))
        kb.op("vector", lambda e, o=var, x=m2, y=msq: e.tensor_tensor(o, x, y, ALU.subtract),
              inc=kb.v)
        kb.wait("scalar", kb.v)
        kb.op("scalar", lambda e, o=std, i=var: e.activation(o, i, AF.Sqrt,
              bias=cx.cvals[0:1, 0:1]), inc=kb.s)
        kb.wait("vector", kb.s)
        kb.op("vector", lambda e, o=a_, i=std: e.reciprocal(o, i))
        kb.op("vector", lambda e, o=b_, x=mean, y=a_: e.tensor_tensor(o, x, y, ALU.mult),
              inc=kb.v)
        bca = cx.ppbc.next()
        bcb = cx.ppbc.next()
        for t_ in (bca, bcb):
            if t_.uses > 0:
                kb.wait("tensor", t_.sem, t_.uses)
            t_.uses += 1
        kb.wait("tensor", kb.v)
        kb.op("tensor", lambda e, o=bca.t[:, 0:w], l=cx.ones_row_f32[0:1, :], r=a_:
              e.matmul(o, l, r, start=True, stop=True))
        kb.op("tensor", lambda e, o=bcb.t[:, 0:w], l=cx.ones_row_f32[0:1, :], r=b_:
              e.matmul(o, l, r, start=True, stop=True), inc=kb.pe)
        kb.wait("vector", kb.pe)
        kb.wait("vector", kb.g)
        g_after = []
        for c in range(n_ch):
            u = cx.f32scr[:, 10 + (c % 2), 0:w]
            if resid3 is not None and c >= 2:
                kb.wait("vector", kb.g, g_after[c - 2])
            kb.op("vector", lambda e, o=u, i=src3[:, src_off + c, sl:sl + w],
                  b=bca.t[:, 0:w]: e.tensor_tensor(o, i, b, ALU.mult))
            last = (c == n_ch - 1)
            if resid3 is not None:
                u2 = cx.f32scr[:, 8 + (c % 2), 0:w]
                kb.op("vector", lambda e, o=u2, i=u, b=bcb.t[:, 0:w]:
                      e.tensor_tensor(o, i, b, ALU.subtract))
                u3 = u
                rsrc = resid3 if resid_src3 is None else resid_src3
                rl = tl if rs_tl0 is None else rs_tl0 + (tl - t_lo)
                kb.op("vector", lambda e, o=u3, i=u2, x=rsrc[:, c, rl:rl + w]:
                      e.tensor_tensor(o, i, x, ALU.add), inc=kb.v)
                kb.wait("gpsimd", kb.v)
                kb.op("gpsimd", lambda e, o=resid3[:, c, tl:tl + w], i=u3:
                      e.tensor_copy(o, i), inc=kb.g)
                g_after.append(kb.g.n)
            elif gamma_col is not None:
                u2 = cx.f32scr[:, 8 + (c % 2), 0:w]
                kb.op("vector", lambda e, o=u2, i=u, b=bcb.t[:, 0:w]:
                      e.tensor_tensor(o, i, b, ALU.subtract))
                kb.op("vector", lambda e, o=dst3[:, dst_off + c, tl:tl + w], i=u2,
                      g=gamma_col[:, c:c + 1], bb=beta_col[:, c:c + 1]:
                      e.tensor_scalar(o, i, g, bb, ALU.mult, ALU.add),
                      inc=kb.v if last else None)
            else:
                kb.op("vector", lambda e, o=dst3[:, dst_off + c, tl:tl + w], i=u,
                      b=bcb.t[:, 0:w]: e.tensor_tensor(o, i, b, ALU.subtract),
                      inc=kb.v if last else None)
        bca.sem.n += 1
        bcb.sem.n += 1
        kb.rec("vector", lambda e, h=bca.sem.h: e.nop().then_inc(h, 1))
        kb.rec("vector", lambda e, h=bcb.sem.h: e.nop().then_inc(h, 1))


def emit_linear(kb, cx, w_dram, nk, in3, in_off=0, out3=None, out_off=0,
                m0=0, nm=None, t_lo=0, t_hi=T, evac_eng="vector",
                evac_scale=None, evac_f32=False, evac_fn=None, w_ready=None,
                out_tl0=None):
    wview = w_dram.rearrange("(c p) m -> p c m", p=128)
    M = w_dram.shape[1]
    nm = nm if nm is not None else M // 128
    rot = cx.wrot if nk <= 8 else cx.wbig
    for mi in range(nm):
        wsl, wj, wpend = rot.next()
        for s, v in wpend:
            kb.wait("gpsimd", s, v)
        mcols = slice(128 * (m0 + mi), 128 * (m0 + mi) + 128)
        nld = kb.dma("gpsimd", wsl[:, 0:nk * 128].rearrange("p (c m) -> p c m", m=128),
                     wview[:, 0:nk, mcols])
        wslv = wsl[:, 0:nk * 128].rearrange("p (c m) -> p c m", m=128)
        for tl in range(t_lo, t_hi, 512):
            w = min(512, t_hi - tl)
            mms = []
            for kc in range(nk):
                mms.append(lambda e, pt, kc=kc, tl=tl, w=w, wslv=wslv:
                           e.matmul(pt[:, 0:w], wslv[:, kc, :],
                                    in3[:, in_off + kc, tl:tl + w],
                                    start=(kc == 0), stop=(kc == nk - 1)))
            otl = tl if out_tl0 is None else out_tl0 + (tl - t_lo)
            dst = (out3[:, out_off + mi, otl:otl + w] if out3 is not None else None)
            if evac_fn is not None:
                ev = lambda e, pt, mi=mi, tl=tl, w=w, dst=dst: evac_fn(e, pt, mi, tl, w, dst)
            elif evac_scale is not None:
                ev = lambda e, pt, dst=dst, w=w: e.tensor_scalar_mul(dst, pt[:, 0:w], evac_scale)
            else:
                ev = lambda e, pt, dst=dst, w=w: e.tensor_copy(dst, pt[:, 0:w])
            waits = [(kb.ld, nld)]
            if w_ready is not None:
                if isinstance(w_ready, list):
                    waits.extend(w_ready)
                else:
                    waits.append(w_ready)
            tile = mm_evac(kb, cx.pp, mms, evac_eng, ev, extra_mm_waits=waits)
        rot.done(wj, kb.pe, kb.pe.n)


def emit_attn_causal(kb, cx, q3, kdup, vtm, att3, masks_sb):
    smrot = Rot([(cx.bigB, 0), (cx.bigB, 1)])
    perot = Rot([(cx.bigB, 2), (cx.bigB, 3)])
    pnrot = Rot([(cx.bigB, 4), (cx.bigB, 5)])
    ptrot = Rot([(cx.bigB, 6), (cx.bigB, 7)])
    nmrot = Rot(list(range(4)))
    suro = Rot(list(range(4, 8)))
    rsro = Rot(list(range(8, 12)))
    stc = lambda j: cx.stt[:, j:j + 1]
    for h in range(16):
        pb = (h % 2) * 64
        ch = h // 2
        for qt in range(16):
            span = 128 * (qt + 1)
            nkc = (qt + 4) // 4
            (smb, smc), smj, smw = smrot.next()
            sm = lambda a, b, smb=smb, smc=smc: smb[:, smc, a:b]
            qsl = q3[pb:pb + 64, ch, 128 * qt:128 * qt + 128]
            for kc in range(nkc):
                w = min(512, span - 512 * kc)
                diag = (kc == nkc - 1)
                mms = [lambda e, pt, w=w, kc=kc, qsl=qsl, pb=pb:
                       e.matmul(pt[:, 0:w], qsl,
                                kdup[pb:pb + 64, 0, 512 * kc:512 * kc + w],
                                start=True, stop=True)]
                if diag:
                    ev = (lambda e, pt, w=w, kc=kc, sm=sm:
                          e.scalar_tensor_tensor(sm(512 * kc, 512 * kc + w),
                                                 pt[:, 0:w], 1.0,
                                                 masks_sb[:, 640 - w:640],
                                                 ALU.mult, ALU.add))
                else:
                    ev = (lambda e, pt, w=w, kc=kc, sm=sm:
                          e.tensor_copy(sm(512 * kc, 512 * kc + w), pt[:, 0:w]))
                if kc == 0:
                    for s_, v_ in smw:
                        kb.wait("vector", s_, v_)
                mm_evac(kb, cx.pp, mms, "vector", ev)
            nmj0, nmj, nmw = nmrot.next()
            for s_, v_ in nmw:
                kb.wait("vector", s_, v_)
            kb.op("vector", lambda e, o=stc(nmj0), a=0, b=span, sm=sm:
                  e.reduce_max(o, sm(a, b), axis=AX.X, negate=True), inc=kb.v)
            suj0, suj, suw = suro.next()
            (peb, pec), pej, pew = perot.next()
            kb.wait("scalar", kb.v)
            for s_, v_ in pew + suw:
                kb.wait("scalar", s_, v_)
            kb.op("scalar", lambda e, o=peb[:, pec, 0:span], a=0, b=span,
                  bi=stc(nmj0), ac=stc(suj0), sm=sm:
                  e.activation(o, sm(a, b), AF.Exp, bias=bi, accum_out=ac),
                  inc=kb.s)
            smrot.done(smj, kb.s, kb.s.n)
            nmrot.done(nmj, kb.s, kb.s.n)
            rsj0, rsj, rsw = rsro.next()
            kb.wait("vector", kb.s)
            for s_, v_ in rsw:
                kb.wait("vector", s_, v_)
            kb.op("vector", lambda e, o=stc(rsj0), i=stc(suj0):
                  e.reciprocal(o, i), inc=kb.v)
            suro.done(suj, kb.v, kb.v.n)
            (pnb, pnc), pnj, pnw = pnrot.next()
            kb.wait("gpsimd", kb.v)
            for s_, v_ in pnw:
                kb.wait("gpsimd", s_, v_)
            kb.op("gpsimd", lambda e, o=pnb[:, pnc, 0:span], i=peb[:, pec, 0:span],
                  r=stc(rsj0): e.tensor_scalar_mul(o, i, r), inc=kb.g)
            perot.done(pej, kb.g, kb.g.n)
            rsro.done(rsj, kb.g, kb.g.n)
            (ptb, ptc), ptj, ptw = ptrot.next()
            kb.wait("scalar", kb.g)
            for s_, v_ in ptw:
                kb.wait("scalar", s_, v_)
            nkb = span // 128
            ndt = kb.dmaT("scalar",
                          ptb[:, ptc, 0:span].rearrange("p (kb q) -> p kb q", q=128),
                          pnb[:, pnc, 0:span].rearrange("q (kb p) -> q kb p", p=128))
            pnrot.done(pnj, kb.dt, ndt)
            mms = []
            for kb_ in range(nkb):
                mms.append(lambda e, pt, kb_=kb_, ptb=ptb, ptc=ptc, nkb=nkb:
                           e.matmul(pt[0:64, 0:128], vtm[:, kb_, :],
                                    ptb[:, ptc, 128 * kb_:128 * kb_ + 128],
                                    start=(kb_ == 0), stop=(kb_ == nkb - 1)))
            ev = (lambda e, pt, pb=pb, ch=ch, qt=qt:
                  e.copy(att3[pb:pb + 64, ch, 128 * qt:128 * qt + 128],
                         pt[0:64, 0:128]))
            mm_evac(kb, cx.pp, mms, "scalar", ev,
                    extra_mm_waits=[(kb.dt, ndt)])
            ptrot.done(ptj, kb.pe, kb.pe.n)


def emit_l0_attn(kb, cx, W):
    lat = cx.bigB
    emit_ln(kb, cx, cx.xbuf, 0, T, dst3=cx.bigA)
    kb.wait("tensor", kb.v)
    emit_linear(kb, cx, W["wcd"], 8, cx.bigA, out3=lat, nm=4)
    kb.wait("tensor", kb.v)
    emit_linear(kb, cx, W["wqu"], 4, lat, out3=cx.bigC, nm=8)
    emit_linear(kb, cx, W["wkk"], 4, lat, out3=cx.kdup, nm=1, evac_scale=0.125)
    nld = kb.dma("gpsimd", cx.wv_sb[:, :, :],
                 W["wv"].rearrange("(c p) m -> p c m", p=128))
    kb.wait("tensor", kb.v)
    for tt in range(16):
        mms = []
        for kc in range(4):
            mms.append(lambda e, pt, kc=kc, tt=tt:
                       e.matmul(pt[:, 0:64], lat[:, kc, 128 * tt:128 * tt + 128],
                                cx.wv_sb[:, kc, :], start=(kc == 0), stop=(kc == 3)))
        ev = lambda e, pt, tt=tt: e.tensor_copy(cx.vtm[:, tt, :], pt[:, 0:64])
        mm_evac(kb, cx.pp, mms, "vector", ev,
                extra_mm_waits=[(kb.ld, nld)] if tt == 0 else ())
    kb.wait("tensor", kb.v)
    emit_attn_causal(kb, cx, cx.bigC, cx.kdup, cx.vtm, cx.bigA, cx.cmask)
    kb.wait("tensor", kb.s)
    for tl in range(0, T, 512):
        emit_linear(kb, cx, W["wcp"], 8, cx.bigA, nm=8, t_lo=tl, t_hi=tl + 512,
                    w_ready=(kb.s, kb.s.n),
                    evac_fn=lambda e, pt, mi, tl_, w, dst:
                    e.tensor_copy(cx.f32scr[:, mi, 0:w], pt[:, 0:w]))
        kb.wait("scalar", kb.v)
        kb.wait("gpsimd", kb.v)
        emit_ln(kb, cx, cx.f32scr, tl, tl + 512, resid3=cx.xbuf, src_tl0=0)


def build_l0a():
    nc = bass.Bass("TRN2", target_bir_lowering=False, debug=False)
    xT = nc.dram_tensor("xT", [D, T], F32, kind="ExternalInput").ap()
    W = {}
    W["wcd"] = nc.dram_tensor("wcd", [D, 512], F32, kind="ExternalInput").ap()
    W["wqu"] = nc.dram_tensor("wqu", [512, D], F32, kind="ExternalInput").ap()
    W["wkk"] = nc.dram_tensor("wkk", [512, 128], F32, kind="ExternalInput").ap()
    W["wv"] = nc.dram_tensor("wv", [512, 64], F32, kind="ExternalInput").ap()
    W["wcp"] = nc.dram_tensor("wcp", [D, D], F32, kind="ExternalInput").ap()
    cm = nc.dram_tensor("cm", [128, 640], F32, kind="ExternalInput").ap()
    out = nc.dram_tensor("xo", [D, T], F32, kind="ExternalOutput").ap()
    with ExitStack() as stack:
        kb = KB(nc, stack)
        cx = make_ctx(kb)
        emit_const_init(kb, cx)
        kb.dma("sync", cx.xbuf[:, :, :], xT.rearrange("(c p) t -> p c t", p=128))
        kb.dma("gpsimd", cx.cmask[:, :], cm)
        for eng in ("scalar", "tensor", "vector", "gpsimd"):
            kb.wait(eng, kb.sx)
        kb.wait("vector", kb.ld)
        kb.wait("tensor", kb.v)
        emit_l0_attn(kb, cx, W)
        kb.wait("sync", kb.g)
        kb.dma("sync", out.rearrange("(c p) t -> p c t", p=128), cx.xbuf[:, :, :])
        kb.finalize([(kb.sx, kb.sx.n)])
    return nc


def l0a_inputs(x_b_T, inp):
    wk = inp["L0_attn_Wkv_up"][:, :64]
    return {
        "xT": np.ascontiguousarray(x_b_T, dtype=np.float32),
        "wcd": np.ascontiguousarray(inp["L0_attn_Wc_down"], np.float32),
        "wqu": np.ascontiguousarray(inp["L0_attn_Wq_up"], np.float32),
        "wkk": np.ascontiguousarray(np.concatenate([wk, wk], axis=1), np.float32),
        "wv": np.ascontiguousarray(inp["L0_attn_Wkv_up"][:, 64:], np.float32),
        "wcp": np.ascontiguousarray(inp["L0_attn_Wc_proj"], np.float32),
        "cm": host_causal_masks(),
    }


def _to_f32(a):
    return np.ascontiguousarray(np.asarray(a), np.float32)


def kernel_unfused(**inputs):
    x = _to_f32(inputs["L0_attn_x"])
    cores = list(range(NCORES))

    nc1 = build_l0a()
    maps1 = [l0a_inputs(x[b].T, inputs) for b in range(NCORES)]
    r1 = run_bass_kernel_spmd(nc1, maps1, core_ids=cores)
    x1T = [np.ascontiguousarray(r1.results[b]["xo"]) for b in range(NCORES)]

    nc2 = build_l0f()
    base2 = hostf_l0ffn_consts(inputs)
    maps2 = [{**base2, "xT": x1T[b]} for b in range(NCORES)]
    r2 = run_bass_kernel_spmd(nc2, maps2, core_ids=cores)
    x2T = [np.ascontiguousarray(r2.results[b]["xo"]) for b in range(NCORES)]

    nc3 = build_l1a()
    base3 = host_l1a_consts(inputs)
    maps3 = [{**base3, "xT": x2T[b]} for b in range(NCORES)]
    r3 = run_bass_kernel_spmd(nc3, maps3, core_ids=cores)
    x3T = [np.ascontiguousarray(r3.results[b]["xo"]) for b in range(NCORES)]

    nc4 = build_l1f()
    base4 = host_l1f_consts(inputs)
    maps4 = []
    for j in range(NCORES):
        xj = np.concatenate([x3T[b][:, 256 * j:256 * (j + 1)] for b in range(NCORES)],
                            axis=1)
        maps4.append({**base4, "xT": np.ascontiguousarray(xj)})
    r4 = run_bass_kernel_spmd(nc4, maps4, core_ids=cores)

    out = np.empty((8, T, D), np.float32)
    for j in range(NCORES):
        oj = r4.results[j]["xo"].T.reshape(8, 256, D)
        out[:, 256 * j:256 * (j + 1), :] = oj
    return out


def hostf_l0ffn_consts(inp):
    DI = 2730
    DIP = 2816
    W1 = np.asarray(inp["L0_ffn_W1"], np.float32)
    b1 = np.asarray(inp["L0_ffn_b1"], np.float32)
    W2 = np.asarray(inp["L0_ffn_W2"], np.float32)
    W1p = np.zeros((6, 1024, 2 * DIP), np.float32)
    W1p[:, :, 0:DI] = W1[:, :, 0:DI]
    W1p[:, :, DIP:DIP + DI] = W1[:, :, DI:]
    b1p = np.zeros((6, 2 * DIP), np.float32)
    b1p[:, 0:DI] = b1[:, 0:DI]
    b1p[:, DIP:DIP + DI] = b1[:, DI:]
    W2p = np.zeros((6, DIP, 1024), np.float32)
    W2p[:, 0:DI, :] = W2
    qs = np.asarray(inp["L0_ffn_q_scale"], np.float32)
    ks = np.asarray(inp["L0_ffn_k_scale"], np.float32)
    qscol = np.stack([np.tile(qs, (1, 2)), np.tile(ks, (1, 2))], axis=2)
    lng = np.asarray(inp["L0_ffn_ln_g"], np.float32).reshape(6, 8, 128).transpose(0, 2, 1).copy()
    lnb = np.asarray(inp["L0_ffn_ln_b"], np.float32).reshape(6, 8, 128).transpose(0, 2, 1).copy()
    b1col = b1p.reshape(6, 44, 128).transpose(0, 2, 1).copy()
    b2col = np.asarray(inp["L0_ffn_b2"], np.float32).reshape(6, 8, 128).transpose(0, 2, 1).copy()
    A, B, Ai, Bi = host_rotary_tables()
    return {
        "f_wqkv": np.ascontiguousarray(inp["L0_ffn_Wqkv"], np.float32),
        "f_wo": np.ascontiguousarray(inp["L0_ffn_Wo"], np.float32),
        "f_w1": W1p, "f_w2": W2p,
        "f_qscol": qscol, "f_lng": lng, "f_lnb": lnb,
        "f_b1col": b1col, "f_b2col": b2col,
        "f_tbl": np.stack([A, B, 8.0 * Ai, 8.0 * Bi]),
        "f_wmask": np.stack(list(host_window_masks()[0]) + list(host_window_masks()[1])),
        "f_ptc": host_rot_perm().astype(np.float32),
        "f_e2c": np.concatenate([np.repeat([[1.0, 0.0]], 64, 0),
                                 np.repeat([[0.0, 1.0]], 64, 0)]).astype(np.float32),
        "f_e2t": np.concatenate([np.repeat([[1.0, 0.0]], 64, 0),
                                 np.repeat([[0.0, 1.0]], 64, 0)]).astype(np.float32).T.copy(),
    }


def declare_l0ffn(nc):
    W = {}
    W["f_wqkv"] = nc.dram_tensor("f_wqkv", [6, D, 3072], F32, kind="ExternalInput").ap()
    W["f_wo"] = nc.dram_tensor("f_wo", [6, D, D], F32, kind="ExternalInput").ap()
    W["f_w1"] = nc.dram_tensor("f_w1", [6, D, 5632], F32, kind="ExternalInput").ap()
    W["f_w2"] = nc.dram_tensor("f_w2", [6, 2816, D], F32, kind="ExternalInput").ap()
    W["f_qscol"] = nc.dram_tensor("f_qscol", [6, 128, 2], F32, kind="ExternalInput").ap()
    W["f_lng"] = nc.dram_tensor("f_lng", [6, 128, 8], F32, kind="ExternalInput").ap()
    W["f_lnb"] = nc.dram_tensor("f_lnb", [6, 128, 8], F32, kind="ExternalInput").ap()
    W["f_b1col"] = nc.dram_tensor("f_b1col", [6, 128, 44], F32, kind="ExternalInput").ap()
    W["f_b2col"] = nc.dram_tensor("f_b2col", [6, 128, 8], F32, kind="ExternalInput").ap()
    W["f_tbl"] = nc.dram_tensor("f_tbl", [4, 128, T], F32, kind="ExternalInput").ap()
    W["f_wmask"] = nc.dram_tensor("f_wmask", [4, 128, 512], F32, kind="ExternalInput").ap()
    W["f_ptc"] = nc.dram_tensor("f_ptc", [128, 128], F32, kind="ExternalInput").ap()
    W["f_e2c"] = nc.dram_tensor("f_e2c", [128, 2], F32, kind="ExternalInput").ap()
    W["f_e2t"] = nc.dram_tensor("f_e2t", [2, 128], F32, kind="ExternalInput").ap()
    return W


def make_ctx_ffn(kb, cx):
    cx.tbl = kb.sbuf("tbl", [128, 2, 1024], BF16)
    cx.wmask = cx.f32scr[:, 4:6, :].bitcast(BF16).rearrange("p a t -> p (a t)").rearrange("p (a t) -> p a t", a=4)
    cx.ptc = kb.sbuf("ptc", [128, 128], BF16)
    cx.qtmp = cx.bigB[:, 4, :].rearrange("p (a t) -> p a t", t=1024)
    cx.khalo = cx.f32scr[:, 0:2, :].bitcast(BF16).rearrange("p a t -> p (a t)").rearrange("p (a t) -> p a t", a=8)
    cx.vhalo = cx.f32scr[:, 2:4, :].bitcast(BF16).rearrange("p a t -> p (a t)").rearrange("p (a t) -> p a t", a=2)
    cx.xsv = kb.sbuf("xsv", [128, 8, 4], F32)
    cx.hrows = cx.rows[0:2, 0:1024]
    cx.rotm = kb.sbuf("rotm", [128, 16], F32)
    cx.e2c = kb.sbuf("e2c", [128, 2], F32)
    cx.e2t = kb.sbuf("e2t", [2, 128], F32)
    cx.colc = kb.sbuf("colc", [128, 64], F32)
    cx.colc2 = kb.sbuf("colc2", [128, 8], F32)
    cx.xs = cx.bigA[:, :, 0:1024]
    cx.vfm = cx.bigA[:, :, 1024:2048]
    cx.att_f = cx.bigA[:, :, 1024:2048]
    cx.q3 = cx.bigC[:, :, 0:1024]
    cx.k3 = cx.bigC[:, :, 1024:2048]
    cx.vtm_f = cx.bigB[:, 0:4, :].rearrange("p a (b t) -> p (a b) t", t=1024)
    s4 = cx.bigB[:, 4, :]
    s5 = cx.bigB[:, 5, :]
    cx.w_sm = [s4[:, 0:512], s4[:, 512:1024]]
    cx.w_pe = [s4[:, 1024:1536], s4[:, 1536:2048]]
    cx.w_pn = [s5[:, 0:512], s5[:, 512:1024]]
    cx.w_pt = [s5[:, 1024:1536], s5[:, 1536:2048]]
    cx.xs2 = cx.bigC[:, 0:2, :].rearrange("p a (b t) -> p (a b) t", t=512)
    cx.xn = cx.bigC[:, 2:4, :].rearrange("p a (b t) -> p (a b) t", t=512)
    cx.ggel = cx.bigA[:, 0:6, :].rearrange("p a (b t) -> p (a b) t", t=512)
    cx.uge = cx.bigB[:, 0:6, :].rearrange("p a (b t) -> p (a b) t", t=512)
    return cx


def emit_ffn_init(kb, cx, W):
    kb.dma("gpsimd", cx.ptc[:, :], W["f_ptc"])
    kb.dma("gpsimd", cx.e2c[:, :], W["f_e2c"])
    kb.dma("gpsimd", cx.e2t[:, :], W["f_e2t"])
    kb.dma("gpsimd", cx.wmask[:, :, :], W["f_wmask"].rearrange("m p t -> p m t"))


def emit_shift(kb, cx, dst3, gl0, gl1, sv_col=0, eng="gpsimd"):
    w = gl1 - gl0
    m = kb.ms[eng]
    for c in range(4):
        kb.op(eng, lambda e, o=dst3[:, c, 0:w], i=cx.xbuf[:, c, gl0:gl1]:
              e.tensor_copy(o, i))
    for c in range(4, 8):
        kb.op(eng, lambda e, o=dst3[:, c, 1:w], i=cx.xbuf[:, c, gl0:gl1 - 1]:
              e.tensor_copy(o, i))
        if gl0 == 0:
            kb.op(eng, lambda e, o=dst3[:, c, 0:1]: e.memset(o, 0.0),
                  inc=m if c == 7 else None)
        else:
            kb.op(eng, lambda e, o=dst3[:, c, 0:1],
                  i=cx.xsv[:, c, sv_col:sv_col + 1]:
                  e.tensor_copy(o, i), inc=m if c == 7 else None)


def emit_rotary(kb, cx, x3, blk_col, tblA, tblB):
    sqw = cx.f32scr[:, 8:10, :].rearrange("p a t -> p (a t)")
    n2 = cx.hrows[:, 0:512]
    nr = cx.hrows[:, 512:1024]
    for c in range(NCH):
        kb.wait("scalar", kb.v)
        kb.wait("scalar", kb.g)
        kb.op("scalar", lambda e, o=sqw[:, 0:1024], i=x3[:, c, :]:
              e.activation(o, i, AF.Square), inc=kb.s)
        for n in range(2):
            mm = [lambda e, pt, n=n: e.matmul(pt[0:2, 0:512], blk_col,
                                              sqw[:, 512 * n:512 * n + 512],
                                              start=True, stop=True)]
            ev = lambda e, pt: e.tensor_copy(n2[:, 0:512], pt[0:2, 0:512])
            mm_evac(kb, cx.pp, mm, "vector", ev,
                    extra_mm_waits=[(kb.s, kb.s.n), (kb.s, kb.s.n)])
            kb.wait("scalar", kb.v)
            kb.op("scalar", lambda e, o=nr[:, 0:512], i=n2[:, 0:512]:
                  e.activation(o, i, AF.Sqrt), inc=kb.s)
            kb.wait("vector", kb.s)
            kb.op("vector", lambda e, o=n2[:, 0:512], i=nr[:, 0:512]:
                  e.tensor_scalar_max(o, i, 1e-12))
            kb.op("vector", lambda e, o=nr[:, 0:512], i=n2[:, 0:512]:
                  e.reciprocal(o, i), inc=kb.v)
            mm = [lambda e, pt: e.matmul(pt[:, 0:512], cx.e2t[:, :],
                                         nr[:, 0:512],
                                         start=True, stop=True)]
            ev = (lambda e, pt, n=n, c=c:
                  e.scalar_tensor_tensor(cx.qtmp[:, 0, 512 * n:512 * n + 512],
                                         x3[:, c, 512 * n:512 * n + 512], 1.0,
                                         pt[:, 0:512], ALU.mult, ALU.mult))
            mm_evac(kb, cx.pp, mm, "vector", ev,
                    extra_mm_waits=[(kb.v, kb.v.n), (kb.g, kb.g.n)])
        kb.op("vector", lambda e, o=cx.qtmp[:, 1, :], i=cx.qtmp[:, 0, :], t=tblA:
              e.tensor_tensor(o, i, t, ALU.mult), inc=kb.v)
        for n in range(2):
            mm = [lambda e, pt, n=n: e.matmul(pt[:, 0:512], cx.ptc[:, :],
                                              cx.qtmp[:, 0, 512 * n:512 * n + 512],
                                              start=True, stop=True)]
            ev = (lambda e, pt, n=n:
                  e.scalar_tensor_tensor(cx.qtmp[:, 0, 512 * n:512 * n + 512],
                                         pt[:, 0:512], 1.0,
                                         tblB[:, 512 * n:512 * n + 512],
                                         ALU.mult, ALU.mult))
            mm_evac(kb, cx.pp, mm, "vector", ev,
                    extra_mm_waits=[(kb.v, kb.v.n)])
        kb.wait("gpsimd", kb.v)
        kb.op("gpsimd", lambda e, o=x3[:, c, :], a=cx.qtmp[:, 1, :], b=cx.qtmp[:, 0, :]:
              e.tensor_tensor(o, a, b, ALU.add), inc=kb.g)


def emit_attn_win(kb, cx, tc):
    smro = Rot([0, 1])
    pero = Rot([0, 1])
    pnro = Rot([0, 1])
    ptro = Rot([0, 1])
    nmro = Rot(list(range(4)))
    suro = Rot(list(range(4, 8)))
    rsro = Rot(list(range(8, 12)))
    stc = lambda j: cx.stt[:, j:j + 1]
    kg0 = kb.g.n
    for r in (smro, pero, pnro, ptro):
        for j in range(len(r.views)):
            r.done(j, kb.g, kg0)
    for h in range(16):
        pb = (h % 2) * 64
        ch = h // 2
        for qt in range(8):
            w = qt // 2
            p = qt % 2
            glob0 = (w == 0 and tc == 0)
            smj0, smj, smw = smro.next()
            sm = cx.w_sm[smj0]
            qsl = cx.q3[pb:pb + 64, ch, 128 * qt:128 * qt + 128]
            mms = []
            if not glob0:
                if w == 0:
                    ksrc = cx.khalo[pb:pb + 64, ch, 0:256]
                else:
                    ksrc = cx.k3[pb:pb + 64, ch, 256 * (w - 1):256 * w]
                mms.append(lambda e, pt, qsl=qsl, ksrc=ksrc:
                           e.matmul(pt[:, 0:256], qsl, ksrc, start=True, stop=True))
            mms.append(lambda e, pt, qsl=qsl, a=cx.k3[pb:pb + 64, ch, 256 * w:256 * w + 256]:
                       e.matmul(pt[:, 256:512], qsl, a, start=True, stop=True))
            mi = (2 + p) if glob0 else p
            if glob0:
                for s_, v_ in smw:
                    kb.wait("vector", s_, v_)
                kb.op("vector", lambda e, o=sm[:, 0:256]: e.memset(o, NEG))
                ev = (lambda e, pt, sm=sm, mi=mi:
                      e.scalar_tensor_tensor(sm[:, 256:512], pt[:, 256:512], 1.0,
                                             cx.wmask[:, mi, 256:512], ALU.mult, ALU.add))
            else:
                if qt == 0 or True:
                    for s_, v_ in smw:
                        kb.wait("vector", s_, v_)
                ev = (lambda e, pt, sm=sm, mi=mi:
                      e.scalar_tensor_tensor(sm[:, 0:512], pt[:, 0:512], 1.0,
                                             cx.wmask[:, mi, 0:512], ALU.mult, ALU.add))
            mm_evac(kb, cx.pp, mms, "vector", ev, extra_mm_waits=[(kb.g, kg0)])
            nmj0, nmj, nmw = nmro.next()
            for s_, v_ in nmw:
                kb.wait("vector", s_, v_)
            kb.op("vector", lambda e, o=stc(nmj0), sm=sm:
                  e.reduce_max(o, sm[:, 0:512], axis=AX.X, negate=True), inc=kb.v)
            suj0, suj, suw = suro.next()
            pej0, pej, pew = pero.next()
            peb = cx.w_pe[pej0]
            kb.wait("scalar", kb.v)
            for s_, v_ in pew + suw:
                kb.wait("scalar", s_, v_)
            kb.op("scalar", lambda e, o=peb[:, 0:512], sm=sm, bi=stc(nmj0), ac=stc(suj0):
                  e.activation(o, sm[:, 0:512], AF.Exp, bias=bi, accum_out=ac),
                  inc=kb.s)
            smro.done(smj, kb.s, kb.s.n)
            nmro.done(nmj, kb.s, kb.s.n)
            rsj0, rsj, rsw = rsro.next()
            kb.wait("vector", kb.s)
            for s_, v_ in rsw:
                kb.wait("vector", s_, v_)
            kb.op("vector", lambda e, o=stc(rsj0), i=stc(suj0):
                  e.reciprocal(o, i), inc=kb.v)
            suro.done(suj, kb.v, kb.v.n)
            pnj0, pnj, pnw = pnro.next()
            pnb = cx.w_pn[pnj0]
            kb.wait("gpsimd", kb.v)
            for s_, v_ in pnw:
                kb.wait("gpsimd", s_, v_)
            kb.op("gpsimd", lambda e, o=pnb[:, 0:512], i=peb[:, 0:512], r=stc(rsj0):
                  e.tensor_scalar_mul(o, i, r), inc=kb.g)
            pero.done(pej, kb.g, kb.g.n)
            rsro.done(rsj, kb.g, kb.g.n)
            ptj0, ptj, ptw = ptro.next()
            ptb = cx.w_pt[ptj0]
            kb.wait("scalar", kb.g)
            for s_, v_ in ptw:
                kb.wait("scalar", s_, v_)
            ndt = kb.dmaT("scalar",
                          ptb[:, 0:512].rearrange("p (kb q) -> p kb q", q=128),
                          pnb[:, 0:512].rearrange("q (kb p) -> q kb p", p=128))
            pnro.done(pnj, kb.dt, ndt)
            sts = [2 * (w - 1) + j for j in range(4)]
            mms = []
            valid = [st for st in sts if not (st < 0 and tc == 0)]
            for idx, st in enumerate(valid):
                j = sts.index(st)
                if st < 0:
                    vsl = cx.vhalo[:, 2 + st, 64 * h:64 * h + 64]
                else:
                    vsl = cx.vtm_f[:, st, 64 * h:64 * h + 64]
                mms.append(lambda e, pt, j=j, vsl=vsl, s=(idx == 0), q=(idx == len(valid) - 1), ptb=ptb:
                           e.matmul(pt[0:64, 0:128], vsl,
                                    ptb[:, 128 * j:128 * j + 128],
                                    start=s, stop=q))
            ev = (lambda e, pt, pb=pb, ch=ch, qt=qt:
                  e.copy(cx.att_f[pb:pb + 64, ch, 128 * qt:128 * qt + 128],
                         pt[0:64, 0:128]))
            mm_evac(kb, cx.pp, mms, "scalar", ev, extra_mm_waits=[(kb.dt, ndt)])
            ptro.done(ptj, kb.pe, kb.pe.n)


def emit_l0_ffn_layer(kb, cx, W, li, do_attn=True, do_ffn=True, tcs=(0, 1)):
    n1 = kb.dma("gpsimd", cx.colc[:, 0:8], W["f_lng"][li])
    kb.dma("gpsimd", cx.colc[:, 8:16], W["f_lnb"][li])
    kb.dma("gpsimd", cx.colc[:, 16:60], W["f_b1col"][li])
    kb.dma("gpsimd", cx.colc2[:, 0:8], W["f_b2col"][li])
    nqs = kb.dma("gpsimd", cx.rotm[:, 0:2], W["f_qscol"][li])
    kb.wait("vector", kb.ld)
    kb.op("vector", lambda e: e.reciprocal(cx.rotm[:, 2:4], cx.rotm[:, 0:2]), inc=kb.v)
    kb.wait("scalar", kb.v)
    kb.op("scalar", lambda e: e.activation(cx.rotm[:, 4:6], cx.rotm[:, 2:4], AF.Square),
          inc=kb.s)
    kb.wait("vector", kb.s)
    kb.op("vector", lambda e: e.tensor_scalar(cx.rotm[:, 6:8], cx.e2c[:, 0:2],
                                              cx.rotm[:, 4:5], None, ALU.mult),
          )
    kb.op("vector", lambda e: e.tensor_scalar(cx.rotm[:, 8:10], cx.e2c[:, 0:2],
                                              cx.rotm[:, 5:6], None, ALU.mult),
          inc=kb.v)
    blkq = cx.rotm[:, 6:8]
    blkk = cx.rotm[:, 8:10]
    qcol = cx.rotm[:, 0:1]
    kcol = cx.rotm[:, 1:2]
    kb.wait("gpsimd", kb.v)
    kb.op("gpsimd", lambda e: e.tensor_copy(cx.xsv[:, 4:8, 0:1], cx.xbuf[:, 4:8, 1023:1024]),
          inc=kb.g)
    for tc in (tcs if do_attn else ()):
        gl0 = 1024 * tc
        ntb = 0
        for j in range(2):
            ntb = kb.dma("gpsimd", cx.tbl[:, j, :], W["f_tbl"][j][:, gl0:gl0 + 1024])
        kb.wait("gpsimd", kb.pe)
        kb.wait("gpsimd", kb.v)
        emit_shift(kb, cx, cx.xs, gl0, gl0 + 1024, sv_col=0)
        wq = W["f_wqkv"][li]
        kb.wait("tensor", kb.g)
        kb.wait("tensor", kb.ld, nqs)
        emit_linear(kb, cx, wq, 8, cx.xs, out3=cx.q3, m0=0, nm=8, t_lo=0, t_hi=1024,
                    evac_fn=lambda e, pt, mi_, tl, w_, dst:
                    e.tensor_scalar(cx.q3[:, mi_, tl:tl + w_], pt[:, 0:w_],
                                    qcol, None, ALU.mult))
        emit_linear(kb, cx, wq, 8, cx.xs, out3=cx.k3, m0=8, nm=8, t_lo=0, t_hi=1024,
                    evac_fn=lambda e, pt, mi_, tl, w_, dst:
                    e.tensor_scalar(cx.k3[:, mi_, tl:tl + w_], pt[:, 0:w_],
                                    kcol, None, ALU.mult))
        if tc == 1:
            kb.wait("vector", kb.g)
        emit_linear(kb, cx, wq, 8, cx.xs, out3=cx.vfm, m0=16, nm=8, t_lo=0, t_hi=1024)
        kb.wait("scalar", kb.v)
        kb.wait("scalar", kb.g)
        nvt = 0
        for c in range(8):
            nvt = kb.dmaT("scalar",
                          cx.vtm_f[:, :, 128 * c:128 * c + 128],
                          cx.vfm[:, c, :].rearrange("p (tt q) -> p tt q", q=128))
        kb.wait("vector", kb.ld, ntb)
        emit_rotary(kb, cx, cx.q3, blkq, cx.tbl[:, 0, :], cx.tbl[:, 1, :])
        kb.wait("gpsimd", kb.v)
        ntb2 = 0
        for j in range(2):
            ntb2 = kb.dma("gpsimd", cx.tbl[:, j, :], W["f_tbl"][2 + j][:, gl0:gl0 + 1024])
        kb.wait("vector", kb.ld, ntb2)
        emit_rotary(kb, cx, cx.k3, blkk, cx.tbl[:, 0, :], cx.tbl[:, 1, :])
        kb.wait("tensor", kb.dt, nvt)
        emit_attn_win(kb, cx, tc)
        if tc == 0:
            kb.wait("gpsimd", kb.pe)
            kb.wait("gpsimd", kb.s)
            kb.op("gpsimd", lambda e: e.tensor_copy(cx.khalo[:, :, :],
                                                    cx.k3[:, :, 768:1024]))
            kb.op("gpsimd", lambda e: e.tensor_copy(
                cx.vhalo[:, :, :].rearrange("p a t -> p (a t)"),
                cx.vtm_f[:, 6:8, :].rearrange("p a t -> p (a t)")), inc=kb.g)
        kb.wait("tensor", kb.s)
        emit_linear(kb, cx, W["f_wo"][li], 8, cx.att_f, nm=8, t_lo=0, t_hi=1024,
                    evac_eng="vector",
                    evac_fn=lambda e, pt, mi_, tl, w_, dst, gl0=gl0:
                    e.scalar_tensor_tensor(cx.xbuf[:, mi_, gl0 + tl:gl0 + tl + w_],
                                           pt[:, 0:w_], 1.0,
                                           cx.xbuf[:, mi_, gl0 + tl:gl0 + tl + w_],
                                           ALU.mult, ALU.add))
    if not do_ffn:
        return
    kb.wait("gpsimd", kb.v)
    for j, bc in enumerate((511, 1023, 1535)):
        kb.op("gpsimd", lambda e, j=j, bc=bc:
              e.tensor_copy(cx.xsv[:, 4:8, 1 + j:2 + j], cx.xbuf[:, 4:8, bc:bc + 1]),
              inc=kb.g if j == 2 else None)
    gcol = cx.colc[:, 0:8]
    bcol = cx.colc[:, 8:16]
    for tl in range(0, T, 512):
        kb.wait("gpsimd", kb.pe)
        kb.wait("gpsimd", kb.v)
        emit_shift(kb, cx, cx.xs2, tl, tl + 512, sv_col=tl // 512)
        emit_ln(kb, cx, cx.xs2, 0, 512, dst3=cx.xn, gamma_col=gcol, beta_col=bcol,
                src_bf=True)
        w1 = W["f_w1"][li]
        emit_linear(kb, cx, w1, 8, cx.xn, m0=22, nm=22, t_lo=0, t_hi=512,
                    evac_eng="scalar", w_ready=(kb.v, kb.v.n),
                    evac_fn=lambda e, pt, mi_, tl_, w_, dst:
                    e.activation(cx.ggel[:, mi_, 0:w_], pt[:, 0:w_], AF.Gelu,
                                 bias=cx.colc[:, 38 + mi_:39 + mi_]))
        kb.wait("tensor", kb.s)
        emit_linear(kb, cx, w1, 8, cx.xn, m0=0, nm=22, t_lo=0, t_hi=512,
                    evac_eng="vector", w_ready=(kb.s, kb.s.n),
                    evac_fn=lambda e, pt, mi_, tl_, w_, dst:
                    e.scalar_tensor_tensor(cx.uge[:, mi_, 0:w_], pt[:, 0:w_],
                                           cx.colc[:, 16 + mi_:17 + mi_],
                                           cx.ggel[:, mi_, 0:w_],
                                           ALU.add, ALU.mult))
        emit_linear(kb, cx, W["f_w2"][li], 22, cx.uge, nm=8, t_lo=0, t_hi=512,
                    evac_eng="vector", w_ready=(kb.v, kb.v.n),
                    evac_fn=lambda e, pt, mi_, tl_, w_, dst, tl0=tl:
                    e.scalar_tensor_tensor(cx.xbuf[:, mi_, tl0:tl0 + w_],
                                           pt[:, 0:w_],
                                           cx.colc2[:, mi_:mi_ + 1],
                                           cx.xbuf[:, mi_, tl0:tl0 + w_],
                                           ALU.add, ALU.add))


def emit_l0_ffn(kb, cx, W):
    for li in range(6):
        emit_l0_ffn_layer(kb, cx, W, li)


def emit_l0f_block(kb, cx, W, x1sv, n_layers=6):
    kb.wait("sync", kb.g)
    kb.wait("sync", kb.v)
    nst = kb.dma("sync", x1sv.rearrange("(c p) t -> p c t", p=128), cx.xbuf[:, :, :])
    kb.wait("vector", kb.sx, nst)
    emit_ln(kb, cx, cx.xbuf, 0, T, dst3=cx.xbuf)
    for li in range(n_layers):
        emit_l0_ffn_layer(kb, cx, W, li)
    kb.wait("sync", kb.pe)
    kb.wait("sync", kb.s)
    for tl in range(0, T, 512):
        kb.wait("sync", kb.v)
        kb.wait("sync", kb.g)
        nld = kb.dma("sync", cx.f32scr[:, 0:8, :],
                     x1sv.rearrange("(c p) t -> p c t", p=128)[:, :, tl:tl + 512])
        kb.wait("vector", kb.sx, nld)
        emit_ln(kb, cx, cx.xbuf, tl, tl + 512, resid3=cx.xbuf,
                resid_src3=cx.f32scr, rs_tl0=0)


def build_l0f(n_layers=6, dump_stream=False):
    nc = bass.Bass("TRN2", target_bir_lowering=False, debug=False)
    xT = nc.dram_tensor("xT", [D, T], F32, kind="ExternalInput").ap()
    W = declare_l0ffn(nc)
    out = nc.dram_tensor("xo", [D, T], F32, kind="ExternalOutput").ap()
    x1sv = nc.dram_tensor("x1sv", [D, T], F32, kind="Internal").ap()
    with ExitStack() as stack:
        kb = KB(nc, stack)
        cx = make_ctx(kb)
        cx = make_ctx_ffn(kb, cx)
        emit_const_init(kb, cx)
        emit_ffn_init(kb, cx, W)
        kb.dma("sync", cx.xbuf[:, :, :], xT.rearrange("(c p) t -> p c t", p=128))
        for eng in ("scalar", "tensor", "vector", "gpsimd"):
            kb.wait(eng, kb.sx)
            kb.wait(eng, kb.ld)
        kb.wait("tensor", kb.v)
        if dump_stream:
            for li in range(n_layers):
                emit_l0_ffn_layer(kb, cx, W, li)
        else:
            emit_l0f_block(kb, cx, W, x1sv, n_layers)
        kb.wait("sync", kb.g)
        kb.wait("sync", kb.v)
        kb.dma("sync", out.rearrange("(c p) t -> p c t", p=128), cx.xbuf[:, :, :])
        kb.finalize([(kb.sx, kb.sx.n), (kb.ld, kb.ld.n), (kb.dt, kb.dt.n)])
    return nc


def l0f_inputs(x_T, inp):
    d = hostf_l0ffn_consts(inp)
    d["xT"] = np.ascontiguousarray(x_T, np.float32)
    return d


def host_l1a_consts(inp):
    glw = np.asarray(inp["L1_attn_gate_local_w"], np.float32)
    grw = np.asarray(inp["L1_attn_gate_remote_w"], np.float32)
    glwb = np.zeros((8, 128, 16), np.float32)
    for c in range(8):
        for j in range(2):
            glwb[c, 64 * j:64 * j + 64, 2 * c + j] = glw[2 * c + j]
    grwB = np.zeros((8, 128, 128), np.float32)
    for c in range(8):
        for j in range(2):
            grwB[c, 64 * j:64 * j + 64, 64 * j:64 * j + 64] = grw[2 * c + j][:, None]
    m = np.zeros((2, 128, 512), np.float32)
    m0 = np.zeros((2, 128, 512), np.float32)
    for p in range(2):
        for r in range(128):
            q = 128 * p + r
            c = np.arange(512)
            bad = (c <= q) | (c > q + 256)
            m[p, r, bad] = NEG
            bad0 = bad | (c < 256)
            m0[p, r, bad0] = NEG
    return {
        "a_wq": np.ascontiguousarray(inp["L1_attn_wq"], np.float32),
        "a_wk": np.ascontiguousarray(inp["L1_attn_wk"], np.float32),
        "a_wv": np.ascontiguousarray(inp["L1_attn_wv"], np.float32),
        "a_wgq": np.ascontiguousarray(inp["L1_attn_wgq"], np.float32),
        "a_wgk": np.ascontiguousarray(inp["L1_attn_wgk"], np.float32),
        "a_wgv": np.ascontiguousarray(inp["L1_attn_wgv"], np.float32),
        "a_wro": np.ascontiguousarray(inp["L1_attn_r_out_w"], np.float32),
        "a_wout": np.ascontiguousarray(inp["L1_attn_w_out"], np.float32),
        "a_glwb": glwb, "a_grwB": grwB,
        "a_wmask": np.stack(list(m) + list(m0)),
        "a_lmask": host_latent_masks(),
    }


def declare_l1a(nc):
    W = {}
    for nm, shp in (("a_wq", [D, D]), ("a_wk", [D, D]), ("a_wv", [D, D]),
                    ("a_wgq", [D, 512]), ("a_wgk", [D, 512]), ("a_wgv", [D, 512]),
                    ("a_wro", [512, D]), ("a_wout", [D, D]),
                    ("a_glwb", [8, 128, 16]), ("a_grwB", [8, 128, 128]),
                    ("a_wmask", [4, 128, 512]), ("a_lmask", [16, 128, 64])):
        W[nm] = nc.dram_tensor(nm, shp, F32, kind="ExternalInput").ap()
    return W


def make_ctx_l1a(kb, cx):
    tblf = cx.tbl[:, :, :].rearrange("p a t -> p (a t)")
    cx.grwB = tblf[:, 0:1024].rearrange("p (a t) -> p a t", t=128)
    cx.glwb = tblf[:, 1024:1152].rearrange("p (a t) -> p a t", t=16)
    cx.px = tblf[:, 1152:1664].rearrange("p (a t) -> p a t", t=64)
    cx.gk1 = tblf[:, 1664:1920].rearrange("p (a t) -> p a t", t=64)
    cx.gvf = cx.bigB[:, 4, 0:512].rearrange("p (a t) -> p a t", t=128)
    cx.gvt = cx.f32scr[:, 7, 0:256].bitcast(BF16).rearrange("p (a t) -> p a t", t=128)
    cx.glT = kb.sbuf("glT", [128, 4, 16], F32)
    cx.q1 = cx.bigC[:, :, 0:512]
    cx.k1 = cx.bigC[:, :, 512:1024]
    cx.vf1 = cx.bigC[:, :, 1024:1536]
    cx.kh1 = cx.bigC[:, :, 1536:1792]
    cx.vh1 = cx.bigC[:, :, 1792:2048]
    cx.vt1 = cx.bigB[:, 0:2, :].rearrange("p a (b t) -> p (a b) t", t=1024)
    cx.att1 = cx.bigB[:, 2:4, :].rearrange("p a (b t) -> p (a b) t", t=512)
    cx.gq1 = cx.bigB[:, 6, :].rearrange("p (a t) -> p a t", t=512)
    cx.lat1 = cx.bigB[:, 7, :].rearrange("p (a t) -> p a t", t=512)
    cx.lmask = cx.f32scr[:, 6, :].bitcast(BF16).rearrange("p (a t) -> p a t", t=64)
    return cx


def emit_l1a_init(kb, cx, W):
    kb.dma("gpsimd", cx.wmask[:, :, :], W["a_wmask"].rearrange("m p t -> p m t"))
    kb.dma("gpsimd", cx.lmask[:, :, :], W["a_lmask"].rearrange("m p t -> p m t"))
    kb.dma("gpsimd", cx.glwb[:, :, :], W["a_glwb"].rearrange("c p t -> p c t"))
    kb.dma("gpsimd", cx.grwB[:, :, :], W["a_grwB"].rearrange("c p t -> p c t"))


def emit_l1_attn(kb, cx, W, att_sv):
    emit_ln(kb, cx, cx.xbuf, 0, T, dst3=cx.bigA)
    kb.wait("vector", kb.v)
    for c in range(NCH):
        pxf = cx.f32scr[:, 0, 0:64]
        kb.op("vector", lambda e, c=c, pxf=pxf:
              e.reduce_sum(pxf.unsqueeze(2),
                           cx.bigA[:, c, :].rearrange("p (g w) -> p g w", w=32),
                           axis=AX.X))
        kb.op("vector", lambda e, c=c, pxf=pxf:
              e.tensor_scalar_mul(cx.px[:, c, :], pxf, 1.0 / 32.0),
              inc=kb.v if c == NCH - 1 else None)
    kb.wait("tensor", kb.v)
    emit_linear(kb, cx, W["a_wgk"], 8, cx.px, out3=cx.gk1, nm=4, t_lo=0, t_hi=64)
    emit_linear(kb, cx, W["a_wgv"], 8, cx.px, out3=cx.gvf, nm=4, t_lo=0, t_hi=64)
    kb.wait("scalar", kb.v)
    ngv = kb.dmaT("scalar", cx.gvt[:, :, :], cx.gvf[:, :, :])
    smro = Rot([0, 1]); pero = Rot([0, 1]); pnro = Rot([0, 1]); ptro = Rot([0, 1])
    nmro = Rot(list(range(4))); suro = Rot(list(range(4, 8))); rsro = Rot(list(range(8, 12)))
    stc = lambda j: cx.stt[:, j:j + 1]
    for tc in range(4):
        gl0 = 512 * tc
        kb.wait("tensor", kb.g)
        kb.wait("tensor", kb.s)
        kb.wait("tensor", kb.sx)
        emit_linear(kb, cx, W["a_wq"], 8, cx.bigA, out3=cx.q1, nm=8,
                    t_lo=gl0, t_hi=gl0 + 512, out_tl0=0)
        emit_linear(kb, cx, W["a_wk"], 8, cx.bigA, out3=cx.k1, nm=8,
                    t_lo=gl0, t_hi=gl0 + 512, out_tl0=0, evac_scale=0.125)
        emit_linear(kb, cx, W["a_wv"], 8, cx.bigA, out3=cx.vf1, nm=8,
                    t_lo=gl0, t_hi=gl0 + 512, out_tl0=0)
        emit_linear(kb, cx, W["a_wgq"], 8, cx.bigA, out3=cx.gq1, nm=4,
                    t_lo=gl0, t_hi=gl0 + 512, out_tl0=0)
        kb.wait("scalar", kb.v)
        kb.wait("scalar", kb.pe)
        nvt = 0
        for c in range(8):
            nvt = kb.dmaT("scalar",
                          cx.vt1[:, :, 128 * c:128 * c + 128],
                          cx.vf1[:, c, :].rearrange("p (tt q) -> p tt q", q=128))
        for tt in range(4):
            mms = []
            for c in range(8):
                mms.append(lambda e, pt, c=c, tt=tt:
                           e.matmul(pt[:, 0:16], cx.q1[:, c, 128 * tt:128 * tt + 128],
                                    cx.glwb[:, c, :], start=(c == 0), stop=(c == 7)))
            ev = (lambda e, pt, tt=tt:
                  e.activation(cx.glT[:, tt, :], pt[:, 0:16], AF.Sigmoid))
            mm_evac(kb, cx.pp, mms, "scalar", ev, extra_mm_waits=[(kb.v, kb.v.n)])
        kg0 = kb.g.n
        for r in (smro, pero, pnro, ptro):
            for j in range(len(r.views)):
                r.done(j, kb.g, kg0)
        for h in range(16):
            pb = (h % 2) * 64
            ch = h // 2
            for qt in range(4):
                w = qt // 2
                p = qt % 2
                glob0 = (w == 0 and tc == 0)
                smj0, smj, smw = smro.next()
                sm = cx.w_sm[smj0]
                qsl = cx.q1[pb:pb + 64, ch, 128 * qt:128 * qt + 128]
                mms = []
                if not glob0:
                    if w == 0:
                        ksrc = cx.kh1[pb:pb + 64, ch, 0:256]
                    else:
                        ksrc = cx.k1[pb:pb + 64, ch, 0:256]
                    mms.append(lambda e, pt, qsl=qsl, ksrc=ksrc:
                               e.matmul(pt[:, 0:256], qsl, ksrc, start=True, stop=True))
                mms.append(lambda e, pt, qsl=qsl,
                           a=cx.k1[pb:pb + 64, ch, 256 * w:256 * w + 256]:
                           e.matmul(pt[:, 256:512], qsl, a, start=True, stop=True))
                mi = (2 + p) if glob0 else p
                for s_, v_ in smw:
                    kb.wait("vector", s_, v_)
                if glob0:
                    kb.op("vector", lambda e, o=sm[:, 0:256]: e.memset(o, NEG))
                    ev = (lambda e, pt, sm=sm, mi=mi:
                          e.scalar_tensor_tensor(sm[:, 256:512], pt[:, 256:512], 1.0,
                                                 cx.wmask[:, mi, 256:512],
                                                 ALU.mult, ALU.add))
                else:
                    ev = (lambda e, pt, sm=sm, mi=mi:
                          e.scalar_tensor_tensor(sm[:, 0:512], pt[:, 0:512], 1.0,
                                                 cx.wmask[:, mi, 0:512],
                                                 ALU.mult, ALU.add))
                mm_evac(kb, cx.pp, mms, "vector", ev, extra_mm_waits=[(kb.g, kg0)])
                nmj0, nmj, nmw = nmro.next()
                for s_, v_ in nmw:
                    kb.wait("vector", s_, v_)
                kb.op("vector", lambda e, o=stc(nmj0), sm=sm:
                      e.reduce_max(o, sm[:, 0:512], axis=AX.X, negate=True), inc=kb.v)
                suj0, suj, suw = suro.next()
                pej0, pej, pew = pero.next()
                peb = cx.w_pe[pej0]
                kb.wait("scalar", kb.v)
                for s_, v_ in pew + suw:
                    kb.wait("scalar", s_, v_)
                kb.op("scalar", lambda e, o=peb[:, 0:512], sm=sm, bi=stc(nmj0),
                      ac=stc(suj0):
                      e.activation(o, sm[:, 0:512], AF.Exp, bias=bi, accum_out=ac),
                      inc=kb.s)
                smro.done(smj, kb.s, kb.s.n)
                nmro.done(nmj, kb.s, kb.s.n)
                rsj0, rsj, rsw = rsro.next()
                kb.wait("vector", kb.s)
                for s_, v_ in rsw:
                    kb.wait("vector", s_, v_)
                kb.op("vector", lambda e, o=stc(rsj0), i=stc(suj0):
                      e.reciprocal(o, i), inc=kb.v)
                suro.done(suj, kb.v, kb.v.n)
                pnj0, pnj, pnw = pnro.next()
                pnb = cx.w_pn[pnj0]
                kb.wait("gpsimd", kb.v)
                for s_, v_ in pnw:
                    kb.wait("gpsimd", s_, v_)
                kb.op("gpsimd", lambda e, o=pnb[:, 0:512], i=peb[:, 0:512],
                      r=stc(rsj0), g=cx.glT[:, qt, h:h + 1]:
                      e.tensor_scalar(o, i, r, g, ALU.mult, ALU.mult), inc=kb.g)
                pero.done(pej, kb.g, kb.g.n)
                rsro.done(rsj, kb.g, kb.g.n)
                ptj0, ptj, ptw = ptro.next()
                ptb = cx.w_pt[ptj0]
                kb.wait("scalar", kb.g)
                for s_, v_ in ptw:
                    kb.wait("scalar", s_, v_)
                ndt = kb.dmaT("scalar",
                              ptb[:, 0:512].rearrange("p (kb q) -> p kb q", q=128),
                              pnb[:, 0:512].rearrange("q (kb p) -> q kb p", p=128))
                pnro.done(pnj, kb.dt, ndt)
                sts = [2 * (w - 1) + j for j in range(4)]
                mms = []
                valid = [st for st in sts if not (st < 0 and tc == 0)]
                for idx, st in enumerate(valid):
                    j = sts.index(st)
                    if st < 0:
                        sth = 2 + st
                        flat = sth * 1024 + 64 * h
                        vsl = cx.vh1[:, flat // 256, flat % 256:flat % 256 + 64]
                    else:
                        vsl = cx.vt1[:, st, 64 * h:64 * h + 64]
                    mms.append(lambda e, pt, j=j, vsl=vsl, s=(idx == 0),
                               q=(idx == len(valid) - 1), ptb=ptb:
                               e.matmul(pt[0:64, 0:128], vsl,
                                        ptb[:, 128 * j:128 * j + 128],
                                        start=s, stop=q))
                ev = (lambda e, pt, pb=pb, ch=ch, qt=qt:
                      e.copy(cx.att1[pb:pb + 64, ch, 128 * qt:128 * qt + 128],
                             pt[0:64, 0:128]))
                mm_evac(kb, cx.pp, mms, "scalar", ev, extra_mm_waits=[(kb.dt, ndt)])
                ptro.done(ptj, kb.pe, kb.pe.n)
        kb.wait("gpsimd", kb.pe)
        kb.wait("gpsimd", kb.s)
        kb.op("gpsimd", lambda e: e.tensor_copy(cx.kh1[:, :, :], cx.k1[:, :, 256:512]))
        kb.op("gpsimd", lambda e: e.tensor_copy(
            cx.vh1[:, :, :],
            cx.vt1[:, 2:4, :].rearrange("p a (b t) -> p (a b) t", t=256)), inc=kb.g)
        for lh in range(8):
            pb = (lh % 2) * 64
            ch = lh // 2
            for qt in range(4):
                gqt = 4 * tc + qt
                smj0, smj, smw = smro.next()
                sm = cx.w_sm[smj0]
                mms = [lambda e, pt, pb=pb, ch=ch, qt=qt:
                       e.matmul(pt[:, 0:64], cx.gq1[pb:pb + 64, ch, 128 * qt:128 * qt + 128],
                                cx.gk1[pb:pb + 64, ch, :], start=True, stop=True)]
                for s_, v_ in smw:
                    kb.wait("vector", s_, v_)
                ev = (lambda e, pt, sm=sm, gqt=gqt:
                      e.scalar_tensor_tensor(sm[:, 0:64], pt[:, 0:64], 0.125,
                                             cx.lmask[:, gqt, :], ALU.mult, ALU.add))
                mm_evac(kb, cx.pp, mms, "vector", ev, extra_mm_waits=[(kb.g, kb.g.n)])
                nmj0, nmj, nmw = nmro.next()
                for s_, v_ in nmw:
                    kb.wait("vector", s_, v_)
                kb.op("vector", lambda e, o=stc(nmj0), sm=sm:
                      e.reduce_max(o, sm[:, 0:64], axis=AX.X, negate=True), inc=kb.v)
                suj0, suj, suw = suro.next()
                pej0, pej, pew = pero.next()
                peb = cx.w_pe[pej0]
                kb.wait("scalar", kb.v)
                for s_, v_ in pew + suw:
                    kb.wait("scalar", s_, v_)
                kb.op("scalar", lambda e, o=peb[:, 0:64], sm=sm, bi=stc(nmj0),
                      ac=stc(suj0):
                      e.activation(o, sm[:, 0:64], AF.Exp, bias=bi, accum_out=ac),
                      inc=kb.s)
                smro.done(smj, kb.s, kb.s.n)
                nmro.done(nmj, kb.s, kb.s.n)
                rsj0, rsj, rsw = rsro.next()
                kb.wait("vector", kb.s)
                for s_, v_ in rsw:
                    kb.wait("vector", s_, v_)
                kb.op("vector", lambda e, o=stc(rsj0), i=stc(suj0):
                      e.reciprocal(o, i), inc=kb.v)
                suro.done(suj, kb.v, kb.v.n)
                pnj0, pnj, pnw = pnro.next()
                pnb = cx.w_pn[pnj0]
                kb.wait("gpsimd", kb.v)
                for s_, v_ in pnw:
                    kb.wait("gpsimd", s_, v_)
                kb.op("gpsimd", lambda e, o=pnb[:, 0:64], i=peb[:, 0:64], r=stc(rsj0):
                      e.tensor_scalar_mul(o, i, r), inc=kb.g)
                pero.done(pej, kb.g, kb.g.n)
                rsro.done(rsj, kb.g, kb.g.n)
                ptj0, ptj, ptw = ptro.next()
                ptb = cx.w_pt[ptj0]
                kb.wait("scalar", kb.g)
                for s_, v_ in ptw:
                    kb.wait("scalar", s_, v_)
                ndt = kb.dmaT("scalar", ptb[:, 0:128], pnb[:, 0:128])
                pnro.done(pnj, kb.dt, ndt)
                mms = [lambda e, pt, lh=lh, ptb=ptb:
                       e.matmul(pt[0:64, 0:128],
                                cx.gvt[0:64, lh // 2, (lh % 2) * 64:(lh % 2) * 64 + 64],
                                ptb[0:64, 0:128], start=True, stop=True)]
                ev = (lambda e, pt, pb=pb, ch=ch, qt=qt:
                      e.copy(cx.lat1[pb:pb + 64, ch, 128 * qt:128 * qt + 128],
                             pt[0:64, 0:128]))
                mm_evac(kb, cx.pp, mms, "scalar", ev,
                        extra_mm_waits=[(kb.dt, ndt), (kb.dt, ngv)])
                ptro.done(ptj, kb.pe, kb.pe.n)
        kb.wait("tensor", kb.s)
        for c in range(8):
            mms = [lambda e, pt, c=c:
                   e.matmul(pt[:, 0:512], cx.grwB[:, c, :], cx.q1[:, c, :],
                            start=True, stop=True)]
            grb = cx.w_sm[0]
            ev = (lambda e, pt, grb=grb:
                  e.activation(grb[:, 0:512], pt[:, 0:512], AF.Sigmoid))
            mm_evac(kb, cx.pp, mms, "scalar", ev,
                    extra_mm_waits=[(kb.g, kb.g.n), (kb.s, kb.s.n)])
            def remev(e, pt, mi_, tl_, w_, dst, c=c, grb=grb):
                return e.tensor_tensor(cx.w_pe[0][:, 0:512], pt[:, 0:512],
                                       grb[:, 0:512], ALU.mult)
            emit_linear(kb, cx, W["a_wro"], 4, cx.lat1, m0=c, nm=1,
                        t_lo=0, t_hi=512, evac_eng="vector",
                        w_ready=[(kb.s, kb.s.n), (kb.g, kb.g.n)], evac_fn=remev)
            kb.wait("gpsimd", kb.v)
            kb.op("gpsimd", lambda e, c=c:
                  e.tensor_tensor(cx.att1[:, c, :], cx.att1[:, c, :],
                                  cx.w_pe[0][:, 0:512], ALU.add), inc=kb.g)
        kb.wait("sync", kb.g)
        kb.dma("sync", att_sv.rearrange("(c p) t -> p c t", p=128)[:, :, gl0:gl0 + 512],
               cx.att1[:, :, :])
    natt = kb.dma("sync", cx.bigA[:, :, :], att_sv.rearrange("(c p) t -> p c t", p=128))
    kb.wait("tensor", kb.sx, natt)
    for tl in range(0, T, 512):
        kb.wait("vector", kb.g)
        emit_linear(kb, cx, W["a_wout"], 8, cx.bigA, nm=8, t_lo=tl, t_hi=tl + 512,
                    evac_fn=lambda e, pt, mi, tl_, w_, dst:
                    e.tensor_copy(cx.f32scr[:, mi, 0:w_], pt[:, 0:w_]))
        kb.wait("scalar", kb.v)
        kb.wait("gpsimd", kb.v)
        emit_ln(kb, cx, cx.f32scr, tl, tl + 512, resid3=cx.xbuf, src_tl0=0)


def host_l1f_consts(inp):
    d = {}
    for nm in ("wf", "uf", "wi", "ui", "wc", "uc", "wo", "uo"):
        d["g_" + nm] = np.ascontiguousarray(np.asarray(inp["L1_ffn_" + nm], np.float32).T)
    for nm in ("bf", "bi", "bc", "bo", "in_c"):
        d["g_" + nm] = np.ascontiguousarray(
            np.asarray(inp["L1_ffn_" + nm], np.float32).reshape(8, 128).T)
    return d


def declare_l1f(nc):
    W = {}
    for nm in ("g_wf", "g_uf", "g_wi", "g_ui", "g_wc", "g_uc", "g_wo", "g_uo"):
        W[nm] = nc.dram_tensor(nm, [D, D], F32, kind="ExternalInput").ap()
    for nm in ("g_bf", "g_bi", "g_bc", "g_bo", "g_in_c"):
        W[nm] = nc.dram_tensor(nm, [128, 8], F32, kind="ExternalInput").ap()
    return W


def make_ctx_l1f(kb, cx):
    cx.c_st = cx.f32scr[:, 0:4, :].rearrange("p a (b t) -> p (a b) t", t=256)
    cx.t1 = cx.f32scr[:, 4:8, :].rearrange("p a (b t) -> p (a b) t", t=256)
    cx.t2 = cx.f32scr[:, 8:12, :].rearrange("p a (b t) -> p (a b) t", t=256)
    cx.uf_s = cx.bigB[:, 0:4, :].rearrange("p a (b t) -> p (a b) t", t=1024)
    cx.uf_r = cx.bigB[:, 0:4, :].rearrange("p a (b m) -> p (a b) m", m=1024)
    cx.ui_r = cx.bigB[:, 4:8, :].rearrange("p a (b m) -> p (a b) m", m=1024)
    cx.uc_r = cx.bigC[:, 0:4, :].rearrange("p a (b m) -> p (a b) m", m=1024)
    cx.g_f = cx.bigC[:, 4, :].rearrange("p (a t) -> p a t", t=256)
    cx.g_i = cx.bigC[:, 5, :].rearrange("p (a t) -> p a t", t=256)
    cx.g_g = cx.bigC[:, 6, :].rearrange("p (a t) -> p a t", t=256)
    cx.c_bf = cx.bigC[:, 7, :].rearrange("p (a t) -> p a t", t=256)
    cx.bcols = kb.sbuf("bcols", [128, 5, 8], F32)
    return cx


def emit_resident_u(kb, cx, W):
    n = 0
    for w_dram, dst in ((W["g_uf"], cx.uf_r), (W["g_ui"], cx.ui_r), (W["g_uc"], cx.uc_r)):
        wv = w_dram.rearrange("(c p) m -> p c m", p=128)
        n = kb.dma("gpsimd", dst.rearrange("p (c x) m -> p c x m", x=1) if False else dst,
                   wv)
    return n


def emit_l1_ffn(kb, cx, W):
    for j, nm in enumerate(("g_bf", "g_bi", "g_bc", "g_bo", "g_in_c")):
        kb.dma("gpsimd", cx.bcols[:, j, :], W[nm])
    emit_ln(kb, cx, cx.xbuf, 0, T, dst3=cx.bigA)
    nur = emit_resident_u(kb, cx, W)
    kb.wait("gpsimd", kb.ld)
    for c in range(NCH):
        kb.op("gpsimd", lambda e, c=c:
              e.tensor_copy(cx.c_st[:, c, :],
                            cx.bcols[:, 4, c:c + 1].broadcast_to([128, 256])), inc=kb.g)
    for step in range(8):
        s0 = 256 * step
        kb.wait("gpsimd", kb.v)
        for c in range(NCH):
            kb.op("gpsimd", lambda e, c=c:
                  e.tensor_copy(cx.c_bf[:, c, :], cx.c_st[:, c, :]),
                  inc=kb.g if c == NCH - 1 else None)
        for gate, wname, uname, bj, act in (
                (cx.g_f, "g_wf", cx.uf_r, 0, AF.Sigmoid),
                (cx.g_i, "g_wi", cx.ui_r, 1, AF.Sigmoid),
                (cx.g_g, "g_wc", cx.uc_r, 2, AF.Tanh)):
            wv = W[wname].rearrange("(c p) m -> p c m", p=128)
            for mi in range(8):
                wsl, wj, wpend = cx.wrot.next()
                for s_, v_ in wpend:
                    kb.wait("gpsimd", s_, v_)
                nld = kb.dma("gpsimd",
                             wsl[:, 0:1024].rearrange("p (c m) -> p c m", m=128),
                             wv[:, :, 128 * mi:128 * mi + 128])
                wslv = wsl[:, 0:1024].rearrange("p (c m) -> p c m", m=128)
                mms = []
                for kc in range(8):
                    mms.append(lambda e, pt, kc=kc, wslv=wslv, mi=mi, s0=s0:
                               e.matmul(pt[:, 0:256], wslv[:, kc, :],
                                        cx.bigA[:, kc, s0:s0 + 256],
                                        start=(kc == 0), stop=False))
                for kc in range(8):
                    mms.append(lambda e, pt, kc=kc, uname=uname, mi=mi:
                               e.matmul(pt[:, 0:256], uname[:, kc, 128 * mi:128 * mi + 128],
                                        cx.c_bf[:, kc, :],
                                        start=False, stop=(kc == 7)))
                ev = (lambda e, pt, gate=gate, mi=mi, bj=bj, act=act:
                      e.activation(gate[:, mi, :], pt[:, 0:256], act,
                                   bias=cx.bcols[:, bj, mi:mi + 1]))
                mm_evac(kb, cx.pp, mms, "scalar", ev,
                        extra_mm_waits=[(kb.ld, nld), (kb.g, kb.g.n),
                                        (kb.ld, nur)])
                cx.wrot.done(wj, kb.pe, kb.pe.n)
        kb.wait("vector", kb.s)
        for c in range(NCH):
            kb.op("vector", lambda e, c=c:
                  e.tensor_tensor(cx.t1[:, c, :], cx.g_i[:, c, :], cx.g_g[:, c, :],
                                  ALU.mult))
            kb.op("vector", lambda e, c=c:
                  e.tensor_tensor(cx.t2[:, c, :], cx.g_f[:, c, :], cx.c_st[:, c, :],
                                  ALU.mult))
            kb.op("vector", lambda e, c=c:
                  e.tensor_tensor(cx.c_st[:, c, :], cx.t1[:, c, :], cx.t2[:, c, :],
                                  ALU.add), inc=kb.v if c == NCH - 1 else None)
    hT = cx.bigB[:, :, :]
    wvo = W["g_wo"].rearrange("(c p) m -> p c m", p=128)
    uvo = W["g_uo"].rearrange("(c p) m -> p c m", p=128)
    kb.wait("gpsimd", kb.v)
    for c in range(NCH):
        kb.op("gpsimd", lambda e, c=c:
              e.tensor_copy(cx.c_bf[:, c, :], cx.c_st[:, c, :]),
              inc=kb.g if c == NCH - 1 else None)
    for mi in range(8):
        usl, uj, upend = cx.wbig.next()
        for s_, v_ in upend:
            kb.wait("gpsimd", s_, v_)
        nld2 = kb.dma("gpsimd", usl[:, 0:1024].rearrange("p (c m) -> p c m", m=128),
                      uvo[:, :, 128 * mi:128 * mi + 128])
        uslv = usl[:, 0:1024].rearrange("p (c m) -> p c m", m=128)
        mms = []
        for kc in range(8):
            mms.append(lambda e, pt, kc=kc, uslv=uslv:
                       e.matmul(pt[:, 0:256], uslv[:, kc, :], cx.c_bf[:, kc, :],
                                start=(kc == 0), stop=(kc == 7)))
        zuo = cx.t1[:, 0:1, :]
        ev = (lambda e, pt: e.tensor_copy(cx.t1[:, 0, :], pt[:, 0:256]))
        mm_evac(kb, cx.pp, mms, "vector", ev,
                extra_mm_waits=[(kb.ld, nld2), (kb.g, kb.g.n)])
        cx.wbig.done(uj, kb.pe, kb.pe.n)
        wsl, wj, wpend = cx.wrot.next()
        for s_, v_ in wpend:
            kb.wait("gpsimd", s_, v_)
        nld = kb.dma("gpsimd", wsl[:, 0:1024].rearrange("p (c m) -> p c m", m=128),
                     wvo[:, :, 128 * mi:128 * mi + 128])
        wslv = wsl[:, 0:1024].rearrange("p (c m) -> p c m", m=128)
        for tl in range(0, T, 512):
            mms = []
            for kc in range(8):
                mms.append(lambda e, pt, kc=kc, wslv=wslv, tl=tl:
                           e.matmul(pt[:, 0:512], wslv[:, kc, :],
                                    cx.bigA[:, kc, tl:tl + 512],
                                    start=(kc == 0), stop=(kc == 7)))
            def hev(e, pt, mi=mi, tl=tl):
                zb = cx.t1[:, 0, :].unsqueeze(1).broadcast_to([128, 2, 256])
                return e.tensor_tensor(
                    cx.t2[:, 0:2, :], pt[:, 0:512].rearrange("p (a t) -> p a t", t=256),
                    zb, ALU.add)
            mm_evac(kb, cx.pp, mms, "vector", hev,
                    extra_mm_waits=[(kb.ld, nld), (kb.v, kb.v.n), (kb.s, kb.s.n)])
            kb.wait("scalar", kb.v)
            kb.op("scalar", lambda e, mi=mi, tl=tl:
                  e.activation(hT[:, mi, tl:tl + 512],
                               cx.t2[:, 0:2, :].rearrange("p a t -> p (a t)"),
                               AF.Tanh, bias=cx.bcols[:, 3, mi:mi + 1]), inc=kb.s)
        cx.wrot.done(wj, kb.pe, kb.pe.n)
    for tl in range(0, T, 512):
        kb.wait("scalar", kb.s)
        emit_ln(kb, cx, hT, tl, tl + 512, resid3=cx.xbuf, src_bf=True)


def build_l1a():
    nc = bass.Bass("TRN2", target_bir_lowering=False, debug=False)
    xT = nc.dram_tensor("xT", [D, T], F32, kind="ExternalInput").ap()
    W = declare_l1a(nc)
    out = nc.dram_tensor("xo", [D, T], F32, kind="ExternalOutput").ap()
    att_sv = nc.dram_tensor("att_sv", [D, T], BF16, kind="Internal").ap()
    with ExitStack() as stack:
        kb = KB(nc, stack)
        cx = make_ctx(kb)
        cx = make_ctx_ffn(kb, cx)
        cx = make_ctx_l1a(kb, cx)
        emit_const_init(kb, cx)
        emit_l1a_init(kb, cx, W)
        kb.dma("sync", cx.xbuf[:, :, :], xT.rearrange("(c p) t -> p c t", p=128))
        for eng in ("scalar", "tensor", "vector", "gpsimd"):
            kb.wait(eng, kb.sx)
            kb.wait(eng, kb.ld)
        kb.wait("tensor", kb.v)
        emit_l1_attn(kb, cx, W, att_sv)
        kb.wait("sync", kb.g)
        kb.wait("sync", kb.v)
        kb.dma("sync", out.rearrange("(c p) t -> p c t", p=128), cx.xbuf[:, :, :])
        kb.finalize([(kb.sx, kb.sx.n), (kb.ld, kb.ld.n), (kb.dt, kb.dt.n)])
    return nc


def build_l1f():
    nc = bass.Bass("TRN2", target_bir_lowering=False, debug=False)
    xT = nc.dram_tensor("xT", [D, T], F32, kind="ExternalInput").ap()
    W = declare_l1f(nc)
    out = nc.dram_tensor("xo", [D, T], F32, kind="ExternalOutput").ap()
    with ExitStack() as stack:
        kb = KB(nc, stack)
        cx = make_ctx(kb)
        cx = make_ctx_ffn(kb, cx)
        cx = make_ctx_l1f(kb, cx)
        emit_const_init(kb, cx)
        kb.dma("sync", cx.xbuf[:, :, :], xT.rearrange("(c p) t -> p c t", p=128))
        for eng in ("scalar", "tensor", "vector", "gpsimd"):
            kb.wait(eng, kb.sx)
            kb.wait(eng, kb.ld)
        kb.wait("tensor", kb.v)
        emit_l1_ffn(kb, cx, W)
        kb.wait("sync", kb.g)
        kb.wait("sync", kb.v)
        kb.dma("sync", out.rearrange("(c p) t -> p c t", p=128), cx.xbuf[:, :, :])
        kb.finalize([(kb.sx, kb.sx.n), (kb.ld, kb.ld.n), (kb.dt, kb.dt.n)])
    return nc


def _phase(kb, eng):
    for s in (kb.pe, kb.v, kb.s, kb.g, kb.ld, kb.sx, kb.dt):
        if kb.ms.get(eng) is s:
            continue
        kb.wait(eng, s)


def emit_l0a_init(kb, cx, W):
    kb.dma("gpsimd", cx.cmask[:, :], W["cm"])


def emit_reshard(kb, cx, cin, cout):
    kb.wait("sync", kb.g)
    kb.wait("sync", kb.v)
    n1 = kb.dma("sync", cin.rearrange("(c p) t -> p c t", p=128), cx.xbuf[:, :, :])
    kb.wait("gpsimd", kb.sx, n1)
    cc = kb.sem("ccol")
    cc.n += 1
    kb.rec("gpsimd", lambda e, h=cc.h: e.collective_compute(
        "AllGather", ALU.bypass, replica_groups=[list(range(NCORES))],
        ins=[cin.opt()], outs=[cout.opt()]).then_inc(h, 1))
    kb.wait("gpsimd", cc, cc.n)
    nc = kb.nc
    for b in range(NCORES):
        def ld(e, b=b):
            pid = nc.partition_id(engines=[mybir.EngineType.Pool])
            view = cout.rearrange("(b c p) t -> b p c t", b=NCORES, p=128)[b]
            return e.dma_start(out=cx.xbuf[:, :, 256 * b:256 * b + 256],
                               in_=view[:, :, bass.ts(pid, 256)])
        kb.ld.n += 16
        kb.rec("gpsimd", lambda e, ld=ld: ld(e).then_inc(kb.ld.h, 16))
    kb.wait("gpsimd", kb.ld)


def build_fused():
    nc = bass.Bass("TRN2", target_bir_lowering=False, debug=False)
    xT = nc.dram_tensor("xT", [D, T], F32, kind="ExternalInput").ap()
    Wa = {}
    Wa["wcd"] = nc.dram_tensor("wcd", [D, 512], F32, kind="ExternalInput").ap()
    Wa["wqu"] = nc.dram_tensor("wqu", [512, D], F32, kind="ExternalInput").ap()
    Wa["wkk"] = nc.dram_tensor("wkk", [512, 128], F32, kind="ExternalInput").ap()
    Wa["wv"] = nc.dram_tensor("wv", [512, 64], F32, kind="ExternalInput").ap()
    Wa["wcp"] = nc.dram_tensor("wcp", [D, D], F32, kind="ExternalInput").ap()
    Wa["cm"] = nc.dram_tensor("cm", [128, 640], F32, kind="ExternalInput").ap()
    Wf = declare_l0ffn(nc)
    W1a = declare_l1a(nc)
    W1f = declare_l1f(nc)
    out = nc.dram_tensor("xo", [D, T], F32, kind="ExternalOutput").ap()
    x1sv = nc.dram_tensor("x1sv", [D, T], F32, kind="Internal").ap()
    att_sv = nc.dram_tensor("att_sv", [D, T], BF16, kind="Internal").ap()
    cin = nc.dram_tensor("cin", [D, T], F32, kind="Internal")
    cout = nc.dram_tensor("cout", [NCORES * D, T], F32, kind="Internal")
    with ExitStack() as stack:
        kb = KB(nc, stack)
        cx = make_ctx(kb)
        cx = make_ctx_ffn(kb, cx)
        cx = make_ctx_l1a(kb, cx)
        cx = make_ctx_l1f(kb, cx)
        emit_const_init(kb, cx)
        emit_l0a_init(kb, cx, Wa)
        kb.dma("sync", cx.xbuf[:, :, :], xT.rearrange("(c p) t -> p c t", p=128))
        for eng in ("scalar", "tensor", "vector", "gpsimd"):
            kb.wait(eng, kb.sx)
            kb.wait(eng, kb.ld)
        kb.wait("tensor", kb.v)
        kb.wait("vector", kb.ld)
        emit_l0_attn(kb, cx, Wa)
        _phase(kb, "gpsimd")
        emit_ffn_init(kb, cx, Wf)
        kb.wait("vector", kb.ld)
        kb.wait("tensor", kb.ld)
        emit_l0f_block(kb, cx, Wf, x1sv, 6)
        _phase(kb, "gpsimd")
        emit_l1a_init(kb, cx, W1a)
        kb.wait("vector", kb.ld)
        kb.wait("tensor", kb.ld)
        emit_l1_attn(kb, cx, W1a, att_sv)
        _phase(kb, "sync")
        emit_reshard(kb, cx, cin.ap(), cout.ap())
        _phase(kb, "gpsimd")
        _phase(kb, "vector")
        _phase(kb, "scalar")
        _phase(kb, "tensor")
        emit_l1_ffn(kb, cx, W1f)
        kb.wait("sync", kb.g)
        kb.wait("sync", kb.v)
        kb.dma("sync", out.rearrange("(c p) t -> p c t", p=128), cx.xbuf[:, :, :])
        kb.finalize([(kb.sx, kb.sx.n), (kb.ld, kb.ld.n), (kb.dt, kb.dt.n)])
    return nc


def kernel(**inputs):
    x = _to_f32(inputs["L0_attn_x"])
    nc = build_fused()
    base = {}
    base.update(hostf_l0ffn_consts(inputs))
    base.update(host_l1a_consts(inputs))
    base.update(host_l1f_consts(inputs))
    maps = []
    for b in range(NCORES):
        m = dict(base)
        m.update(l0a_inputs(x[b].T, inputs))
        maps.append(m)
    res = run_bass_kernel_spmd(nc, maps, core_ids=list(range(NCORES)))
    out = np.empty((8, T, D), np.float32)
    for j in range(NCORES):
        oj = res.results[j]["xo"].T.reshape(8, 256, D)
        out[:, 256 * j:256 * (j + 1), :] = oj
    return out


# version 04 (final)
# speedup vs baseline: 1.2484x; 1.2484x over previous
import math
from contextlib import ExitStack

import numpy as np
import ml_dtypes

import concourse.bass as bass
import concourse.mybir as mybir
from concourse.bass_utils import run_bass_kernel_spmd

F32 = mybir.dt.float32
BF16 = mybir.dt.bfloat16
AF = mybir.ActivationFunctionType
ALU = mybir.AluOpType
AX = mybir.AxisListType

NCORES = 8
D = 1024
T = 2048
NCH = D // 128
NEG = -1.0e30

ENGS = ("sync", "scalar", "vector", "gpsimd", "tensor")


class Sem:
    __slots__ = ("h", "n")

    def __init__(self, h):
        self.h = h
        self.n = 0


class KB:

    def __init__(self, nc, stack):
        self.nc = nc
        self.stack = stack
        self.prog = {e: [] for e in ENGS}
        self._n = 0
        self.pe = self.sem("m_pe")
        self.v = self.sem("m_v")
        self.s = self.sem("m_s")
        self.g = self.sem("m_g")
        self.ld = self.sem("q_ld")
        self.sx = self.sem("q_sx")
        self.dt = self.sem("q_dt")
        self.ms = {"tensor": self.pe, "vector": self.v, "scalar": self.s,
                   "gpsimd": self.g}

    def sem(self, name=None):
        if name is None:
            name = f"s{self._n}"
            self._n += 1
        return Sem(self.stack.enter_context(self.nc.semaphore(name)))

    def sbuf(self, name, shape, dt):
        return self.stack.enter_context(self.nc.sbuf_tensor(name, shape, dt))

    def psum(self, name, shape, dt=F32):
        return self.stack.enter_context(self.nc.psum_tensor(name, shape, dt))

    def rec(self, eng, fn):
        self.prog[eng].append(fn)

    def wait(self, eng, sem, val=None):
        v = sem.n if val is None else val
        if v > 0:
            self.rec(eng, lambda e, h=sem.h, v=v: e.wait_ge(h, v))

    def op(self, eng, fn, inc=None, inc_n=1):
        if inc is not None:
            inc.n += inc_n
            self.rec(eng, lambda e, fn=fn, h=inc.h, k=inc_n: fn(e).then_inc(h, k))
        else:
            self.rec(eng, fn)

    def mark(self, eng):
        m = self.ms[eng]
        self.op(eng, lambda e: e.nop(), inc=m)
        return m.n

    def dma(self, eng, out, in_, sem=None, **kw):
        sem = sem or {"gpsimd": self.ld, "sync": self.sx, "scalar": self.dt}[eng]
        sem.n += 16
        self.rec(eng, lambda e, o=out, i=in_, h=sem.h, kw=dict(kw):
                 e.dma_start(out=o, in_=i, **kw).then_inc(h, 16))
        return sem.n

    def dmaT(self, eng, out, in_, sem=None):
        sem = sem or {"sync": self.sx, "scalar": self.dt}[eng]
        sem.n += 16
        self.rec(eng, lambda e, o=out, i=in_, h=sem.h:
                 e.dma_start_transpose(o, i).then_inc(h, 16))
        return sem.n

    def finalize(self, out_waits):
        for s, v in out_waits:
            self.wait("sync", s, v)
        with self.nc.Block() as block:
            for eng in ENGS:
                ops = self.prog[eng]
                if ops:
                    getattr(block, eng)(lambda e, ops=ops: [f(e) for f in ops])


class PsTile:
    def __init__(self, kb, name, shape, dt=F32):
        self.kb = kb
        self.t = kb.psum(name, shape, dt)
        self.sem = kb.sem(name + "_f")
        self.uses = 0


class PsPool:
    def __init__(self, kb, n, shape=(128, 512), dt=F32, tag="ps"):
        self.tiles = [PsTile(kb, f"{tag}{i}", list(shape), dt) for i in range(n)]
        self.i = 0

    def next(self):
        t = self.tiles[self.i % len(self.tiles)]
        self.i += 1
        return t


def mm_evac(kb, pool, mms, evac_eng, evac_fn, extra_mm_waits=(), evac_inc=None):
    t = pool.next()
    if t.uses > 0:
        kb.wait("tensor", t.sem, t.uses)
    t.uses += 1
    for s, v in extra_mm_waits:
        kb.wait("tensor", s, v)
    for i, fn in enumerate(mms):
        kb.op("tensor", lambda e, fn=fn, t=t: fn(e, t.t),
              inc=kb.pe if i == len(mms) - 1 else None)
    kb.wait(evac_eng, kb.pe)
    m = kb.ms[evac_eng]
    m.n += 1
    kb.rec(evac_eng, lambda e, fn=evac_fn, t=t, h=m.h: fn(e, t.t).then_inc(h, 1))
    t.sem.n += 1
    kb.rec(evac_eng, lambda e, h=t.sem.h: e.nop().then_inc(h, 1))
    return t


class Rot:

    def __init__(self, views):
        self.views = list(views)
        self.pend = [[] for _ in self.views]
        self.i = 0

    def next(self):
        j = self.i % len(self.views)
        self.i += 1
        w = self.pend[j]
        self.pend[j] = []
        return self.views[j], j, w

    def done(self, j, sem, val):
        self.pend[j].append((sem, val))


def host_causal_masks():
    m = np.zeros((128, 640), np.float32)
    r = np.arange(128)
    m[:, 512:640] = np.where(r[None, :] > r[:, None], NEG, 0.0)
    return m


def host_window_masks():
    m = np.zeros((2, 128, 512), np.float32)
    m0 = np.zeros((2, 128, 512), np.float32)
    for p in range(2):
        for r in range(128):
            lo = 128 * p + r
            hi = lo + 256
            c = np.arange(512)
            bad = (c < lo) | (c > hi)
            m[p, r, bad] = NEG
            bad0 = bad | (c < 256)
            m0[p, r, bad0] = NEG
    return m, m0


def host_rotary_tables():
    n, dh, win = T, 64, 256
    half = np.arange(0, dh, 2, dtype=np.float64)
    inv_freq = 1.0 / 10000.0 ** (half / dh)
    t = np.arange(n, dtype=np.float64)
    freqs = t[:, None] * inv_freq[None, :]
    freqs = np.concatenate([freqs, freqs], axis=1)
    base = (half + 0.4 * dh) / (1.4 * dh)
    power = (t - n // 2) / (win // 2)
    scale = base[None, :] ** power[:, None]
    scale = np.concatenate([scale, scale], axis=1)
    cs, sn = np.cos(freqs), np.sin(freqs)
    A = (cs * scale).T
    B = (sn * scale).T
    Ai = (cs / scale).T
    Bi = (sn / scale).T
    out = [np.tile(x, (2, 1)).astype(np.float32) for x in (A, B, Ai, Bi)]
    return out


def host_rot_perm():
    P = np.zeros((128, 128), np.float32)
    for blk in range(2):
        o = blk * 64
        for dd in range(32):
            P[o + dd, o + dd + 32] = -1.0
        for dd in range(32, 64):
            P[o + dd, o + dd - 32] = 1.0
    return np.ascontiguousarray(P.T).astype(ml_dtypes.bfloat16)


def host_ehead():
    E = np.zeros((8, 16, 128), np.float32)
    for c in range(8):
        E[c, 2 * c, 0:64] = 1.0
        E[c, 2 * c + 1, 64:128] = 1.0
    return E


def host_latent_masks():
    m = np.zeros((16, 128, 64), np.float32)
    for qt in range(16):
        for r in range(128):
            q = 128 * qt + r
            m[qt, r, q // 32 + 1:] = NEG
    return m


def make_ctx(kb, n_f32scr=12):
    cx = type("Ctx", (), {})()
    cx.kb = kb
    cx.xbuf = kb.sbuf("xbuf", [128, NCH, T], F32)
    cx.bigA = kb.sbuf("bigA", [128, NCH, T], BF16)
    cx.bigB = kb.sbuf("bigB", [128, NCH, T], BF16)
    cx.bigC = kb.sbuf("bigC", [128, NCH, T], BF16)
    cx.f32scr = kb.sbuf("f32scr", [128, n_f32scr, 512], F32)
    cx.rows = kb.sbuf("lnrows", [2, 4 * 512], F32)
    cx.stt = kb.sbuf("stt", [128, 16], F32)
    cx.wrot = Rot([kb.sbuf(f"w{i}", [128, 8 * 128], BF16) for i in range(2)])
    cx.wbig = Rot([kb.sbuf("wbig", [128, 22 * 128], BF16)])
    cx.ones1_f32 = kb.sbuf("ones1", [128, 1], F32)
    cx.ones1_bf = kb.sbuf("ones1b", [128, 1], BF16)
    cx.cvals = kb.sbuf("cvals", [128, 4], F32)
    cx.ones_row_f32 = kb.sbuf("onesr", [1, 128], F32)
    cx.kdup = cx.f32scr[:, 0:2, :].bitcast(BF16).rearrange("p a t -> p (a t)").unsqueeze(1).transpose([0, 2, 1]) if False else \
        cx.f32scr[:, 0:2, :].bitcast(BF16).rearrange("p a t -> p (a t)").rearrange("p (a t) -> p a t", a=1)
    cx.vtm = cx.f32scr[:, 2, :].bitcast(BF16).rearrange("p (a b) -> p a b", b=64)
    cx.wv_sb = cx.f32scr[:, 3, 0:128].bitcast(BF16).rearrange("p (a b) -> p a b", b=64)
    cx.cmask = cx.f32scr[:, 6, 0:320].bitcast(BF16)
    cx.pp = PsPool(kb, 4, (128, 512), F32, "pp")
    cx.pps = PsPool(kb, 2, (1, 512), F32, "pst")
    cx.ppbc = PsPool(kb, 2, (128, 512), F32, "pbc")
    return cx


def emit_const_init(kb, cx):
    kb.op("vector", lambda e: e.memset(cx.ones1_f32[:, :], 1.0))
    kb.op("vector", lambda e: e.memset(cx.ones_row_f32[:, :], 1.0))
    kb.op("vector", lambda e: e.memset(cx.ones1_bf[:, :], 1.0))
    kb.op("vector", lambda e: e.memset(cx.cvals[:, 0:1], 1e-5), inc=kb.v)


def emit_ln(kb, cx, src3, t_lo, t_hi, dst3=None, dst_off=0, src_off=0,
            gamma_col=None, beta_col=None, resid3=None, n_ch=NCH, eps=1e-5,
            src_tl0=None, src_bf=False, resid_src3=None, rs_tl0=None):
    rows = cx.rows
    rv = lambda i: rows[0:1, 512 * i:512 * i + 512]
    mean, m2, msq, var = (rv(i) for i in range(4))
    std = msq
    a_ = var
    b_ = m2
    D_ = 128.0 * n_ch
    for tl in range(t_lo, t_hi, 512):
        w = 512
        sl = tl if src_tl0 is None else src_tl0 + (tl - t_lo)
        kb.wait("scalar", kb.g)
        kb.wait("scalar", kb.v)
        ts = cx.pps.next()
        ts2 = cx.pps.next()
        for t_ in (ts, ts2):
            if t_.uses > 0:
                kb.wait("tensor", t_.sem, t_.uses)
            t_.uses += 1
        pe_after = []
        for c in range(n_ch):
            sq = cx.f32scr[:, 8 + (c % 2), 0:w]
            if c >= 2:
                kb.wait("scalar", kb.pe, pe_after[c - 2])
            kb.op("scalar", lambda e, o=sq, i=src3[:, src_off + c, sl:sl + w]:
                  e.activation(o, i, AF.Square), inc=kb.s)
            kb.wait("tensor", kb.s)
            ones_l = cx.ones1_bf if src_bf else cx.ones1_f32
            kb.op("tensor", lambda e, o=ts.t[:, 0:w], l=ones_l[:, 0:1],
                  r=src3[:, src_off + c, sl:sl + w], s=(c == 0), p=(c == n_ch - 1):
                  e.matmul(o, l, r, start=s, stop=p))
            kb.op("tensor", lambda e, o=ts2.t[:, 0:w], l=cx.ones1_f32[:, 0:1],
                  r=sq, s=(c == 0), p=(c == n_ch - 1):
                  e.matmul(o, l, r, start=s, stop=p), inc=kb.pe)
            pe_after.append(kb.pe.n)
        kb.wait("vector", kb.pe)
        kb.op("vector", lambda e, o=mean, i=ts.t[0:1, 0:w]: e.tensor_scalar_mul(o, i, 1.0 / D_))
        kb.op("vector", lambda e, o=m2, i=ts2.t[0:1, 0:w]: e.tensor_scalar_mul(o, i, 1.0 / D_),
              inc=kb.v)
        ts.sem.n += 1
        ts2.sem.n += 1
        kb.rec("vector", lambda e, h=ts.sem.h: e.nop().then_inc(h, 1))
        kb.rec("vector", lambda e, h=ts2.sem.h: e.nop().then_inc(h, 1))
        kb.op("vector", lambda e, o=msq, i=mean: e.tensor_tensor(o, i, i, ALU.mult))
        kb.op("vector", lambda e, o=var, x=m2, y=msq: e.tensor_tensor(o, x, y, ALU.subtract),
              inc=kb.v)
        kb.wait("scalar", kb.v)
        kb.op("scalar", lambda e, o=std, i=var: e.activation(o, i, AF.Sqrt,
              bias=cx.cvals[0:1, 0:1]), inc=kb.s)
        kb.wait("vector", kb.s)
        kb.op("vector", lambda e, o=a_, i=std: e.reciprocal(o, i))
        kb.op("vector", lambda e, o=b_, x=mean, y=a_: e.tensor_tensor(o, x, y, ALU.mult),
              inc=kb.v)
        bca = cx.ppbc.next()
        bcb = cx.ppbc.next()
        for t_ in (bca, bcb):
            if t_.uses > 0:
                kb.wait("tensor", t_.sem, t_.uses)
            t_.uses += 1
        kb.wait("tensor", kb.v)
        kb.op("tensor", lambda e, o=bca.t[:, 0:w], l=cx.ones_row_f32[0:1, :], r=a_:
              e.matmul(o, l, r, start=True, stop=True))
        kb.op("tensor", lambda e, o=bcb.t[:, 0:w], l=cx.ones_row_f32[0:1, :], r=b_:
              e.matmul(o, l, r, start=True, stop=True), inc=kb.pe)
        kb.wait("vector", kb.pe)
        kb.wait("vector", kb.g)
        g_after = []
        for c in range(n_ch):
            u = cx.f32scr[:, 10 + (c % 2), 0:w]
            if resid3 is not None and c >= 2:
                kb.wait("vector", kb.g, g_after[c - 2])
            kb.op("vector", lambda e, o=u, i=src3[:, src_off + c, sl:sl + w],
                  b=bca.t[:, 0:w]: e.tensor_tensor(o, i, b, ALU.mult))
            last = (c == n_ch - 1)
            if resid3 is not None:
                u2 = cx.f32scr[:, 8 + (c % 2), 0:w]
                kb.op("vector", lambda e, o=u2, i=u, b=bcb.t[:, 0:w]:
                      e.tensor_tensor(o, i, b, ALU.subtract))
                u3 = u
                rsrc = resid3 if resid_src3 is None else resid_src3
                rl = tl if rs_tl0 is None else rs_tl0 + (tl - t_lo)
                kb.op("vector", lambda e, o=u3, i=u2, x=rsrc[:, c, rl:rl + w]:
                      e.tensor_tensor(o, i, x, ALU.add), inc=kb.v)
                kb.wait("gpsimd", kb.v)
                kb.op("gpsimd", lambda e, o=resid3[:, c, tl:tl + w], i=u3:
                      e.tensor_copy(o, i), inc=kb.g)
                g_after.append(kb.g.n)
            elif gamma_col is not None:
                u2 = cx.f32scr[:, 8 + (c % 2), 0:w]
                kb.op("vector", lambda e, o=u2, i=u, b=bcb.t[:, 0:w]:
                      e.tensor_tensor(o, i, b, ALU.subtract))
                kb.op("vector", lambda e, o=dst3[:, dst_off + c, tl:tl + w], i=u2,
                      g=gamma_col[:, c:c + 1], bb=beta_col[:, c:c + 1]:
                      e.tensor_scalar(o, i, g, bb, ALU.mult, ALU.add),
                      inc=kb.v if last else None)
            else:
                kb.op("vector", lambda e, o=dst3[:, dst_off + c, tl:tl + w], i=u,
                      b=bcb.t[:, 0:w]: e.tensor_tensor(o, i, b, ALU.subtract),
                      inc=kb.v if last else None)
        bca.sem.n += 1
        bcb.sem.n += 1
        kb.rec("vector", lambda e, h=bca.sem.h: e.nop().then_inc(h, 1))
        kb.rec("vector", lambda e, h=bcb.sem.h: e.nop().then_inc(h, 1))


def emit_linear(kb, cx, w_dram, nk, in3, in_off=0, out3=None, out_off=0,
                m0=0, nm=None, t_lo=0, t_hi=T, evac_eng="vector",
                evac_scale=None, evac_f32=False, evac_fn=None, w_ready=None,
                out_tl0=None):
    wview = w_dram.rearrange("(c p) m -> p c m", p=128)
    M = w_dram.shape[1]
    nm = nm if nm is not None else M // 128
    rot = cx.wrot if nk <= 8 else cx.wbig
    for mi in range(nm):
        wsl, wj, wpend = rot.next()
        for s, v in wpend:
            kb.wait("gpsimd", s, v)
        mcols = slice(128 * (m0 + mi), 128 * (m0 + mi) + 128)
        nld = kb.dma("gpsimd", wsl[:, 0:nk * 128].rearrange("p (c m) -> p c m", m=128),
                     wview[:, 0:nk, mcols])
        wslv = wsl[:, 0:nk * 128].rearrange("p (c m) -> p c m", m=128)
        for tl in range(t_lo, t_hi, 512):
            w = min(512, t_hi - tl)
            mms = []
            for kc in range(nk):
                mms.append(lambda e, pt, kc=kc, tl=tl, w=w, wslv=wslv:
                           e.matmul(pt[:, 0:w], wslv[:, kc, :],
                                    in3[:, in_off + kc, tl:tl + w],
                                    start=(kc == 0), stop=(kc == nk - 1)))
            otl = tl if out_tl0 is None else out_tl0 + (tl - t_lo)
            dst = (out3[:, out_off + mi, otl:otl + w] if out3 is not None else None)
            if evac_fn is not None:
                ev = lambda e, pt, mi=mi, tl=tl, w=w, dst=dst: evac_fn(e, pt, mi, tl, w, dst)
            elif evac_scale is not None:
                ev = lambda e, pt, dst=dst, w=w: e.tensor_scalar_mul(dst, pt[:, 0:w], evac_scale)
            else:
                ev = lambda e, pt, dst=dst, w=w: e.tensor_copy(dst, pt[:, 0:w])
            waits = [(kb.ld, nld)]
            if w_ready is not None:
                if isinstance(w_ready, list):
                    waits.extend(w_ready)
                else:
                    waits.append(w_ready)
            tile = mm_evac(kb, cx.pp, mms, evac_eng, ev, extra_mm_waits=waits)
        rot.done(wj, kb.pe, kb.pe.n)


def emit_attn_causal(kb, cx, q3, kdup, vtm, att3, masks_sb):
    smrot = Rot([(cx.bigB, 0), (cx.bigB, 1)])
    perot = Rot([(cx.bigB, 2), (cx.bigB, 3)])
    pnrot = Rot([(cx.bigB, 4), (cx.bigB, 5)])
    ptrot = Rot([(cx.bigB, 6), (cx.bigB, 7)])
    nmrot = Rot(list(range(4)))
    suro = Rot(list(range(4, 8)))
    rsro = Rot(list(range(8, 12)))
    stc = lambda j: cx.stt[:, j:j + 1]
    for h in range(16):
        pb = (h % 2) * 64
        ch = h // 2
        for qt in range(16):
            span = 128 * (qt + 1)
            nkc = (qt + 4) // 4
            (smb, smc), smj, smw = smrot.next()
            sm = lambda a, b, smb=smb, smc=smc: smb[:, smc, a:b]
            qsl = q3[pb:pb + 64, ch, 128 * qt:128 * qt + 128]
            for kc in range(nkc):
                w = min(512, span - 512 * kc)
                diag = (kc == nkc - 1)
                mms = [lambda e, pt, w=w, kc=kc, qsl=qsl, pb=pb:
                       e.matmul(pt[:, 0:w], qsl,
                                kdup[pb:pb + 64, 0, 512 * kc:512 * kc + w],
                                start=True, stop=True)]
                if diag:
                    ev = (lambda e, pt, w=w, kc=kc, sm=sm:
                          e.scalar_tensor_tensor(sm(512 * kc, 512 * kc + w),
                                                 pt[:, 0:w], 1.0,
                                                 masks_sb[:, 640 - w:640],
                                                 ALU.mult, ALU.add))
                else:
                    ev = (lambda e, pt, w=w, kc=kc, sm=sm:
                          e.tensor_copy(sm(512 * kc, 512 * kc + w), pt[:, 0:w]))
                if kc == 0:
                    for s_, v_ in smw:
                        kb.wait("vector", s_, v_)
                mm_evac(kb, cx.pp, mms, "vector", ev)
            nmj0, nmj, nmw = nmrot.next()
            for s_, v_ in nmw:
                kb.wait("vector", s_, v_)
            kb.op("vector", lambda e, o=stc(nmj0), a=0, b=span, sm=sm:
                  e.reduce_max(o, sm(a, b), axis=AX.X, negate=True), inc=kb.v)
            suj0, suj, suw = suro.next()
            (peb, pec), pej, pew = perot.next()
            kb.wait("scalar", kb.v)
            for s_, v_ in pew + suw:
                kb.wait("scalar", s_, v_)
            kb.op("scalar", lambda e, o=peb[:, pec, 0:span], a=0, b=span,
                  bi=stc(nmj0), ac=stc(suj0), sm=sm:
                  e.activation(o, sm(a, b), AF.Exp, bias=bi, accum_out=ac),
                  inc=kb.s)
            smrot.done(smj, kb.s, kb.s.n)
            nmrot.done(nmj, kb.s, kb.s.n)
            rsj0, rsj, rsw = rsro.next()
            kb.wait("vector", kb.s)
            for s_, v_ in rsw:
                kb.wait("vector", s_, v_)
            kb.op("vector", lambda e, o=stc(rsj0), i=stc(suj0):
                  e.reciprocal(o, i), inc=kb.v)
            suro.done(suj, kb.v, kb.v.n)
            (pnb, pnc), pnj, pnw = pnrot.next()
            kb.wait("scalar", kb.v)
            for s_, v_ in pnw:
                kb.wait("scalar", s_, v_)
            kb.op("scalar", lambda e, o=pnb[:, pnc, 0:span], i=peb[:, pec, 0:span],
                  r=stc(rsj0): e.mul(o, i, r), inc=kb.s)
            perot.done(pej, kb.s, kb.s.n)
            rsro.done(rsj, kb.s, kb.s.n)
            (ptb, ptc), ptj, ptw = ptrot.next()
            for s_, v_ in ptw:
                kb.wait("scalar", s_, v_)
            nkb = span // 128
            ndt = kb.dmaT("scalar",
                          ptb[:, ptc, 0:span].rearrange("p (kb q) -> p kb q", q=128),
                          pnb[:, pnc, 0:span].rearrange("q (kb p) -> q kb p", p=128))
            pnrot.done(pnj, kb.dt, ndt)
            mms = []
            for kb_ in range(nkb):
                mms.append(lambda e, pt, kb_=kb_, ptb=ptb, ptc=ptc, nkb=nkb:
                           e.matmul(pt[0:64, 0:128], vtm[:, kb_, :],
                                    ptb[:, ptc, 128 * kb_:128 * kb_ + 128],
                                    start=(kb_ == 0), stop=(kb_ == nkb - 1)))
            ev = (lambda e, pt, pb=pb, ch=ch, qt=qt:
                  e.copy(att3[pb:pb + 64, ch, 128 * qt:128 * qt + 128],
                         pt[0:64, 0:128]))
            mm_evac(kb, cx.pp, mms, "scalar", ev,
                    extra_mm_waits=[(kb.dt, ndt)])
            ptrot.done(ptj, kb.pe, kb.pe.n)


def emit_l0_attn(kb, cx, W):
    lat = cx.bigB
    emit_ln(kb, cx, cx.xbuf, 0, T, dst3=cx.bigA)
    kb.wait("tensor", kb.v)
    emit_linear(kb, cx, W["wcd"], 8, cx.bigA, out3=lat, nm=4)
    kb.wait("tensor", kb.v)
    emit_linear(kb, cx, W["wqu"], 4, lat, out3=cx.bigC, nm=8)
    emit_linear(kb, cx, W["wkk"], 4, lat, out3=cx.kdup, nm=1, evac_scale=0.125)
    nld = kb.dma("gpsimd", cx.wv_sb[:, :, :],
                 W["wv"].rearrange("(c p) m -> p c m", p=128))
    kb.wait("tensor", kb.v)
    for tt in range(16):
        mms = []
        for kc in range(4):
            mms.append(lambda e, pt, kc=kc, tt=tt:
                       e.matmul(pt[:, 0:64], lat[:, kc, 128 * tt:128 * tt + 128],
                                cx.wv_sb[:, kc, :], start=(kc == 0), stop=(kc == 3)))
        ev = lambda e, pt, tt=tt: e.tensor_copy(cx.vtm[:, tt, :], pt[:, 0:64])
        mm_evac(kb, cx.pp, mms, "vector", ev,
                extra_mm_waits=[(kb.ld, nld)] if tt == 0 else ())
    kb.wait("tensor", kb.v)
    emit_attn_causal(kb, cx, cx.bigC, cx.kdup, cx.vtm, cx.bigA, cx.cmask)
    kb.wait("tensor", kb.s)
    for tl in range(0, T, 512):
        emit_linear(kb, cx, W["wcp"], 8, cx.bigA, nm=8, t_lo=tl, t_hi=tl + 512,
                    w_ready=(kb.s, kb.s.n),
                    evac_fn=lambda e, pt, mi, tl_, w, dst:
                    e.tensor_copy(cx.f32scr[:, mi, 0:w], pt[:, 0:w]))
        kb.wait("scalar", kb.v)
        kb.wait("gpsimd", kb.v)
        emit_ln(kb, cx, cx.f32scr, tl, tl + 512, resid3=cx.xbuf, src_tl0=0)


def build_l0a():
    nc = bass.Bass("TRN2", target_bir_lowering=False, debug=False)
    xT = nc.dram_tensor("xT", [D, T], F32, kind="ExternalInput").ap()
    W = {}
    W["wcd"] = nc.dram_tensor("wcd", [D, 512], F32, kind="ExternalInput").ap()
    W["wqu"] = nc.dram_tensor("wqu", [512, D], F32, kind="ExternalInput").ap()
    W["wkk"] = nc.dram_tensor("wkk", [512, 128], F32, kind="ExternalInput").ap()
    W["wv"] = nc.dram_tensor("wv", [512, 64], F32, kind="ExternalInput").ap()
    W["wcp"] = nc.dram_tensor("wcp", [D, D], F32, kind="ExternalInput").ap()
    cm = nc.dram_tensor("cm", [128, 640], F32, kind="ExternalInput").ap()
    out = nc.dram_tensor("xo", [D, T], F32, kind="ExternalOutput").ap()
    with ExitStack() as stack:
        kb = KB(nc, stack)
        cx = make_ctx(kb)
        emit_const_init(kb, cx)
        kb.dma("sync", cx.xbuf[:, :, :], xT.rearrange("(c p) t -> p c t", p=128))
        kb.dma("gpsimd", cx.cmask[:, :], cm)
        for eng in ("scalar", "tensor", "vector", "gpsimd"):
            kb.wait(eng, kb.sx)
        kb.wait("vector", kb.ld)
        kb.wait("tensor", kb.v)
        emit_l0_attn(kb, cx, W)
        kb.wait("sync", kb.g)
        kb.dma("sync", out.rearrange("(c p) t -> p c t", p=128), cx.xbuf[:, :, :])
        kb.finalize([(kb.sx, kb.sx.n)])
    return nc


def l0a_inputs(x_b_T, inp):
    wk = inp["L0_attn_Wkv_up"][:, :64]
    return {
        "xT": np.ascontiguousarray(x_b_T, dtype=np.float32),
        "wcd": np.ascontiguousarray(inp["L0_attn_Wc_down"], np.float32),
        "wqu": np.ascontiguousarray(inp["L0_attn_Wq_up"], np.float32),
        "wkk": np.ascontiguousarray(np.concatenate([wk, wk], axis=1), np.float32),
        "wv": np.ascontiguousarray(inp["L0_attn_Wkv_up"][:, 64:], np.float32),
        "wcp": np.ascontiguousarray(inp["L0_attn_Wc_proj"], np.float32),
        "cm": host_causal_masks(),
    }


def _to_f32(a):
    return np.ascontiguousarray(np.asarray(a), np.float32)


def kernel_unfused(**inputs):
    x = _to_f32(inputs["L0_attn_x"])
    cores = list(range(NCORES))

    nc1 = build_l0a()
    maps1 = [l0a_inputs(x[b].T, inputs) for b in range(NCORES)]
    r1 = run_bass_kernel_spmd(nc1, maps1, core_ids=cores)
    x1T = [np.ascontiguousarray(r1.results[b]["xo"]) for b in range(NCORES)]

    nc2 = build_l0f()
    base2 = hostf_l0ffn_consts(inputs)
    maps2 = [{**base2, "xT": x1T[b]} for b in range(NCORES)]
    r2 = run_bass_kernel_spmd(nc2, maps2, core_ids=cores)
    x2T = [np.ascontiguousarray(r2.results[b]["xo"]) for b in range(NCORES)]

    nc3 = build_l1a()
    base3 = host_l1a_consts(inputs)
    maps3 = [{**base3, "xT": x2T[b]} for b in range(NCORES)]
    r3 = run_bass_kernel_spmd(nc3, maps3, core_ids=cores)
    x3T = [np.ascontiguousarray(r3.results[b]["xo"]) for b in range(NCORES)]

    nc4 = build_l1f()
    base4 = host_l1f_consts(inputs)
    maps4 = []
    for j in range(NCORES):
        xj = np.concatenate([x3T[b][:, 256 * j:256 * (j + 1)] for b in range(NCORES)],
                            axis=1)
        maps4.append({**base4, "xT": np.ascontiguousarray(xj)})
    r4 = run_bass_kernel_spmd(nc4, maps4, core_ids=cores)

    out = np.empty((8, T, D), np.float32)
    for j in range(NCORES):
        oj = r4.results[j]["xo"].T.reshape(8, 256, D)
        out[:, 256 * j:256 * (j + 1), :] = oj
    return out


def hostf_l0ffn_consts(inp):
    DI = 2730
    DIP = 2816
    W1 = np.asarray(inp["L0_ffn_W1"], np.float32)
    b1 = np.asarray(inp["L0_ffn_b1"], np.float32)
    W2 = np.asarray(inp["L0_ffn_W2"], np.float32)
    W1p = np.zeros((6, 1024, 2 * DIP), np.float32)
    W1p[:, :, 0:DI] = W1[:, :, 0:DI]
    W1p[:, :, DIP:DIP + DI] = W1[:, :, DI:]
    b1p = np.zeros((6, 2 * DIP), np.float32)
    b1p[:, 0:DI] = b1[:, 0:DI]
    b1p[:, DIP:DIP + DI] = b1[:, DI:]
    W2p = np.zeros((6, DIP, 1024), np.float32)
    W2p[:, 0:DI, :] = W2
    qs = np.asarray(inp["L0_ffn_q_scale"], np.float32)
    ks = np.asarray(inp["L0_ffn_k_scale"], np.float32)
    qscol = np.stack([np.tile(qs, (1, 2)), np.tile(ks, (1, 2))], axis=2)
    lng = np.asarray(inp["L0_ffn_ln_g"], np.float32).reshape(6, 8, 128).transpose(0, 2, 1).copy()
    lnb = np.asarray(inp["L0_ffn_ln_b"], np.float32).reshape(6, 8, 128).transpose(0, 2, 1).copy()
    b1col = b1p.reshape(6, 44, 128).transpose(0, 2, 1).copy()
    b2col = np.asarray(inp["L0_ffn_b2"], np.float32).reshape(6, 8, 128).transpose(0, 2, 1).copy()
    A, B, Ai, Bi = host_rotary_tables()
    return {
        "f_wqkv": np.ascontiguousarray(inp["L0_ffn_Wqkv"], np.float32),
        "f_wo": np.ascontiguousarray(inp["L0_ffn_Wo"], np.float32),
        "f_w1": W1p, "f_w2": W2p,
        "f_qscol": qscol, "f_lng": lng, "f_lnb": lnb,
        "f_b1col": b1col, "f_b2col": b2col,
        "f_tbl": np.stack([A, B, 8.0 * Ai, 8.0 * Bi]),
        "f_wmask": np.stack(list(host_window_masks()[0]) + list(host_window_masks()[1])),
        "f_ptc": host_rot_perm().astype(np.float32),
        "f_e2c": np.concatenate([np.repeat([[1.0, 0.0]], 64, 0),
                                 np.repeat([[0.0, 1.0]], 64, 0)]).astype(np.float32),
        "f_e2t": np.concatenate([np.repeat([[1.0, 0.0]], 64, 0),
                                 np.repeat([[0.0, 1.0]], 64, 0)]).astype(np.float32).T.copy(),
    }


def declare_l0ffn(nc):
    W = {}
    W["f_wqkv"] = nc.dram_tensor("f_wqkv", [6, D, 3072], F32, kind="ExternalInput").ap()
    W["f_wo"] = nc.dram_tensor("f_wo", [6, D, D], F32, kind="ExternalInput").ap()
    W["f_w1"] = nc.dram_tensor("f_w1", [6, D, 5632], F32, kind="ExternalInput").ap()
    W["f_w2"] = nc.dram_tensor("f_w2", [6, 2816, D], F32, kind="ExternalInput").ap()
    W["f_qscol"] = nc.dram_tensor("f_qscol", [6, 128, 2], F32, kind="ExternalInput").ap()
    W["f_lng"] = nc.dram_tensor("f_lng", [6, 128, 8], F32, kind="ExternalInput").ap()
    W["f_lnb"] = nc.dram_tensor("f_lnb", [6, 128, 8], F32, kind="ExternalInput").ap()
    W["f_b1col"] = nc.dram_tensor("f_b1col", [6, 128, 44], F32, kind="ExternalInput").ap()
    W["f_b2col"] = nc.dram_tensor("f_b2col", [6, 128, 8], F32, kind="ExternalInput").ap()
    W["f_tbl"] = nc.dram_tensor("f_tbl", [4, 128, T], F32, kind="ExternalInput").ap()
    W["f_wmask"] = nc.dram_tensor("f_wmask", [4, 128, 512], F32, kind="ExternalInput").ap()
    W["f_ptc"] = nc.dram_tensor("f_ptc", [128, 128], F32, kind="ExternalInput").ap()
    W["f_e2c"] = nc.dram_tensor("f_e2c", [128, 2], F32, kind="ExternalInput").ap()
    W["f_e2t"] = nc.dram_tensor("f_e2t", [2, 128], F32, kind="ExternalInput").ap()
    return W


def make_ctx_ffn(kb, cx):
    cx.tbl = kb.sbuf("tbl", [128, 2, 1024], BF16)
    cx.wmask = cx.f32scr[:, 4:6, :].bitcast(BF16).rearrange("p a t -> p (a t)").rearrange("p (a t) -> p a t", a=4)
    cx.ptc = kb.sbuf("ptc", [128, 128], BF16)
    cx.qtmp = cx.bigB[:, 4, :].rearrange("p (a t) -> p a t", t=1024)
    cx.khalo = cx.f32scr[:, 0:2, :].bitcast(BF16).rearrange("p a t -> p (a t)").rearrange("p (a t) -> p a t", a=8)
    cx.vhalo = cx.f32scr[:, 2:4, :].bitcast(BF16).rearrange("p a t -> p (a t)").rearrange("p (a t) -> p a t", a=2)
    cx.xsv = kb.sbuf("xsv", [128, 8, 4], F32)
    cx.hrows = cx.rows[0:2, 0:1024]
    cx.rotm = kb.sbuf("rotm", [128, 16], F32)
    cx.e2c = kb.sbuf("e2c", [128, 2], F32)
    cx.e2t = kb.sbuf("e2t", [2, 128], F32)
    cx.colc = kb.sbuf("colc", [128, 64], F32)
    cx.colc2 = kb.sbuf("colc2", [128, 8], F32)
    cx.xs = cx.bigA[:, :, 0:1024]
    cx.vfm = cx.bigA[:, :, 1024:2048]
    cx.att_f = cx.bigA[:, :, 1024:2048]
    cx.q3 = cx.bigC[:, :, 0:1024]
    cx.k3 = cx.bigC[:, :, 1024:2048]
    cx.vtm_f = cx.bigB[:, 0:4, :].rearrange("p a (b t) -> p (a b) t", t=1024)
    s4 = cx.bigB[:, 4, :]
    s5 = cx.bigB[:, 5, :]
    cx.w_sm = [s4[:, 0:512], s4[:, 512:1024]]
    cx.w_pe = [s4[:, 1024:1536], s4[:, 1536:2048]]
    cx.w_pn = [s5[:, 0:512], s5[:, 512:1024]]
    cx.w_pt = [s5[:, 1024:1536], s5[:, 1536:2048]]
    cx.xs2 = cx.bigC[:, 0:2, :].rearrange("p a (b t) -> p (a b) t", t=512)
    cx.xn = cx.bigC[:, 2:4, :].rearrange("p a (b t) -> p (a b) t", t=512)
    cx.ggel = cx.bigA[:, 0:6, :].rearrange("p a (b t) -> p (a b) t", t=512)
    cx.uge = cx.bigB[:, 0:6, :].rearrange("p a (b t) -> p (a b) t", t=512)
    return cx


def emit_ffn_init(kb, cx, W):
    kb.dma("gpsimd", cx.ptc[:, :], W["f_ptc"])
    kb.dma("gpsimd", cx.e2c[:, :], W["f_e2c"])
    kb.dma("gpsimd", cx.e2t[:, :], W["f_e2t"])
    kb.dma("gpsimd", cx.wmask[:, :, :], W["f_wmask"].rearrange("m p t -> p m t"))


def emit_shift(kb, cx, dst3, gl0, gl1, sv_col=0, eng="gpsimd"):
    w = gl1 - gl0
    m = kb.ms[eng]
    for c in range(4):
        kb.op(eng, lambda e, o=dst3[:, c, 0:w], i=cx.xbuf[:, c, gl0:gl1]:
              e.tensor_copy(o, i))
    for c in range(4, 8):
        kb.op(eng, lambda e, o=dst3[:, c, 1:w], i=cx.xbuf[:, c, gl0:gl1 - 1]:
              e.tensor_copy(o, i))
        if gl0 == 0:
            kb.op(eng, lambda e, o=dst3[:, c, 0:1]: e.memset(o, 0.0),
                  inc=m if c == 7 else None)
        else:
            kb.op(eng, lambda e, o=dst3[:, c, 0:1],
                  i=cx.xsv[:, c, sv_col:sv_col + 1]:
                  e.tensor_copy(o, i), inc=m if c == 7 else None)


def emit_rotary(kb, cx, x3, blk_col, tblA, tblB):
    sqw = cx.f32scr[:, 8:10, :].rearrange("p a t -> p (a t)")
    n2 = cx.hrows[:, 0:512]
    nr = cx.hrows[:, 512:1024]
    for c in range(NCH):
        kb.wait("scalar", kb.v)
        kb.wait("scalar", kb.g)
        kb.op("scalar", lambda e, o=sqw[:, 0:1024], i=x3[:, c, :]:
              e.activation(o, i, AF.Square), inc=kb.s)
        for n in range(2):
            mm = [lambda e, pt, n=n: e.matmul(pt[0:2, 0:512], blk_col,
                                              sqw[:, 512 * n:512 * n + 512],
                                              start=True, stop=True)]
            ev = lambda e, pt: e.tensor_copy(n2[:, 0:512], pt[0:2, 0:512])
            mm_evac(kb, cx.pp, mm, "vector", ev,
                    extra_mm_waits=[(kb.s, kb.s.n), (kb.s, kb.s.n)])
            kb.wait("scalar", kb.v)
            kb.op("scalar", lambda e, o=nr[:, 0:512], i=n2[:, 0:512]:
                  e.activation(o, i, AF.Sqrt), inc=kb.s)
            kb.wait("vector", kb.s)
            kb.op("vector", lambda e, o=n2[:, 0:512], i=nr[:, 0:512]:
                  e.tensor_scalar_max(o, i, 1e-12))
            kb.op("vector", lambda e, o=nr[:, 0:512], i=n2[:, 0:512]:
                  e.reciprocal(o, i), inc=kb.v)
            mm = [lambda e, pt: e.matmul(pt[:, 0:512], cx.e2t[:, :],
                                         nr[:, 0:512],
                                         start=True, stop=True)]
            ev = (lambda e, pt, n=n, c=c:
                  e.scalar_tensor_tensor(cx.qtmp[:, 0, 512 * n:512 * n + 512],
                                         x3[:, c, 512 * n:512 * n + 512], 1.0,
                                         pt[:, 0:512], ALU.mult, ALU.mult))
            mm_evac(kb, cx.pp, mm, "vector", ev,
                    extra_mm_waits=[(kb.v, kb.v.n), (kb.g, kb.g.n)])
        kb.op("vector", lambda e, o=cx.qtmp[:, 1, :], i=cx.qtmp[:, 0, :], t=tblA:
              e.tensor_tensor(o, i, t, ALU.mult), inc=kb.v)
        for n in range(2):
            mm = [lambda e, pt, n=n: e.matmul(pt[:, 0:512], cx.ptc[:, :],
                                              cx.qtmp[:, 0, 512 * n:512 * n + 512],
                                              start=True, stop=True)]
            ev = (lambda e, pt, n=n:
                  e.scalar_tensor_tensor(cx.qtmp[:, 0, 512 * n:512 * n + 512],
                                         pt[:, 0:512], 1.0,
                                         tblB[:, 512 * n:512 * n + 512],
                                         ALU.mult, ALU.mult))
            mm_evac(kb, cx.pp, mm, "vector", ev,
                    extra_mm_waits=[(kb.v, kb.v.n)])
        kb.wait("gpsimd", kb.v)
        kb.op("gpsimd", lambda e, o=x3[:, c, :], a=cx.qtmp[:, 1, :], b=cx.qtmp[:, 0, :]:
              e.tensor_tensor(o, a, b, ALU.add), inc=kb.g)


def emit_attn_win(kb, cx, tc):
    smro = Rot([0, 1])
    pero = Rot([0, 1])
    pnro = Rot([0, 1])
    ptro = Rot([0, 1])
    nmro = Rot(list(range(4)))
    suro = Rot(list(range(4, 8)))
    rsro = Rot(list(range(8, 12)))
    stc = lambda j: cx.stt[:, j:j + 1]
    kg0 = kb.g.n
    for r in (smro, pero, pnro, ptro):
        for j in range(len(r.views)):
            r.done(j, kb.g, kg0)
    for h in range(16):
        pb = (h % 2) * 64
        ch = h // 2
        for qt in range(8):
            w = qt // 2
            p = qt % 2
            glob0 = (w == 0 and tc == 0)
            smj0, smj, smw = smro.next()
            sm = cx.w_sm[smj0]
            qsl = cx.q3[pb:pb + 64, ch, 128 * qt:128 * qt + 128]
            mms = []
            if not glob0:
                if w == 0:
                    ksrc = cx.khalo[pb:pb + 64, ch, 0:256]
                else:
                    ksrc = cx.k3[pb:pb + 64, ch, 256 * (w - 1):256 * w]
                mms.append(lambda e, pt, qsl=qsl, ksrc=ksrc:
                           e.matmul(pt[:, 0:256], qsl, ksrc, start=True, stop=True))
            mms.append(lambda e, pt, qsl=qsl, a=cx.k3[pb:pb + 64, ch, 256 * w:256 * w + 256]:
                       e.matmul(pt[:, 256:512], qsl, a, start=True, stop=True))
            mi = (2 + p) if glob0 else p
            if glob0:
                for s_, v_ in smw:
                    kb.wait("vector", s_, v_)
                kb.op("vector", lambda e, o=sm[:, 0:256]: e.memset(o, NEG))
                ev = (lambda e, pt, sm=sm, mi=mi:
                      e.scalar_tensor_tensor(sm[:, 256:512], pt[:, 256:512], 1.0,
                                             cx.wmask[:, mi, 256:512], ALU.mult, ALU.add))
            else:
                if qt == 0 or True:
                    for s_, v_ in smw:
                        kb.wait("vector", s_, v_)
                ev = (lambda e, pt, sm=sm, mi=mi:
                      e.scalar_tensor_tensor(sm[:, 0:512], pt[:, 0:512], 1.0,
                                             cx.wmask[:, mi, 0:512], ALU.mult, ALU.add))
            mm_evac(kb, cx.pp, mms, "vector", ev, extra_mm_waits=[(kb.g, kg0)])
            nmj0, nmj, nmw = nmro.next()
            for s_, v_ in nmw:
                kb.wait("vector", s_, v_)
            kb.op("vector", lambda e, o=stc(nmj0), sm=sm:
                  e.reduce_max(o, sm[:, 0:512], axis=AX.X, negate=True), inc=kb.v)
            suj0, suj, suw = suro.next()
            pej0, pej, pew = pero.next()
            peb = cx.w_pe[pej0]
            kb.wait("scalar", kb.v)
            for s_, v_ in pew + suw:
                kb.wait("scalar", s_, v_)
            kb.op("scalar", lambda e, o=peb[:, 0:512], sm=sm, bi=stc(nmj0), ac=stc(suj0):
                  e.activation(o, sm[:, 0:512], AF.Exp, bias=bi, accum_out=ac),
                  inc=kb.s)
            smro.done(smj, kb.s, kb.s.n)
            nmro.done(nmj, kb.s, kb.s.n)
            rsj0, rsj, rsw = rsro.next()
            kb.wait("vector", kb.s)
            for s_, v_ in rsw:
                kb.wait("vector", s_, v_)
            kb.op("vector", lambda e, o=stc(rsj0), i=stc(suj0):
                  e.reciprocal(o, i), inc=kb.v)
            suro.done(suj, kb.v, kb.v.n)
            pnj0, pnj, pnw = pnro.next()
            pnb = cx.w_pn[pnj0]
            for s_, v_ in pnw:
                kb.wait("vector", s_, v_)
            kb.op("vector", lambda e, o=pnb[:, 0:512], i=peb[:, 0:512], r=stc(rsj0):
                  e.tensor_scalar_mul(o, i, r), inc=kb.v)
            pero.done(pej, kb.v, kb.v.n)
            rsro.done(rsj, kb.v, kb.v.n)
            ptj0, ptj, ptw = ptro.next()
            ptb = cx.w_pt[ptj0]
            kb.wait("scalar", kb.v)
            for s_, v_ in ptw:
                kb.wait("scalar", s_, v_)
            ndt = kb.dmaT("scalar",
                          ptb[:, 0:512].rearrange("p (kb q) -> p kb q", q=128),
                          pnb[:, 0:512].rearrange("q (kb p) -> q kb p", p=128))
            pnro.done(pnj, kb.dt, ndt)
            sts = [2 * (w - 1) + j for j in range(4)]
            mms = []
            valid = [st for st in sts if not (st < 0 and tc == 0)]
            for idx, st in enumerate(valid):
                j = sts.index(st)
                if st < 0:
                    vsl = cx.vhalo[:, 2 + st, 64 * h:64 * h + 64]
                else:
                    vsl = cx.vtm_f[:, st, 64 * h:64 * h + 64]
                mms.append(lambda e, pt, j=j, vsl=vsl, s=(idx == 0), q=(idx == len(valid) - 1), ptb=ptb:
                           e.matmul(pt[0:64, 0:128], vsl,
                                    ptb[:, 128 * j:128 * j + 128],
                                    start=s, stop=q))
            ev = (lambda e, pt, pb=pb, ch=ch, qt=qt:
                  e.copy(cx.att_f[pb:pb + 64, ch, 128 * qt:128 * qt + 128],
                         pt[0:64, 0:128]))
            mm_evac(kb, cx.pp, mms, "scalar", ev, extra_mm_waits=[(kb.dt, ndt)])
            ptro.done(ptj, kb.pe, kb.pe.n)


def emit_l0_ffn_layer(kb, cx, W, li, do_attn=True, do_ffn=True, tcs=(0, 1)):
    n1 = kb.dma("gpsimd", cx.colc[:, 0:8], W["f_lng"][li])
    kb.dma("gpsimd", cx.colc[:, 8:16], W["f_lnb"][li])
    kb.dma("gpsimd", cx.colc[:, 16:60], W["f_b1col"][li])
    kb.dma("gpsimd", cx.colc2[:, 0:8], W["f_b2col"][li])
    nqs = kb.dma("gpsimd", cx.rotm[:, 0:2], W["f_qscol"][li])
    kb.wait("vector", kb.ld)
    kb.op("vector", lambda e: e.reciprocal(cx.rotm[:, 2:4], cx.rotm[:, 0:2]), inc=kb.v)
    kb.wait("scalar", kb.v)
    kb.op("scalar", lambda e: e.activation(cx.rotm[:, 4:6], cx.rotm[:, 2:4], AF.Square),
          inc=kb.s)
    kb.wait("vector", kb.s)
    kb.op("vector", lambda e: e.tensor_scalar(cx.rotm[:, 6:8], cx.e2c[:, 0:2],
                                              cx.rotm[:, 4:5], None, ALU.mult),
          )
    kb.op("vector", lambda e: e.tensor_scalar(cx.rotm[:, 8:10], cx.e2c[:, 0:2],
                                              cx.rotm[:, 5:6], None, ALU.mult),
          inc=kb.v)
    blkq = cx.rotm[:, 6:8]
    blkk = cx.rotm[:, 8:10]
    qcol = cx.rotm[:, 0:1]
    kcol = cx.rotm[:, 1:2]
    kb.wait("gpsimd", kb.v)
    kb.op("gpsimd", lambda e: e.tensor_copy(cx.xsv[:, 4:8, 0:1], cx.xbuf[:, 4:8, 1023:1024]),
          inc=kb.g)
    for tc in (tcs if do_attn else ()):
        gl0 = 1024 * tc
        ntb = 0
        for j in range(2):
            ntb = kb.dma("gpsimd", cx.tbl[:, j, :], W["f_tbl"][j][:, gl0:gl0 + 1024])
        kb.wait("gpsimd", kb.pe)
        kb.wait("gpsimd", kb.v)
        emit_shift(kb, cx, cx.xs, gl0, gl0 + 1024, sv_col=0)
        wq = W["f_wqkv"][li]
        kb.wait("tensor", kb.g)
        kb.wait("tensor", kb.ld, nqs)
        emit_linear(kb, cx, wq, 8, cx.xs, out3=cx.q3, m0=0, nm=8, t_lo=0, t_hi=1024,
                    evac_fn=lambda e, pt, mi_, tl, w_, dst:
                    e.tensor_scalar(cx.q3[:, mi_, tl:tl + w_], pt[:, 0:w_],
                                    qcol, None, ALU.mult))
        emit_linear(kb, cx, wq, 8, cx.xs, out3=cx.k3, m0=8, nm=8, t_lo=0, t_hi=1024,
                    evac_fn=lambda e, pt, mi_, tl, w_, dst:
                    e.tensor_scalar(cx.k3[:, mi_, tl:tl + w_], pt[:, 0:w_],
                                    kcol, None, ALU.mult))
        if tc == 1:
            kb.wait("vector", kb.g)
        emit_linear(kb, cx, wq, 8, cx.xs, out3=cx.vfm, m0=16, nm=8, t_lo=0, t_hi=1024)
        kb.wait("scalar", kb.v)
        kb.wait("scalar", kb.g)
        nvt = 0
        for c in range(8):
            nvt = kb.dmaT("scalar",
                          cx.vtm_f[:, :, 128 * c:128 * c + 128],
                          cx.vfm[:, c, :].rearrange("p (tt q) -> p tt q", q=128))
        kb.wait("vector", kb.ld, ntb)
        emit_rotary(kb, cx, cx.q3, blkq, cx.tbl[:, 0, :], cx.tbl[:, 1, :])
        kb.wait("gpsimd", kb.v)
        ntb2 = 0
        for j in range(2):
            ntb2 = kb.dma("gpsimd", cx.tbl[:, j, :], W["f_tbl"][2 + j][:, gl0:gl0 + 1024])
        kb.wait("vector", kb.ld, ntb2)
        emit_rotary(kb, cx, cx.k3, blkk, cx.tbl[:, 0, :], cx.tbl[:, 1, :])
        kb.wait("tensor", kb.dt, nvt)
        emit_attn_win(kb, cx, tc)
        if tc == 0:
            kb.wait("gpsimd", kb.pe)
            kb.wait("gpsimd", kb.s)
            kb.op("gpsimd", lambda e: e.tensor_copy(cx.khalo[:, :, :],
                                                    cx.k3[:, :, 768:1024]))
            kb.op("gpsimd", lambda e: e.tensor_copy(
                cx.vhalo[:, :, :].rearrange("p a t -> p (a t)"),
                cx.vtm_f[:, 6:8, :].rearrange("p a t -> p (a t)")), inc=kb.g)
        kb.wait("tensor", kb.s)
        emit_linear(kb, cx, W["f_wo"][li], 8, cx.att_f, nm=8, t_lo=0, t_hi=1024,
                    evac_eng="vector",
                    evac_fn=lambda e, pt, mi_, tl, w_, dst, gl0=gl0:
                    e.scalar_tensor_tensor(cx.xbuf[:, mi_, gl0 + tl:gl0 + tl + w_],
                                           pt[:, 0:w_], 1.0,
                                           cx.xbuf[:, mi_, gl0 + tl:gl0 + tl + w_],
                                           ALU.mult, ALU.add))
    if not do_ffn:
        return
    kb.wait("gpsimd", kb.v)
    for j, bc in enumerate((511, 1023, 1535)):
        kb.op("gpsimd", lambda e, j=j, bc=bc:
              e.tensor_copy(cx.xsv[:, 4:8, 1 + j:2 + j], cx.xbuf[:, 4:8, bc:bc + 1]),
              inc=kb.g if j == 2 else None)
    gcol = cx.colc[:, 0:8]
    bcol = cx.colc[:, 8:16]
    for tl in range(0, T, 512):
        kb.wait("gpsimd", kb.pe)
        kb.wait("gpsimd", kb.v)
        emit_shift(kb, cx, cx.xs2, tl, tl + 512, sv_col=tl // 512)
        emit_ln(kb, cx, cx.xs2, 0, 512, dst3=cx.xn, gamma_col=gcol, beta_col=bcol,
                src_bf=True)
        w1 = W["f_w1"][li]
        emit_linear(kb, cx, w1, 8, cx.xn, m0=22, nm=22, t_lo=0, t_hi=512,
                    evac_eng="scalar", w_ready=(kb.v, kb.v.n),
                    evac_fn=lambda e, pt, mi_, tl_, w_, dst:
                    e.activation(cx.ggel[:, mi_, 0:w_], pt[:, 0:w_], AF.Gelu,
                                 bias=cx.colc[:, 38 + mi_:39 + mi_]))
        kb.wait("tensor", kb.s)
        emit_linear(kb, cx, w1, 8, cx.xn, m0=0, nm=22, t_lo=0, t_hi=512,
                    evac_eng="vector", w_ready=(kb.s, kb.s.n),
                    evac_fn=lambda e, pt, mi_, tl_, w_, dst:
                    e.scalar_tensor_tensor(cx.uge[:, mi_, 0:w_], pt[:, 0:w_],
                                           cx.colc[:, 16 + mi_:17 + mi_],
                                           cx.ggel[:, mi_, 0:w_],
                                           ALU.add, ALU.mult))
        emit_linear(kb, cx, W["f_w2"][li], 22, cx.uge, nm=8, t_lo=0, t_hi=512,
                    evac_eng="vector", w_ready=(kb.v, kb.v.n),
                    evac_fn=lambda e, pt, mi_, tl_, w_, dst, tl0=tl:
                    e.scalar_tensor_tensor(cx.xbuf[:, mi_, tl0:tl0 + w_],
                                           pt[:, 0:w_],
                                           cx.colc2[:, mi_:mi_ + 1],
                                           cx.xbuf[:, mi_, tl0:tl0 + w_],
                                           ALU.add, ALU.add))


def emit_l0_ffn(kb, cx, W):
    for li in range(6):
        emit_l0_ffn_layer(kb, cx, W, li)


def emit_l0f_block(kb, cx, W, x1sv, n_layers=6):
    kb.wait("sync", kb.g)
    kb.wait("sync", kb.v)
    nst = kb.dma("sync", x1sv.rearrange("(c p) t -> p c t", p=128), cx.xbuf[:, :, :])
    kb.wait("vector", kb.sx, nst)
    emit_ln(kb, cx, cx.xbuf, 0, T, dst3=cx.xbuf)
    for li in range(n_layers):
        emit_l0_ffn_layer(kb, cx, W, li)
    kb.wait("sync", kb.pe)
    kb.wait("sync", kb.s)
    for tl in range(0, T, 512):
        kb.wait("sync", kb.v)
        kb.wait("sync", kb.g)
        nld = kb.dma("sync", cx.f32scr[:, 0:8, :],
                     x1sv.rearrange("(c p) t -> p c t", p=128)[:, :, tl:tl + 512])
        kb.wait("vector", kb.sx, nld)
        emit_ln(kb, cx, cx.xbuf, tl, tl + 512, resid3=cx.xbuf,
                resid_src3=cx.f32scr, rs_tl0=0)


def build_l0f(n_layers=6, dump_stream=False):
    nc = bass.Bass("TRN2", target_bir_lowering=False, debug=False)
    xT = nc.dram_tensor("xT", [D, T], F32, kind="ExternalInput").ap()
    W = declare_l0ffn(nc)
    out = nc.dram_tensor("xo", [D, T], F32, kind="ExternalOutput").ap()
    x1sv = nc.dram_tensor("x1sv", [D, T], F32, kind="Internal").ap()
    with ExitStack() as stack:
        kb = KB(nc, stack)
        cx = make_ctx(kb)
        cx = make_ctx_ffn(kb, cx)
        emit_const_init(kb, cx)
        emit_ffn_init(kb, cx, W)
        kb.dma("sync", cx.xbuf[:, :, :], xT.rearrange("(c p) t -> p c t", p=128))
        for eng in ("scalar", "tensor", "vector", "gpsimd"):
            kb.wait(eng, kb.sx)
            kb.wait(eng, kb.ld)
        kb.wait("tensor", kb.v)
        if dump_stream:
            for li in range(n_layers):
                emit_l0_ffn_layer(kb, cx, W, li)
        else:
            emit_l0f_block(kb, cx, W, x1sv, n_layers)
        kb.wait("sync", kb.g)
        kb.wait("sync", kb.v)
        kb.dma("sync", out.rearrange("(c p) t -> p c t", p=128), cx.xbuf[:, :, :])
        kb.finalize([(kb.sx, kb.sx.n), (kb.ld, kb.ld.n), (kb.dt, kb.dt.n)])
    return nc


def l0f_inputs(x_T, inp):
    d = hostf_l0ffn_consts(inp)
    d["xT"] = np.ascontiguousarray(x_T, np.float32)
    return d


def host_l1a_consts(inp):
    glw = np.asarray(inp["L1_attn_gate_local_w"], np.float32)
    grw = np.asarray(inp["L1_attn_gate_remote_w"], np.float32)
    glwb = np.zeros((8, 128, 16), np.float32)
    for c in range(8):
        for j in range(2):
            glwb[c, 64 * j:64 * j + 64, 2 * c + j] = glw[2 * c + j]
    grwB = np.zeros((8, 128, 128), np.float32)
    for c in range(8):
        for j in range(2):
            grwB[c, 64 * j:64 * j + 64, 64 * j:64 * j + 64] = grw[2 * c + j][:, None]
    m = np.zeros((2, 128, 512), np.float32)
    m0 = np.zeros((2, 128, 512), np.float32)
    for p in range(2):
        for r in range(128):
            q = 128 * p + r
            c = np.arange(512)
            bad = (c <= q) | (c > q + 256)
            m[p, r, bad] = NEG
            bad0 = bad | (c < 256)
            m0[p, r, bad0] = NEG
    return {
        "a_wq": np.ascontiguousarray(inp["L1_attn_wq"], np.float32),
        "a_wk": np.ascontiguousarray(inp["L1_attn_wk"], np.float32),
        "a_wv": np.ascontiguousarray(inp["L1_attn_wv"], np.float32),
        "a_wgq": np.ascontiguousarray(inp["L1_attn_wgq"], np.float32),
        "a_wgk": np.ascontiguousarray(inp["L1_attn_wgk"], np.float32),
        "a_wgv": np.ascontiguousarray(inp["L1_attn_wgv"], np.float32),
        "a_wro": np.ascontiguousarray(inp["L1_attn_r_out_w"], np.float32),
        "a_wout": np.ascontiguousarray(inp["L1_attn_w_out"], np.float32),
        "a_glwb": glwb, "a_grwB": grwB,
        "a_wmask": np.stack(list(m) + list(m0)),
        "a_lmask": host_latent_masks(),
    }


def declare_l1a(nc):
    W = {}
    for nm, shp in (("a_wq", [D, D]), ("a_wk", [D, D]), ("a_wv", [D, D]),
                    ("a_wgq", [D, 512]), ("a_wgk", [D, 512]), ("a_wgv", [D, 512]),
                    ("a_wro", [512, D]), ("a_wout", [D, D]),
                    ("a_glwb", [8, 128, 16]), ("a_grwB", [8, 128, 128]),
                    ("a_wmask", [4, 128, 512]), ("a_lmask", [16, 128, 64])):
        W[nm] = nc.dram_tensor(nm, shp, F32, kind="ExternalInput").ap()
    return W


def make_ctx_l1a(kb, cx):
    tblf = cx.tbl[:, :, :].rearrange("p a t -> p (a t)")
    cx.grwB = tblf[:, 0:1024].rearrange("p (a t) -> p a t", t=128)
    cx.glwb = tblf[:, 1024:1152].rearrange("p (a t) -> p a t", t=16)
    cx.px = tblf[:, 1152:1664].rearrange("p (a t) -> p a t", t=64)
    cx.gk1 = tblf[:, 1664:1920].rearrange("p (a t) -> p a t", t=64)
    cx.gvf = cx.bigB[:, 4, 0:512].rearrange("p (a t) -> p a t", t=128)
    cx.gvt = cx.f32scr[:, 7, 0:256].bitcast(BF16).rearrange("p (a t) -> p a t", t=128)
    cx.glT = kb.sbuf("glT", [128, 4, 16], F32)
    cx.q1 = cx.bigC[:, :, 0:512]
    cx.k1 = cx.bigC[:, :, 512:1024]
    cx.vf1 = cx.bigC[:, :, 1024:1536]
    cx.kh1 = cx.bigC[:, :, 1536:1792]
    cx.vh1 = cx.bigC[:, :, 1792:2048]
    cx.vt1 = cx.bigB[:, 0:2, :].rearrange("p a (b t) -> p (a b) t", t=1024)
    cx.att1 = cx.bigB[:, 2:4, :].rearrange("p a (b t) -> p (a b) t", t=512)
    cx.gq1 = cx.bigB[:, 6, :].rearrange("p (a t) -> p a t", t=512)
    cx.lat1 = cx.bigB[:, 7, :].rearrange("p (a t) -> p a t", t=512)
    cx.lmask = cx.f32scr[:, 6, :].bitcast(BF16).rearrange("p (a t) -> p a t", t=64)
    return cx


def emit_l1a_init(kb, cx, W):
    kb.dma("gpsimd", cx.wmask[:, :, :], W["a_wmask"].rearrange("m p t -> p m t"))
    kb.dma("gpsimd", cx.lmask[:, :, :], W["a_lmask"].rearrange("m p t -> p m t"))
    kb.dma("gpsimd", cx.glwb[:, :, :], W["a_glwb"].rearrange("c p t -> p c t"))
    kb.dma("gpsimd", cx.grwB[:, :, :], W["a_grwB"].rearrange("c p t -> p c t"))


def emit_l1_attn(kb, cx, W, att_sv):
    emit_ln(kb, cx, cx.xbuf, 0, T, dst3=cx.bigA)
    kb.wait("vector", kb.v)
    for c in range(NCH):
        pxf = cx.f32scr[:, 0, 0:64]
        kb.op("vector", lambda e, c=c, pxf=pxf:
              e.reduce_sum(pxf.unsqueeze(2),
                           cx.bigA[:, c, :].rearrange("p (g w) -> p g w", w=32),
                           axis=AX.X))
        kb.op("vector", lambda e, c=c, pxf=pxf:
              e.tensor_scalar_mul(cx.px[:, c, :], pxf, 1.0 / 32.0),
              inc=kb.v if c == NCH - 1 else None)
    kb.wait("tensor", kb.v)
    emit_linear(kb, cx, W["a_wgk"], 8, cx.px, out3=cx.gk1, nm=4, t_lo=0, t_hi=64)
    emit_linear(kb, cx, W["a_wgv"], 8, cx.px, out3=cx.gvf, nm=4, t_lo=0, t_hi=64)
    kb.wait("scalar", kb.v)
    ngv = kb.dmaT("scalar", cx.gvt[:, :, :], cx.gvf[:, :, :])
    smro = Rot([0, 1]); pero = Rot([0, 1]); pnro = Rot([0, 1]); ptro = Rot([0, 1])
    nmro = Rot(list(range(4))); suro = Rot(list(range(4, 8))); rsro = Rot(list(range(8, 12)))
    stc = lambda j: cx.stt[:, j:j + 1]
    for tc in range(4):
        gl0 = 512 * tc
        kb.wait("tensor", kb.g)
        kb.wait("tensor", kb.s)
        kb.wait("tensor", kb.sx)
        emit_linear(kb, cx, W["a_wq"], 8, cx.bigA, out3=cx.q1, nm=8,
                    t_lo=gl0, t_hi=gl0 + 512, out_tl0=0)
        emit_linear(kb, cx, W["a_wk"], 8, cx.bigA, out3=cx.k1, nm=8,
                    t_lo=gl0, t_hi=gl0 + 512, out_tl0=0, evac_scale=0.125)
        emit_linear(kb, cx, W["a_wv"], 8, cx.bigA, out3=cx.vf1, nm=8,
                    t_lo=gl0, t_hi=gl0 + 512, out_tl0=0)
        emit_linear(kb, cx, W["a_wgq"], 8, cx.bigA, out3=cx.gq1, nm=4,
                    t_lo=gl0, t_hi=gl0 + 512, out_tl0=0)
        kb.wait("scalar", kb.v)
        kb.wait("scalar", kb.pe)
        nvt = 0
        for c in range(8):
            nvt = kb.dmaT("scalar",
                          cx.vt1[:, :, 128 * c:128 * c + 128],
                          cx.vf1[:, c, :].rearrange("p (tt q) -> p tt q", q=128))
        for tt in range(4):
            mms = []
            for c in range(8):
                mms.append(lambda e, pt, c=c, tt=tt:
                           e.matmul(pt[:, 0:16], cx.q1[:, c, 128 * tt:128 * tt + 128],
                                    cx.glwb[:, c, :], start=(c == 0), stop=(c == 7)))
            ev = (lambda e, pt, tt=tt:
                  e.activation(cx.glT[:, tt, :], pt[:, 0:16], AF.Sigmoid))
            mm_evac(kb, cx.pp, mms, "scalar", ev, extra_mm_waits=[(kb.v, kb.v.n)])
        kg0 = kb.g.n
        for r in (smro, pero, pnro, ptro):
            for j in range(len(r.views)):
                r.done(j, kb.g, kg0)
        for h in range(16):
            pb = (h % 2) * 64
            ch = h // 2
            for qt in range(4):
                w = qt // 2
                p = qt % 2
                glob0 = (w == 0 and tc == 0)
                smj0, smj, smw = smro.next()
                sm = cx.w_sm[smj0]
                qsl = cx.q1[pb:pb + 64, ch, 128 * qt:128 * qt + 128]
                mms = []
                if not glob0:
                    if w == 0:
                        ksrc = cx.kh1[pb:pb + 64, ch, 0:256]
                    else:
                        ksrc = cx.k1[pb:pb + 64, ch, 0:256]
                    mms.append(lambda e, pt, qsl=qsl, ksrc=ksrc:
                               e.matmul(pt[:, 0:256], qsl, ksrc, start=True, stop=True))
                mms.append(lambda e, pt, qsl=qsl,
                           a=cx.k1[pb:pb + 64, ch, 256 * w:256 * w + 256]:
                           e.matmul(pt[:, 256:512], qsl, a, start=True, stop=True))
                mi = (2 + p) if glob0 else p
                for s_, v_ in smw:
                    kb.wait("vector", s_, v_)
                if glob0:
                    kb.op("vector", lambda e, o=sm[:, 0:256]: e.memset(o, NEG))
                    ev = (lambda e, pt, sm=sm, mi=mi:
                          e.scalar_tensor_tensor(sm[:, 256:512], pt[:, 256:512], 1.0,
                                                 cx.wmask[:, mi, 256:512],
                                                 ALU.mult, ALU.add))
                else:
                    ev = (lambda e, pt, sm=sm, mi=mi:
                          e.scalar_tensor_tensor(sm[:, 0:512], pt[:, 0:512], 1.0,
                                                 cx.wmask[:, mi, 0:512],
                                                 ALU.mult, ALU.add))
                mm_evac(kb, cx.pp, mms, "vector", ev, extra_mm_waits=[(kb.g, kg0)])
                nmj0, nmj, nmw = nmro.next()
                for s_, v_ in nmw:
                    kb.wait("vector", s_, v_)
                kb.op("vector", lambda e, o=stc(nmj0), sm=sm:
                      e.reduce_max(o, sm[:, 0:512], axis=AX.X, negate=True), inc=kb.v)
                suj0, suj, suw = suro.next()
                pej0, pej, pew = pero.next()
                peb = cx.w_pe[pej0]
                kb.wait("scalar", kb.v)
                for s_, v_ in pew + suw:
                    kb.wait("scalar", s_, v_)
                kb.op("scalar", lambda e, o=peb[:, 0:512], sm=sm, bi=stc(nmj0),
                      ac=stc(suj0):
                      e.activation(o, sm[:, 0:512], AF.Exp, bias=bi, accum_out=ac),
                      inc=kb.s)
                smro.done(smj, kb.s, kb.s.n)
                nmro.done(nmj, kb.s, kb.s.n)
                rsj0, rsj, rsw = rsro.next()
                kb.wait("vector", kb.s)
                for s_, v_ in rsw:
                    kb.wait("vector", s_, v_)
                kb.op("vector", lambda e, o=stc(rsj0), i=stc(suj0):
                      e.reciprocal(o, i), inc=kb.v)
                suro.done(suj, kb.v, kb.v.n)
                pnj0, pnj, pnw = pnro.next()
                pnb = cx.w_pn[pnj0]
                for s_, v_ in pnw:
                    kb.wait("vector", s_, v_)
                kb.op("vector", lambda e, o=pnb[:, 0:512], i=peb[:, 0:512],
                      r=stc(rsj0), g=cx.glT[:, qt, h:h + 1]:
                      e.tensor_scalar(o, i, r, g, ALU.mult, ALU.mult), inc=kb.v)
                pero.done(pej, kb.v, kb.v.n)
                rsro.done(rsj, kb.v, kb.v.n)
                ptj0, ptj, ptw = ptro.next()
                ptb = cx.w_pt[ptj0]
                kb.wait("scalar", kb.v)
                for s_, v_ in ptw:
                    kb.wait("scalar", s_, v_)
                ndt = kb.dmaT("scalar",
                              ptb[:, 0:512].rearrange("p (kb q) -> p kb q", q=128),
                              pnb[:, 0:512].rearrange("q (kb p) -> q kb p", p=128))
                pnro.done(pnj, kb.dt, ndt)
                sts = [2 * (w - 1) + j for j in range(4)]
                mms = []
                valid = [st for st in sts if not (st < 0 and tc == 0)]
                for idx, st in enumerate(valid):
                    j = sts.index(st)
                    if st < 0:
                        sth = 2 + st
                        flat = sth * 1024 + 64 * h
                        vsl = cx.vh1[:, flat // 256, flat % 256:flat % 256 + 64]
                    else:
                        vsl = cx.vt1[:, st, 64 * h:64 * h + 64]
                    mms.append(lambda e, pt, j=j, vsl=vsl, s=(idx == 0),
                               q=(idx == len(valid) - 1), ptb=ptb:
                               e.matmul(pt[0:64, 0:128], vsl,
                                        ptb[:, 128 * j:128 * j + 128],
                                        start=s, stop=q))
                ev = (lambda e, pt, pb=pb, ch=ch, qt=qt:
                      e.copy(cx.att1[pb:pb + 64, ch, 128 * qt:128 * qt + 128],
                             pt[0:64, 0:128]))
                mm_evac(kb, cx.pp, mms, "scalar", ev, extra_mm_waits=[(kb.dt, ndt)])
                ptro.done(ptj, kb.pe, kb.pe.n)
        kb.wait("gpsimd", kb.pe)
        kb.wait("gpsimd", kb.s)
        kb.op("gpsimd", lambda e: e.tensor_copy(cx.kh1[:, :, :], cx.k1[:, :, 256:512]))
        kb.op("gpsimd", lambda e: e.tensor_copy(
            cx.vh1[:, :, :],
            cx.vt1[:, 2:4, :].rearrange("p a (b t) -> p (a b) t", t=256)), inc=kb.g)
        for lh in range(8):
            pb = (lh % 2) * 64
            ch = lh // 2
            for qt in range(4):
                gqt = 4 * tc + qt
                smj0, smj, smw = smro.next()
                sm = cx.w_sm[smj0]
                mms = [lambda e, pt, pb=pb, ch=ch, qt=qt:
                       e.matmul(pt[:, 0:64], cx.gq1[pb:pb + 64, ch, 128 * qt:128 * qt + 128],
                                cx.gk1[pb:pb + 64, ch, :], start=True, stop=True)]
                for s_, v_ in smw:
                    kb.wait("vector", s_, v_)
                ev = (lambda e, pt, sm=sm, gqt=gqt:
                      e.scalar_tensor_tensor(sm[:, 0:64], pt[:, 0:64], 0.125,
                                             cx.lmask[:, gqt, :], ALU.mult, ALU.add))
                mm_evac(kb, cx.pp, mms, "vector", ev, extra_mm_waits=[(kb.g, kb.g.n)])
                nmj0, nmj, nmw = nmro.next()
                for s_, v_ in nmw:
                    kb.wait("vector", s_, v_)
                kb.op("vector", lambda e, o=stc(nmj0), sm=sm:
                      e.reduce_max(o, sm[:, 0:64], axis=AX.X, negate=True), inc=kb.v)
                suj0, suj, suw = suro.next()
                pej0, pej, pew = pero.next()
                peb = cx.w_pe[pej0]
                kb.wait("scalar", kb.v)
                for s_, v_ in pew + suw:
                    kb.wait("scalar", s_, v_)
                kb.op("scalar", lambda e, o=peb[:, 0:64], sm=sm, bi=stc(nmj0),
                      ac=stc(suj0):
                      e.activation(o, sm[:, 0:64], AF.Exp, bias=bi, accum_out=ac),
                      inc=kb.s)
                smro.done(smj, kb.s, kb.s.n)
                nmro.done(nmj, kb.s, kb.s.n)
                rsj0, rsj, rsw = rsro.next()
                kb.wait("vector", kb.s)
                for s_, v_ in rsw:
                    kb.wait("vector", s_, v_)
                kb.op("vector", lambda e, o=stc(rsj0), i=stc(suj0):
                      e.reciprocal(o, i), inc=kb.v)
                suro.done(suj, kb.v, kb.v.n)
                pnj0, pnj, pnw = pnro.next()
                pnb = cx.w_pn[pnj0]
                for s_, v_ in pnw:
                    kb.wait("vector", s_, v_)
                kb.op("vector", lambda e, o=pnb[:, 0:64], i=peb[:, 0:64], r=stc(rsj0):
                      e.tensor_scalar_mul(o, i, r), inc=kb.v)
                pero.done(pej, kb.v, kb.v.n)
                rsro.done(rsj, kb.v, kb.v.n)
                ptj0, ptj, ptw = ptro.next()
                ptb = cx.w_pt[ptj0]
                kb.wait("scalar", kb.v)
                for s_, v_ in ptw:
                    kb.wait("scalar", s_, v_)
                ndt = kb.dmaT("scalar", ptb[:, 0:128], pnb[:, 0:128])
                pnro.done(pnj, kb.dt, ndt)
                mms = [lambda e, pt, lh=lh, ptb=ptb:
                       e.matmul(pt[0:64, 0:128],
                                cx.gvt[0:64, lh // 2, (lh % 2) * 64:(lh % 2) * 64 + 64],
                                ptb[0:64, 0:128], start=True, stop=True)]
                ev = (lambda e, pt, pb=pb, ch=ch, qt=qt:
                      e.copy(cx.lat1[pb:pb + 64, ch, 128 * qt:128 * qt + 128],
                             pt[0:64, 0:128]))
                mm_evac(kb, cx.pp, mms, "scalar", ev,
                        extra_mm_waits=[(kb.dt, ndt), (kb.dt, ngv)])
                ptro.done(ptj, kb.pe, kb.pe.n)
        kb.wait("tensor", kb.s)
        for c in range(8):
            mms = [lambda e, pt, c=c:
                   e.matmul(pt[:, 0:512], cx.grwB[:, c, :], cx.q1[:, c, :],
                            start=True, stop=True)]
            grb = cx.w_sm[0]
            ev = (lambda e, pt, grb=grb:
                  e.activation(grb[:, 0:512], pt[:, 0:512], AF.Sigmoid))
            mm_evac(kb, cx.pp, mms, "scalar", ev,
                    extra_mm_waits=[(kb.g, kb.g.n), (kb.s, kb.s.n)])
            def remev(e, pt, mi_, tl_, w_, dst, c=c, grb=grb):
                return e.tensor_tensor(cx.w_pe[0][:, 0:512], pt[:, 0:512],
                                       grb[:, 0:512], ALU.mult)
            emit_linear(kb, cx, W["a_wro"], 4, cx.lat1, m0=c, nm=1,
                        t_lo=0, t_hi=512, evac_eng="vector",
                        w_ready=[(kb.s, kb.s.n), (kb.g, kb.g.n)], evac_fn=remev)
            kb.wait("gpsimd", kb.v)
            kb.op("gpsimd", lambda e, c=c:
                  e.tensor_tensor(cx.att1[:, c, :], cx.att1[:, c, :],
                                  cx.w_pe[0][:, 0:512], ALU.add), inc=kb.g)
        kb.wait("sync", kb.g)
        kb.dma("sync", att_sv.rearrange("(c p) t -> p c t", p=128)[:, :, gl0:gl0 + 512],
               cx.att1[:, :, :])
    natt = kb.dma("sync", cx.bigA[:, :, :], att_sv.rearrange("(c p) t -> p c t", p=128))
    kb.wait("tensor", kb.sx, natt)
    for tl in range(0, T, 512):
        kb.wait("vector", kb.g)
        emit_linear(kb, cx, W["a_wout"], 8, cx.bigA, nm=8, t_lo=tl, t_hi=tl + 512,
                    evac_fn=lambda e, pt, mi, tl_, w_, dst:
                    e.tensor_copy(cx.f32scr[:, mi, 0:w_], pt[:, 0:w_]))
        kb.wait("scalar", kb.v)
        kb.wait("gpsimd", kb.v)
        emit_ln(kb, cx, cx.f32scr, tl, tl + 512, resid3=cx.xbuf, src_tl0=0)


def host_l1f_consts(inp):
    d = {}
    for nm in ("wf", "uf", "wi", "ui", "wc", "uc", "wo", "uo"):
        d["g_" + nm] = np.ascontiguousarray(np.asarray(inp["L1_ffn_" + nm], np.float32).T)
    for nm in ("bf", "bi", "bc", "bo", "in_c"):
        d["g_" + nm] = np.ascontiguousarray(
            np.asarray(inp["L1_ffn_" + nm], np.float32).reshape(8, 128).T)
    return d


def declare_l1f(nc):
    W = {}
    for nm in ("g_wf", "g_uf", "g_wi", "g_ui", "g_wc", "g_uc", "g_wo", "g_uo"):
        W[nm] = nc.dram_tensor(nm, [D, D], F32, kind="ExternalInput").ap()
    for nm in ("g_bf", "g_bi", "g_bc", "g_bo", "g_in_c"):
        W[nm] = nc.dram_tensor(nm, [128, 8], F32, kind="ExternalInput").ap()
    return W


def make_ctx_l1f(kb, cx):
    cx.c_st = cx.f32scr[:, 0:4, :].rearrange("p a (b t) -> p (a b) t", t=256)
    cx.t1 = cx.f32scr[:, 4:8, :].rearrange("p a (b t) -> p (a b) t", t=256)
    cx.t2 = cx.f32scr[:, 8:12, :].rearrange("p a (b t) -> p (a b) t", t=256)
    cx.uf_s = cx.bigB[:, 0:4, :].rearrange("p a (b t) -> p (a b) t", t=1024)
    cx.uf_r = cx.bigB[:, 0:4, :].rearrange("p a (b m) -> p (a b) m", m=1024)
    cx.ui_r = cx.bigB[:, 4:8, :].rearrange("p a (b m) -> p (a b) m", m=1024)
    cx.uc_r = cx.bigC[:, 0:4, :].rearrange("p a (b m) -> p (a b) m", m=1024)
    cx.g_f = cx.bigC[:, 4, :].rearrange("p (a t) -> p a t", t=256)
    cx.g_i = cx.bigC[:, 5, :].rearrange("p (a t) -> p a t", t=256)
    cx.g_g = cx.bigC[:, 6, :].rearrange("p (a t) -> p a t", t=256)
    cx.c_bf = cx.bigC[:, 7, :].rearrange("p (a t) -> p a t", t=256)
    cx.bcols = kb.sbuf("bcols", [128, 5, 8], F32)
    return cx


def emit_resident_u(kb, cx, W):
    n = 0
    for w_dram, dst in ((W["g_uf"], cx.uf_r), (W["g_ui"], cx.ui_r), (W["g_uc"], cx.uc_r)):
        wv = w_dram.rearrange("(c p) m -> p c m", p=128)
        n = kb.dma("gpsimd", dst.rearrange("p (c x) m -> p c x m", x=1) if False else dst,
                   wv)
    return n


def emit_l1_ffn(kb, cx, W):
    for j, nm in enumerate(("g_bf", "g_bi", "g_bc", "g_bo", "g_in_c")):
        kb.dma("gpsimd", cx.bcols[:, j, :], W[nm])
    emit_ln(kb, cx, cx.xbuf, 0, T, dst3=cx.bigA)
    nur = emit_resident_u(kb, cx, W)
    kb.wait("gpsimd", kb.ld)
    for c in range(NCH):
        kb.op("gpsimd", lambda e, c=c:
              e.tensor_copy(cx.c_st[:, c, :],
                            cx.bcols[:, 4, c:c + 1].broadcast_to([128, 256])), inc=kb.g)
    for step in range(8):
        s0 = 256 * step
        kb.wait("gpsimd", kb.v)
        for c in range(NCH):
            kb.op("gpsimd", lambda e, c=c:
                  e.tensor_copy(cx.c_bf[:, c, :], cx.c_st[:, c, :]),
                  inc=kb.g if c == NCH - 1 else None)
        for gate, wname, uname, bj, act in (
                (cx.g_f, "g_wf", cx.uf_r, 0, AF.Sigmoid),
                (cx.g_i, "g_wi", cx.ui_r, 1, AF.Sigmoid),
                (cx.g_g, "g_wc", cx.uc_r, 2, AF.Tanh)):
            wv = W[wname].rearrange("(c p) m -> p c m", p=128)
            for mi in range(8):
                wsl, wj, wpend = cx.wrot.next()
                for s_, v_ in wpend:
                    kb.wait("gpsimd", s_, v_)
                nld = kb.dma("gpsimd",
                             wsl[:, 0:1024].rearrange("p (c m) -> p c m", m=128),
                             wv[:, :, 128 * mi:128 * mi + 128])
                wslv = wsl[:, 0:1024].rearrange("p (c m) -> p c m", m=128)
                mms = []
                for kc in range(8):
                    mms.append(lambda e, pt, kc=kc, wslv=wslv, mi=mi, s0=s0:
                               e.matmul(pt[:, 0:256], wslv[:, kc, :],
                                        cx.bigA[:, kc, s0:s0 + 256],
                                        start=(kc == 0), stop=False))
                for kc in range(8):
                    mms.append(lambda e, pt, kc=kc, uname=uname, mi=mi:
                               e.matmul(pt[:, 0:256], uname[:, kc, 128 * mi:128 * mi + 128],
                                        cx.c_bf[:, kc, :],
                                        start=False, stop=(kc == 7)))
                ev = (lambda e, pt, gate=gate, mi=mi, bj=bj, act=act:
                      e.activation(gate[:, mi, :], pt[:, 0:256], act,
                                   bias=cx.bcols[:, bj, mi:mi + 1]))
                mm_evac(kb, cx.pp, mms, "scalar", ev,
                        extra_mm_waits=[(kb.ld, nld), (kb.g, kb.g.n),
                                        (kb.ld, nur)])
                cx.wrot.done(wj, kb.pe, kb.pe.n)
        kb.wait("vector", kb.s)
        for c in range(NCH):
            kb.op("vector", lambda e, c=c:
                  e.tensor_tensor(cx.t1[:, c, :], cx.g_i[:, c, :], cx.g_g[:, c, :],
                                  ALU.mult))
            kb.op("vector", lambda e, c=c:
                  e.tensor_tensor(cx.t2[:, c, :], cx.g_f[:, c, :], cx.c_st[:, c, :],
                                  ALU.mult))
            kb.op("vector", lambda e, c=c:
                  e.tensor_tensor(cx.c_st[:, c, :], cx.t1[:, c, :], cx.t2[:, c, :],
                                  ALU.add), inc=kb.v if c == NCH - 1 else None)
    hT = cx.bigB[:, :, :]
    wvo = W["g_wo"].rearrange("(c p) m -> p c m", p=128)
    uvo = W["g_uo"].rearrange("(c p) m -> p c m", p=128)
    kb.wait("gpsimd", kb.v)
    for c in range(NCH):
        kb.op("gpsimd", lambda e, c=c:
              e.tensor_copy(cx.c_bf[:, c, :], cx.c_st[:, c, :]),
              inc=kb.g if c == NCH - 1 else None)
    for mi in range(8):
        usl, uj, upend = cx.wbig.next()
        for s_, v_ in upend:
            kb.wait("gpsimd", s_, v_)
        nld2 = kb.dma("gpsimd", usl[:, 0:1024].rearrange("p (c m) -> p c m", m=128),
                      uvo[:, :, 128 * mi:128 * mi + 128])
        uslv = usl[:, 0:1024].rearrange("p (c m) -> p c m", m=128)
        mms = []
        for kc in range(8):
            mms.append(lambda e, pt, kc=kc, uslv=uslv:
                       e.matmul(pt[:, 0:256], uslv[:, kc, :], cx.c_bf[:, kc, :],
                                start=(kc == 0), stop=(kc == 7)))
        zuo = cx.t1[:, 0:1, :]
        ev = (lambda e, pt: e.tensor_copy(cx.t1[:, 0, :], pt[:, 0:256]))
        mm_evac(kb, cx.pp, mms, "vector", ev,
                extra_mm_waits=[(kb.ld, nld2), (kb.g, kb.g.n)])
        cx.wbig.done(uj, kb.pe, kb.pe.n)
        wsl, wj, wpend = cx.wrot.next()
        for s_, v_ in wpend:
            kb.wait("gpsimd", s_, v_)
        nld = kb.dma("gpsimd", wsl[:, 0:1024].rearrange("p (c m) -> p c m", m=128),
                     wvo[:, :, 128 * mi:128 * mi + 128])
        wslv = wsl[:, 0:1024].rearrange("p (c m) -> p c m", m=128)
        for tl in range(0, T, 512):
            mms = []
            for kc in range(8):
                mms.append(lambda e, pt, kc=kc, wslv=wslv, tl=tl:
                           e.matmul(pt[:, 0:512], wslv[:, kc, :],
                                    cx.bigA[:, kc, tl:tl + 512],
                                    start=(kc == 0), stop=(kc == 7)))
            def hev(e, pt, mi=mi, tl=tl):
                zb = cx.t1[:, 0, :].unsqueeze(1).broadcast_to([128, 2, 256])
                return e.tensor_tensor(
                    cx.t2[:, 0:2, :], pt[:, 0:512].rearrange("p (a t) -> p a t", t=256),
                    zb, ALU.add)
            mm_evac(kb, cx.pp, mms, "vector", hev,
                    extra_mm_waits=[(kb.ld, nld), (kb.v, kb.v.n), (kb.s, kb.s.n)])
            kb.wait("scalar", kb.v)
            kb.op("scalar", lambda e, mi=mi, tl=tl:
                  e.activation(hT[:, mi, tl:tl + 512],
                               cx.t2[:, 0:2, :].rearrange("p a t -> p (a t)"),
                               AF.Tanh, bias=cx.bcols[:, 3, mi:mi + 1]), inc=kb.s)
        cx.wrot.done(wj, kb.pe, kb.pe.n)
    for tl in range(0, T, 512):
        kb.wait("scalar", kb.s)
        emit_ln(kb, cx, hT, tl, tl + 512, resid3=cx.xbuf, src_bf=True)


def build_l1a():
    nc = bass.Bass("TRN2", target_bir_lowering=False, debug=False)
    xT = nc.dram_tensor("xT", [D, T], F32, kind="ExternalInput").ap()
    W = declare_l1a(nc)
    out = nc.dram_tensor("xo", [D, T], F32, kind="ExternalOutput").ap()
    att_sv = nc.dram_tensor("att_sv", [D, T], BF16, kind="Internal").ap()
    with ExitStack() as stack:
        kb = KB(nc, stack)
        cx = make_ctx(kb)
        cx = make_ctx_ffn(kb, cx)
        cx = make_ctx_l1a(kb, cx)
        emit_const_init(kb, cx)
        emit_l1a_init(kb, cx, W)
        kb.dma("sync", cx.xbuf[:, :, :], xT.rearrange("(c p) t -> p c t", p=128))
        for eng in ("scalar", "tensor", "vector", "gpsimd"):
            kb.wait(eng, kb.sx)
            kb.wait(eng, kb.ld)
        kb.wait("tensor", kb.v)
        emit_l1_attn(kb, cx, W, att_sv)
        kb.wait("sync", kb.g)
        kb.wait("sync", kb.v)
        kb.dma("sync", out.rearrange("(c p) t -> p c t", p=128), cx.xbuf[:, :, :])
        kb.finalize([(kb.sx, kb.sx.n), (kb.ld, kb.ld.n), (kb.dt, kb.dt.n)])
    return nc


def build_l1f():
    nc = bass.Bass("TRN2", target_bir_lowering=False, debug=False)
    xT = nc.dram_tensor("xT", [D, T], F32, kind="ExternalInput").ap()
    W = declare_l1f(nc)
    out = nc.dram_tensor("xo", [D, T], F32, kind="ExternalOutput").ap()
    with ExitStack() as stack:
        kb = KB(nc, stack)
        cx = make_ctx(kb)
        cx = make_ctx_ffn(kb, cx)
        cx = make_ctx_l1f(kb, cx)
        emit_const_init(kb, cx)
        kb.dma("sync", cx.xbuf[:, :, :], xT.rearrange("(c p) t -> p c t", p=128))
        for eng in ("scalar", "tensor", "vector", "gpsimd"):
            kb.wait(eng, kb.sx)
            kb.wait(eng, kb.ld)
        kb.wait("tensor", kb.v)
        emit_l1_ffn(kb, cx, W)
        kb.wait("sync", kb.g)
        kb.wait("sync", kb.v)
        kb.dma("sync", out.rearrange("(c p) t -> p c t", p=128), cx.xbuf[:, :, :])
        kb.finalize([(kb.sx, kb.sx.n), (kb.ld, kb.ld.n), (kb.dt, kb.dt.n)])
    return nc


def _phase(kb, eng):
    for s in (kb.pe, kb.v, kb.s, kb.g, kb.ld, kb.sx, kb.dt):
        if kb.ms.get(eng) is s:
            continue
        kb.wait(eng, s)


def emit_l0a_init(kb, cx, W):
    kb.dma("gpsimd", cx.cmask[:, :], W["cm"])


def emit_reshard(kb, cx, cin, cout):
    kb.wait("sync", kb.g)
    kb.wait("sync", kb.v)
    n1 = kb.dma("sync", cin.rearrange("(c p) t -> p c t", p=128), cx.xbuf[:, :, :])
    kb.wait("gpsimd", kb.sx, n1)
    cc = kb.sem("ccol")
    cc.n += 1
    kb.rec("gpsimd", lambda e, h=cc.h: e.collective_compute(
        "AllGather", ALU.bypass, replica_groups=[list(range(NCORES))],
        ins=[cin.opt()], outs=[cout.opt()]).then_inc(h, 1))
    kb.wait("gpsimd", cc, cc.n)
    nc = kb.nc
    for b in range(NCORES):
        def ld(e, b=b):
            pid = nc.partition_id(engines=[mybir.EngineType.Pool])
            view = cout.rearrange("(b c p) t -> b p c t", b=NCORES, p=128)[b]
            return e.dma_start(out=cx.xbuf[:, :, 256 * b:256 * b + 256],
                               in_=view[:, :, bass.ts(pid, 256)])
        kb.ld.n += 16
        kb.rec("gpsimd", lambda e, ld=ld: ld(e).then_inc(kb.ld.h, 16))
    kb.wait("gpsimd", kb.ld)


def build_fused():
    nc = bass.Bass("TRN2", target_bir_lowering=False, debug=False)
    xT = nc.dram_tensor("xT", [D, T], F32, kind="ExternalInput").ap()
    Wa = {}
    Wa["wcd"] = nc.dram_tensor("wcd", [D, 512], F32, kind="ExternalInput").ap()
    Wa["wqu"] = nc.dram_tensor("wqu", [512, D], F32, kind="ExternalInput").ap()
    Wa["wkk"] = nc.dram_tensor("wkk", [512, 128], F32, kind="ExternalInput").ap()
    Wa["wv"] = nc.dram_tensor("wv", [512, 64], F32, kind="ExternalInput").ap()
    Wa["wcp"] = nc.dram_tensor("wcp", [D, D], F32, kind="ExternalInput").ap()
    Wa["cm"] = nc.dram_tensor("cm", [128, 640], F32, kind="ExternalInput").ap()
    Wf = declare_l0ffn(nc)
    W1a = declare_l1a(nc)
    W1f = declare_l1f(nc)
    out = nc.dram_tensor("xo", [D, T], F32, kind="ExternalOutput").ap()
    x1sv = nc.dram_tensor("x1sv", [D, T], F32, kind="Internal").ap()
    att_sv = nc.dram_tensor("att_sv", [D, T], BF16, kind="Internal").ap()
    cin = nc.dram_tensor("cin", [D, T], F32, kind="Internal")
    cout = nc.dram_tensor("cout", [NCORES * D, T], F32, kind="Internal")
    with ExitStack() as stack:
        kb = KB(nc, stack)
        cx = make_ctx(kb)
        cx = make_ctx_ffn(kb, cx)
        cx = make_ctx_l1a(kb, cx)
        cx = make_ctx_l1f(kb, cx)
        emit_const_init(kb, cx)
        emit_l0a_init(kb, cx, Wa)
        kb.dma("sync", cx.xbuf[:, :, :], xT.rearrange("(c p) t -> p c t", p=128))
        for eng in ("scalar", "tensor", "vector", "gpsimd"):
            kb.wait(eng, kb.sx)
            kb.wait(eng, kb.ld)
        kb.wait("tensor", kb.v)
        kb.wait("vector", kb.ld)
        emit_l0_attn(kb, cx, Wa)
        _phase(kb, "gpsimd")
        emit_ffn_init(kb, cx, Wf)
        kb.wait("vector", kb.ld)
        kb.wait("tensor", kb.ld)
        emit_l0f_block(kb, cx, Wf, x1sv, 6)
        _phase(kb, "gpsimd")
        emit_l1a_init(kb, cx, W1a)
        kb.wait("vector", kb.ld)
        kb.wait("tensor", kb.ld)
        emit_l1_attn(kb, cx, W1a, att_sv)
        _phase(kb, "sync")
        emit_reshard(kb, cx, cin.ap(), cout.ap())
        _phase(kb, "gpsimd")
        _phase(kb, "vector")
        _phase(kb, "scalar")
        _phase(kb, "tensor")
        emit_l1_ffn(kb, cx, W1f)
        kb.wait("sync", kb.g)
        kb.wait("sync", kb.v)
        kb.dma("sync", out.rearrange("(c p) t -> p c t", p=128), cx.xbuf[:, :, :])
        kb.finalize([(kb.sx, kb.sx.n), (kb.ld, kb.ld.n), (kb.dt, kb.dt.n)])
    return nc


def kernel(**inputs):
    x = _to_f32(inputs["L0_attn_x"])
    nc = build_fused()
    base = {}
    base.update(hostf_l0ffn_consts(inputs))
    base.update(host_l1a_consts(inputs))
    base.update(host_l1f_consts(inputs))
    maps = []
    for b in range(NCORES):
        m = dict(base)
        m.update(l0a_inputs(x[b].T, inputs))
        maps.append(m)
    res = run_bass_kernel_spmd(nc, maps, core_ids=list(range(NCORES)))
    out = np.empty((8, T, D), np.float32)
    for j in range(NCORES):
        oj = res.results[j]["xo"].T.reshape(8, 256, D)
        out[:, 256 * j:256 * (j + 1), :] = oj
    return out


# version 05
# speedup vs baseline: 1.3830x; 1.1078x over previous
import math
from contextlib import ExitStack

import numpy as np
import ml_dtypes

import concourse.bass as bass
import concourse.mybir as mybir
from concourse.bass_utils import run_bass_kernel_spmd

F32 = mybir.dt.float32
BF16 = mybir.dt.bfloat16
AF = mybir.ActivationFunctionType
ALU = mybir.AluOpType
AX = mybir.AxisListType

NCORES = 8
D = 1024
T = 2048
NCH = D // 128
NEG = -1.0e30

ENGS = ("sync", "scalar", "vector", "gpsimd", "tensor")


class Sem:
    __slots__ = ("h", "n")

    def __init__(self, h):
        self.h = h
        self.n = 0


class KB:

    def __init__(self, nc, stack):
        self.nc = nc
        self.stack = stack
        self.prog = {e: [] for e in ENGS}
        self._n = 0
        self.pe = self.sem("m_pe")
        self.v = self.sem("m_v")
        self.s = self.sem("m_s")
        self.g = self.sem("m_g")
        self.ld = self.sem("q_ld")
        self.sx = self.sem("q_sx")
        self.dt = self.sem("q_dt")
        self.stg = self.sem("q_stg")
        self.ms = {"tensor": self.pe, "vector": self.v, "scalar": self.s,
                   "gpsimd": self.g}

    def sem(self, name=None):
        if name is None:
            name = f"s{self._n}"
            self._n += 1
        return Sem(self.stack.enter_context(self.nc.semaphore(name)))

    def sbuf(self, name, shape, dt):
        return self.stack.enter_context(self.nc.sbuf_tensor(name, shape, dt))

    def psum(self, name, shape, dt=F32):
        return self.stack.enter_context(self.nc.psum_tensor(name, shape, dt))

    def rec(self, eng, fn):
        self.prog[eng].append(fn)

    def wait(self, eng, sem, val=None):
        v = sem.n if val is None else val
        if v > 0:
            self.rec(eng, lambda e, h=sem.h, v=v: e.wait_ge(h, v))

    def op(self, eng, fn, inc=None, inc_n=1):
        if inc is not None:
            inc.n += inc_n
            self.rec(eng, lambda e, fn=fn, h=inc.h, k=inc_n: fn(e).then_inc(h, k))
        else:
            self.rec(eng, fn)

    def mark(self, eng):
        m = self.ms[eng]
        self.op(eng, lambda e: e.nop(), inc=m)
        return m.n

    def dma(self, eng, out, in_, sem=None, **kw):
        sem = sem or {"gpsimd": self.ld, "sync": self.sx, "scalar": self.dt}[eng]
        sem.n += 16
        self.rec(eng, lambda e, o=out, i=in_, h=sem.h, kw=dict(kw):
                 e.dma_start(out=o, in_=i, **kw).then_inc(h, 16))
        return sem.n

    def dmaT(self, eng, out, in_, sem=None):
        sem = sem or self.dt
        sem.n += 16
        self.rec(eng, lambda e, o=out, i=in_, h=sem.h:
                 e.dma_start_transpose(o, i).then_inc(h, 16))
        return sem.n

    def finalize(self, out_waits):
        for s, v in out_waits:
            self.wait("sync", s, v)
        with self.nc.Block() as block:
            for eng in ENGS:
                ops = self.prog[eng]
                if ops:
                    getattr(block, eng)(lambda e, ops=ops: [f(e) for f in ops])


class PsTile:
    def __init__(self, kb, name, shape, dt=F32):
        self.kb = kb
        self.t = kb.psum(name, shape, dt)
        self.sem = kb.sem(name + "_f")
        self.uses = 0


class PsPool:
    def __init__(self, kb, n, shape=(128, 512), dt=F32, tag="ps"):
        self.tiles = [PsTile(kb, f"{tag}{i}", list(shape), dt) for i in range(n)]
        self.i = 0

    def next(self):
        t = self.tiles[self.i % len(self.tiles)]
        self.i += 1
        return t


def wbf(SW, key, li=None):
    if SW is None or key not in SW:
        return None
    tw, rdy = SW[key]
    return ((tw[li] if li is not None else tw), rdy)


def stage_weights(kb, nc, specs):
    out = {}
    for key, ap in specs:
        shp = list(ap.shape)
        tw = nc.dram_tensor(key + "_bf", shp, BF16, kind="Internal").ap()
        mats = ([(tw[i], ap[i]) for i in range(shp[0])] if len(shp) == 3
                else [(tw, ap)])
        for t_, a_ in mats:
            rows = t_.shape[0]
            rc = max(1, (2 * 1024 * 1024) // (t_.shape[1] * 4))
            for r0 in range(0, rows, rc):
                r1 = min(rows, r0 + rc)
                kb.dma("gpsimd", t_[r0:r1], a_[r0:r1], sem=kb.stg)
        out[key] = (tw, (kb.stg, kb.stg.n))
    return out


def mm_evac(kb, pool, mms, evac_eng, evac_fn, extra_mm_waits=(), evac_inc=None):
    t = pool.next()
    if t.uses > 0:
        kb.wait("tensor", t.sem, t.uses)
    t.uses += 1
    for s, v in extra_mm_waits:
        kb.wait("tensor", s, v)
    for i, fn in enumerate(mms):
        kb.op("tensor", lambda e, fn=fn, t=t: fn(e, t.t),
              inc=kb.pe if i == len(mms) - 1 else None)
    kb.wait(evac_eng, kb.pe)
    m = kb.ms[evac_eng]
    m.n += 1
    kb.rec(evac_eng, lambda e, fn=evac_fn, t=t, h=m.h: fn(e, t.t).then_inc(h, 1))
    t.sem.n += 1
    kb.rec(evac_eng, lambda e, h=t.sem.h: e.drain().then_inc(h, 1))
    return t


class Rot:

    def __init__(self, views):
        self.views = list(views)
        self.pend = [[] for _ in self.views]
        self.i = 0

    def next(self):
        j = self.i % len(self.views)
        self.i += 1
        w = self.pend[j]
        self.pend[j] = []
        return self.views[j], j, w

    def done(self, j, sem, val):
        self.pend[j].append((sem, val))


def host_causal_masks():
    m = np.zeros((128, 640), np.float32)
    r = np.arange(128)
    m[:, 512:640] = np.where(r[None, :] > r[:, None], NEG, 0.0)
    return m


def host_window_masks():
    m = np.zeros((2, 128, 512), np.float32)
    m0 = np.zeros((2, 128, 512), np.float32)
    for p in range(2):
        for r in range(128):
            lo = 128 * p + r
            hi = lo + 256
            c = np.arange(512)
            bad = (c < lo) | (c > hi)
            m[p, r, bad] = NEG
            bad0 = bad | (c < 256)
            m0[p, r, bad0] = NEG
    return m, m0


def host_rotary_tables():
    n, dh, win = T, 64, 256
    half = np.arange(0, dh, 2, dtype=np.float64)
    inv_freq = 1.0 / 10000.0 ** (half / dh)
    t = np.arange(n, dtype=np.float64)
    freqs = t[:, None] * inv_freq[None, :]
    freqs = np.concatenate([freqs, freqs], axis=1)
    base = (half + 0.4 * dh) / (1.4 * dh)
    power = (t - n // 2) / (win // 2)
    scale = base[None, :] ** power[:, None]
    scale = np.concatenate([scale, scale], axis=1)
    cs, sn = np.cos(freqs), np.sin(freqs)
    A = (cs * scale).T
    B = (sn * scale).T
    Ai = (cs / scale).T
    Bi = (sn / scale).T
    out = [np.tile(x, (2, 1)).astype(np.float32) for x in (A, B, Ai, Bi)]
    return out


def host_rot_perm():
    P = np.zeros((128, 128), np.float32)
    for blk in range(2):
        o = blk * 64
        for dd in range(32):
            P[o + dd, o + dd + 32] = -1.0
        for dd in range(32, 64):
            P[o + dd, o + dd - 32] = 1.0
    return np.ascontiguousarray(P.T).astype(ml_dtypes.bfloat16)


def host_ehead():
    E = np.zeros((8, 16, 128), np.float32)
    for c in range(8):
        E[c, 2 * c, 0:64] = 1.0
        E[c, 2 * c + 1, 64:128] = 1.0
    return E


def host_latent_masks():
    m = np.zeros((16, 128, 64), np.float32)
    for qt in range(16):
        for r in range(128):
            q = 128 * qt + r
            m[qt, r, q // 32 + 1:] = NEG
    return m


def make_ctx(kb, n_f32scr=12):
    cx = type("Ctx", (), {})()
    cx.kb = kb
    cx.xbuf = kb.sbuf("xbuf", [128, NCH, T], F32)
    cx.bigA = kb.sbuf("bigA", [128, NCH, T], BF16)
    cx.bigB = kb.sbuf("bigB", [128, NCH, T], BF16)
    cx.bigC = kb.sbuf("bigC", [128, NCH, T], BF16)
    cx.f32scr = kb.sbuf("f32scr", [128, n_f32scr, 512], F32)
    cx.rows = kb.sbuf("lnrows", [2, 4 * 512], F32)
    cx.stt = kb.sbuf("stt", [128, 16], F32)
    cx.wrot = Rot([kb.sbuf(f"w{i}", [128, 8 * 128], BF16) for i in range(2)])
    cx.wbig = Rot([kb.sbuf("wbig", [128, 22 * 128], BF16)])
    cx.ones1_f32 = kb.sbuf("ones1", [128, 1], F32)
    cx.ones1_bf = kb.sbuf("ones1b", [128, 1], BF16)
    cx.cvals = kb.sbuf("cvals", [128, 4], F32)
    cx.ones_row_f32 = kb.sbuf("onesr", [1, 128], F32)
    cx.kdup = cx.f32scr[:, 0:2, :].bitcast(BF16).rearrange("p a t -> p (a t)").unsqueeze(1).transpose([0, 2, 1]) if False else \
        cx.f32scr[:, 0:2, :].bitcast(BF16).rearrange("p a t -> p (a t)").rearrange("p (a t) -> p a t", a=1)
    cx.vtm = cx.f32scr[:, 2, :].bitcast(BF16).rearrange("p (a b) -> p a b", b=64)
    cx.wv_sb = cx.f32scr[:, 3, 0:128].bitcast(BF16).rearrange("p (a b) -> p a b", b=64)
    cx.cmask = cx.f32scr[:, 6, 0:320].bitcast(BF16)
    cx.pp = PsPool(kb, 4, (128, 512), F32, "pp")
    cx.pps = PsPool(kb, 2, (1, 512), F32, "pst")
    cx.ppbc = PsPool(kb, 2, (128, 512), F32, "pbc")
    return cx


def emit_const_init(kb, cx):
    kb.op("vector", lambda e: e.memset(cx.ones1_f32[:, :], 1.0))
    kb.op("vector", lambda e: e.memset(cx.ones_row_f32[:, :], 1.0))
    kb.op("vector", lambda e: e.memset(cx.ones1_bf[:, :], 1.0))
    kb.op("vector", lambda e: e.memset(cx.cvals[:, 0:1], 1e-5), inc=kb.v)


def emit_ln(kb, cx, src3, t_lo, t_hi, dst3=None, dst_off=0, src_off=0,
            gamma_col=None, beta_col=None, resid3=None, n_ch=NCH, eps=1e-5,
            src_tl0=None, src_bf=False, resid_src3=None, rs_tl0=None):
    rows = cx.rows
    rv = lambda i: rows[0:1, 512 * i:512 * i + 512]
    mean, m2, msq, var = (rv(i) for i in range(4))
    std = msq
    a_ = var
    b_ = m2
    D_ = 128.0 * n_ch
    for tl in range(t_lo, t_hi, 512):
        w = 512
        sl = tl if src_tl0 is None else src_tl0 + (tl - t_lo)
        kb.wait("scalar", kb.g)
        kb.wait("scalar", kb.v)
        ts = cx.pps.next()
        ts2 = cx.pps.next()
        for t_ in (ts, ts2):
            if t_.uses > 0:
                kb.wait("tensor", t_.sem, t_.uses)
            t_.uses += 1
        pe_after = []
        for c in range(n_ch):
            sq = cx.f32scr[:, 8 + (c % 2), 0:w]
            if c >= 2:
                kb.wait("scalar", kb.pe, pe_after[c - 2])
            kb.op("scalar", lambda e, o=sq, i=src3[:, src_off + c, sl:sl + w]:
                  e.activation(o, i, AF.Square), inc=kb.s)
            kb.wait("tensor", kb.s)
            ones_l = cx.ones1_bf if src_bf else cx.ones1_f32
            kb.op("tensor", lambda e, o=ts.t[:, 0:w], l=ones_l[:, 0:1],
                  r=src3[:, src_off + c, sl:sl + w], s=(c == 0), p=(c == n_ch - 1):
                  e.matmul(o, l, r, start=s, stop=p))
            kb.op("tensor", lambda e, o=ts2.t[:, 0:w], l=cx.ones1_f32[:, 0:1],
                  r=sq, s=(c == 0), p=(c == n_ch - 1):
                  e.matmul(o, l, r, start=s, stop=p), inc=kb.pe)
            pe_after.append(kb.pe.n)
        kb.wait("vector", kb.pe)
        kb.op("vector", lambda e, o=mean, i=ts.t[0:1, 0:w]: e.tensor_scalar_mul(o, i, 1.0 / D_))
        kb.op("vector", lambda e, o=m2, i=ts2.t[0:1, 0:w]: e.tensor_scalar_mul(o, i, 1.0 / D_),
              inc=kb.v)
        ts.sem.n += 1
        ts2.sem.n += 1
        kb.rec("vector", lambda e, h=ts.sem.h: e.drain().then_inc(h, 1))
        kb.rec("vector", lambda e, h=ts2.sem.h: e.nop().then_inc(h, 1))
        kb.op("vector", lambda e, o=msq, i=mean: e.tensor_tensor(o, i, i, ALU.mult))
        kb.op("vector", lambda e, o=var, x=m2, y=msq: e.tensor_tensor(o, x, y, ALU.subtract),
              inc=kb.v)
        kb.wait("scalar", kb.v)
        kb.op("scalar", lambda e, o=std, i=var: e.activation(o, i, AF.Sqrt,
              bias=cx.cvals[0:1, 0:1]), inc=kb.s)
        kb.wait("vector", kb.s)
        kb.op("vector", lambda e, o=a_, i=std: e.reciprocal(o, i))
        kb.op("vector", lambda e, o=b_, x=mean, y=a_: e.tensor_tensor(o, x, y, ALU.mult),
              inc=kb.v)
        bca = cx.ppbc.next()
        bcb = cx.ppbc.next()
        for t_ in (bca, bcb):
            if t_.uses > 0:
                kb.wait("tensor", t_.sem, t_.uses)
            t_.uses += 1
        kb.wait("tensor", kb.v)
        kb.op("tensor", lambda e, o=bca.t[:, 0:w], l=cx.ones_row_f32[0:1, :], r=a_:
              e.matmul(o, l, r, start=True, stop=True))
        kb.op("tensor", lambda e, o=bcb.t[:, 0:w], l=cx.ones_row_f32[0:1, :], r=b_:
              e.matmul(o, l, r, start=True, stop=True), inc=kb.pe)
        kb.wait("vector", kb.pe)
        kb.wait("vector", kb.g)
        g_after = []
        for c in range(n_ch):
            u = cx.f32scr[:, 10 + (c % 2), 0:w]
            if resid3 is not None and c >= 2:
                kb.wait("vector", kb.g, g_after[c - 2])
            kb.op("vector", lambda e, o=u, i=src3[:, src_off + c, sl:sl + w],
                  b=bca.t[:, 0:w]: e.tensor_tensor(o, i, b, ALU.mult))
            last = (c == n_ch - 1)
            if resid3 is not None:
                u2 = cx.f32scr[:, 8 + (c % 2), 0:w]
                kb.op("vector", lambda e, o=u2, i=u, b=bcb.t[:, 0:w]:
                      e.tensor_tensor(o, i, b, ALU.subtract))
                u3 = u
                rsrc = resid3 if resid_src3 is None else resid_src3
                rl = tl if rs_tl0 is None else rs_tl0 + (tl - t_lo)
                kb.op("vector", lambda e, o=u3, i=u2, x=rsrc[:, c, rl:rl + w]:
                      e.tensor_tensor(o, i, x, ALU.add), inc=kb.v)
                kb.wait("gpsimd", kb.v)
                kb.op("gpsimd", lambda e, o=resid3[:, c, tl:tl + w], i=u3:
                      e.tensor_copy(o, i), inc=kb.g)
                g_after.append(kb.g.n)
            elif gamma_col is not None:
                u2 = cx.f32scr[:, 8 + (c % 2), 0:w]
                kb.op("vector", lambda e, o=u2, i=u, b=bcb.t[:, 0:w]:
                      e.tensor_tensor(o, i, b, ALU.subtract))
                kb.op("vector", lambda e, o=dst3[:, dst_off + c, tl:tl + w], i=u2,
                      g=gamma_col[:, c:c + 1], bb=beta_col[:, c:c + 1]:
                      e.tensor_scalar(o, i, g, bb, ALU.mult, ALU.add),
                      inc=kb.v if last else None)
            else:
                kb.op("vector", lambda e, o=dst3[:, dst_off + c, tl:tl + w], i=u,
                      b=bcb.t[:, 0:w]: e.tensor_tensor(o, i, b, ALU.subtract),
                      inc=kb.v if last else None)
        bca.sem.n += 1
        bcb.sem.n += 1
        kb.rec("vector", lambda e, h=bca.sem.h: e.drain().then_inc(h, 1))
        kb.rec("vector", lambda e, h=bcb.sem.h: e.nop().then_inc(h, 1))


def emit_linear(kb, cx, w_dram, nk, in3, in_off=0, out3=None, out_off=0,
                m0=0, nm=None, t_lo=0, t_hi=T, evac_eng="vector",
                evac_scale=None, evac_f32=False, evac_fn=None, w_ready=None,
                out_tl0=None, w_bf=None):
    stg_wait = None
    if w_bf is not None:
        w_dram, stg_wait = w_bf
    wview = w_dram.rearrange("(c p) m -> p c m", p=128)
    M = w_dram.shape[1]
    nm = nm if nm is not None else M // 128
    rot = cx.wrot if nk <= 8 else cx.wbig
    dma_eng = "gpsimd" if w_bf is None else "sync"
    for mi in range(nm):
        wsl, wj, wpend = rot.next()
        for s, v in wpend:
            kb.wait(dma_eng, s, v)
        if mi == 0 and stg_wait is not None:
            kb.wait(dma_eng, *stg_wait)
        mcols = slice(128 * (m0 + mi), 128 * (m0 + mi) + 128)
        nld = kb.dma(dma_eng, wsl[:, 0:nk * 128].rearrange("p (c m) -> p c m", m=128),
                     wview[:, 0:nk, mcols], sem=kb.ld)
        wslv = wsl[:, 0:nk * 128].rearrange("p (c m) -> p c m", m=128)
        for tl in range(t_lo, t_hi, 512):
            w = min(512, t_hi - tl)
            mms = []
            for kc in range(nk):
                mms.append(lambda e, pt, kc=kc, tl=tl, w=w, wslv=wslv:
                           e.matmul(pt[:, 0:w], wslv[:, kc, :],
                                    in3[:, in_off + kc, tl:tl + w],
                                    start=(kc == 0), stop=(kc == nk - 1)))
            otl = tl if out_tl0 is None else out_tl0 + (tl - t_lo)
            dst = (out3[:, out_off + mi, otl:otl + w] if out3 is not None else None)
            if evac_fn is not None:
                ev = lambda e, pt, mi=mi, tl=tl, w=w, dst=dst: evac_fn(e, pt, mi, tl, w, dst)
            elif evac_scale is not None:
                ev = lambda e, pt, dst=dst, w=w: e.tensor_scalar_mul(dst, pt[:, 0:w], evac_scale)
            else:
                ev = lambda e, pt, dst=dst, w=w: e.tensor_copy(dst, pt[:, 0:w])
            waits = [(kb.ld, nld)]
            if w_ready is not None:
                if isinstance(w_ready, list):
                    waits.extend(w_ready)
                else:
                    waits.append(w_ready)
            tile = mm_evac(kb, cx.pp, mms, evac_eng, ev, extra_mm_waits=waits)
        rot.done(wj, kb.pe, kb.pe.n)


def emit_attn_causal(kb, cx, q3, kdup, vtm, att3, masks_sb):
    smrot = Rot([(cx.bigB, 0), (cx.bigB, 1)])
    perot = Rot([(cx.bigB, 2), (cx.bigB, 3)])
    pnrot = Rot([(cx.bigB, 4), (cx.bigB, 5)])
    ptrot = Rot([(cx.bigB, 6), (cx.bigB, 7)])
    nmrot = Rot(list(range(4)))
    suro = Rot(list(range(4, 8)))
    rsro = Rot(list(range(8, 12)))
    stc = lambda j: cx.stt[:, j:j + 1]
    for h in range(16):
        pb = (h % 2) * 64
        ch = h // 2
        for qt in range(16):
            span = 128 * (qt + 1)
            nkc = (qt + 4) // 4
            (smb, smc), smj, smw = smrot.next()
            sm = lambda a, b, smb=smb, smc=smc: smb[:, smc, a:b]
            qsl = q3[pb:pb + 64, ch, 128 * qt:128 * qt + 128]
            for kc in range(nkc):
                w = min(512, span - 512 * kc)
                diag = (kc == nkc - 1)
                mms = [lambda e, pt, w=w, kc=kc, qsl=qsl, pb=pb:
                       e.matmul(pt[:, 0:w], qsl,
                                kdup[pb:pb + 64, 0, 512 * kc:512 * kc + w],
                                start=True, stop=True)]
                if diag:
                    ev = (lambda e, pt, w=w, kc=kc, sm=sm:
                          e.scalar_tensor_tensor(sm(512 * kc, 512 * kc + w),
                                                 pt[:, 0:w], 1.0,
                                                 masks_sb[:, 640 - w:640],
                                                 ALU.mult, ALU.add))
                else:
                    ev = (lambda e, pt, w=w, kc=kc, sm=sm:
                          e.tensor_copy(sm(512 * kc, 512 * kc + w), pt[:, 0:w]))
                if kc == 0:
                    for s_, v_ in smw:
                        kb.wait("vector", s_, v_)
                mm_evac(kb, cx.pp, mms, "vector", ev)
            nmj0, nmj, nmw = nmrot.next()
            for s_, v_ in nmw:
                kb.wait("vector", s_, v_)
            kb.op("vector", lambda e, o=stc(nmj0), a=0, b=span, sm=sm:
                  e.reduce_max(o, sm(a, b), axis=AX.X, negate=True), inc=kb.v)
            suj0, suj, suw = suro.next()
            (peb, pec), pej, pew = perot.next()
            kb.wait("scalar", kb.v)
            for s_, v_ in pew + suw:
                kb.wait("scalar", s_, v_)
            kb.op("scalar", lambda e, o=peb[:, pec, 0:span], a=0, b=span,
                  bi=stc(nmj0), ac=stc(suj0), sm=sm:
                  e.activation(o, sm(a, b), AF.Exp, bias=bi, accum_out=ac),
                  inc=kb.s)
            smrot.done(smj, kb.s, kb.s.n)
            nmrot.done(nmj, kb.s, kb.s.n)
            rsj0, rsj, rsw = rsro.next()
            kb.wait("vector", kb.s)
            for s_, v_ in rsw:
                kb.wait("vector", s_, v_)
            kb.op("vector", lambda e, o=stc(rsj0), i=stc(suj0):
                  e.reciprocal(o, i), inc=kb.v)
            suro.done(suj, kb.v, kb.v.n)
            (pnb, pnc), pnj, pnw = pnrot.next()
            kb.wait("scalar", kb.v)
            for s_, v_ in pnw:
                kb.wait("scalar", s_, v_)
            kb.op("scalar", lambda e, o=pnb[:, pnc, 0:span], i=peb[:, pec, 0:span],
                  r=stc(rsj0): e.mul(o, i, r), inc=kb.s)
            perot.done(pej, kb.s, kb.s.n)
            rsro.done(rsj, kb.s, kb.s.n)
            (ptb, ptc), ptj, ptw = ptrot.next()
            kb.wait("sync", kb.s)
            for s_, v_ in ptw:
                kb.wait("sync", s_, v_)
            nkb = span // 128
            ndt = kb.dmaT("sync",
                          ptb[:, ptc, 0:span].rearrange("p (kb q) -> p kb q", q=128),
                          pnb[:, pnc, 0:span].rearrange("q (kb p) -> q kb p", p=128))
            pnrot.done(pnj, kb.dt, ndt)
            mms = []
            for kb_ in range(nkb):
                mms.append(lambda e, pt, kb_=kb_, ptb=ptb, ptc=ptc, nkb=nkb:
                           e.matmul(pt[0:64, 0:128], vtm[:, kb_, :],
                                    ptb[:, ptc, 128 * kb_:128 * kb_ + 128],
                                    start=(kb_ == 0), stop=(kb_ == nkb - 1)))
            ev = (lambda e, pt, pb=pb, ch=ch, qt=qt:
                  e.copy(att3[pb:pb + 64, ch, 128 * qt:128 * qt + 128],
                         pt[0:64, 0:128]))
            mm_evac(kb, cx.pp, mms, "scalar", ev,
                    extra_mm_waits=[(kb.dt, ndt)])
            ptrot.done(ptj, kb.pe, kb.pe.n)


def emit_l0_attn(kb, cx, W, SW=None):
    lat = cx.bigB
    emit_ln(kb, cx, cx.xbuf, 0, T, dst3=cx.bigA)
    kb.wait("tensor", kb.v)
    emit_linear(kb, cx, W["wcd"], 8, cx.bigA, out3=lat, nm=4, w_bf=wbf(SW, "wcd"))
    kb.wait("tensor", kb.v)
    emit_linear(kb, cx, W["wqu"], 4, lat, out3=cx.bigC, nm=8, w_bf=wbf(SW, "wqu"))
    emit_linear(kb, cx, W["wkk"], 4, lat, out3=cx.kdup, nm=1, evac_scale=0.125,
                w_bf=wbf(SW, "wkk"))
    nld = kb.dma("gpsimd", cx.wv_sb[:, :, :],
                 W["wv"].rearrange("(c p) m -> p c m", p=128))
    kb.wait("tensor", kb.v)
    for tt in range(16):
        mms = []
        for kc in range(4):
            mms.append(lambda e, pt, kc=kc, tt=tt:
                       e.matmul(pt[:, 0:64], lat[:, kc, 128 * tt:128 * tt + 128],
                                cx.wv_sb[:, kc, :], start=(kc == 0), stop=(kc == 3)))
        ev = lambda e, pt, tt=tt: e.tensor_copy(cx.vtm[:, tt, :], pt[:, 0:64])
        mm_evac(kb, cx.pp, mms, "vector", ev,
                extra_mm_waits=[(kb.ld, nld)] if tt == 0 else ())
    kb.wait("tensor", kb.v)
    emit_attn_causal(kb, cx, cx.bigC, cx.kdup, cx.vtm, cx.bigA, cx.cmask)
    kb.wait("tensor", kb.s)
    for tl in range(0, T, 512):
        emit_linear(kb, cx, W["wcp"], 8, cx.bigA, nm=8, t_lo=tl, t_hi=tl + 512,
                    w_ready=(kb.s, kb.s.n), w_bf=wbf(SW, "wcp"),
                    evac_fn=lambda e, pt, mi, tl_, w, dst:
                    e.tensor_copy(cx.f32scr[:, mi, 0:w], pt[:, 0:w]))
        kb.wait("scalar", kb.v)
        kb.wait("gpsimd", kb.v)
        emit_ln(kb, cx, cx.f32scr, tl, tl + 512, resid3=cx.xbuf, src_tl0=0)


def build_l0a():
    nc = bass.Bass("TRN2", target_bir_lowering=False, debug=False)
    xT = nc.dram_tensor("xT", [D, T], F32, kind="ExternalInput").ap()
    W = {}
    W["wcd"] = nc.dram_tensor("wcd", [D, 512], F32, kind="ExternalInput").ap()
    W["wqu"] = nc.dram_tensor("wqu", [512, D], F32, kind="ExternalInput").ap()
    W["wkk"] = nc.dram_tensor("wkk", [512, 128], F32, kind="ExternalInput").ap()
    W["wv"] = nc.dram_tensor("wv", [512, 64], F32, kind="ExternalInput").ap()
    W["wcp"] = nc.dram_tensor("wcp", [D, D], F32, kind="ExternalInput").ap()
    cm = nc.dram_tensor("cm", [128, 640], F32, kind="ExternalInput").ap()
    out = nc.dram_tensor("xo", [D, T], F32, kind="ExternalOutput").ap()
    with ExitStack() as stack:
        kb = KB(nc, stack)
        cx = make_ctx(kb)
        emit_const_init(kb, cx)
        kb.dma("sync", cx.xbuf[:, :, :], xT.rearrange("(c p) t -> p c t", p=128))
        kb.dma("gpsimd", cx.cmask[:, :], cm)
        for eng in ("scalar", "tensor", "vector", "gpsimd"):
            kb.wait(eng, kb.sx)
        kb.wait("vector", kb.ld)
        kb.wait("tensor", kb.v)
        emit_l0_attn(kb, cx, W)
        kb.wait("sync", kb.g)
        kb.dma("sync", out.rearrange("(c p) t -> p c t", p=128), cx.xbuf[:, :, :])
        kb.finalize([(kb.sx, kb.sx.n)])
    return nc


def l0a_inputs(x_b_T, inp):
    wk = inp["L0_attn_Wkv_up"][:, :64]
    return {
        "xT": np.ascontiguousarray(x_b_T, dtype=np.float32),
        "wcd": np.ascontiguousarray(inp["L0_attn_Wc_down"], np.float32),
        "wqu": np.ascontiguousarray(inp["L0_attn_Wq_up"], np.float32),
        "wkk": np.ascontiguousarray(np.concatenate([wk, wk], axis=1), np.float32),
        "wv": np.ascontiguousarray(inp["L0_attn_Wkv_up"][:, 64:], np.float32),
        "wcp": np.ascontiguousarray(inp["L0_attn_Wc_proj"], np.float32),
        "cm": host_causal_masks(),
    }


def _to_f32(a):
    return np.ascontiguousarray(np.asarray(a), np.float32)


def kernel_unfused(**inputs):
    x = _to_f32(inputs["L0_attn_x"])
    cores = list(range(NCORES))

    nc1 = build_l0a()
    maps1 = [l0a_inputs(x[b].T, inputs) for b in range(NCORES)]
    r1 = run_bass_kernel_spmd(nc1, maps1, core_ids=cores)
    x1T = [np.ascontiguousarray(r1.results[b]["xo"]) for b in range(NCORES)]

    nc2 = build_l0f()
    base2 = hostf_l0ffn_consts(inputs)
    maps2 = [{**base2, "xT": x1T[b]} for b in range(NCORES)]
    r2 = run_bass_kernel_spmd(nc2, maps2, core_ids=cores)
    x2T = [np.ascontiguousarray(r2.results[b]["xo"]) for b in range(NCORES)]

    nc3 = build_l1a()
    base3 = host_l1a_consts(inputs)
    maps3 = [{**base3, "xT": x2T[b]} for b in range(NCORES)]
    r3 = run_bass_kernel_spmd(nc3, maps3, core_ids=cores)
    x3T = [np.ascontiguousarray(r3.results[b]["xo"]) for b in range(NCORES)]

    nc4 = build_l1f()
    base4 = host_l1f_consts(inputs)
    maps4 = []
    for j in range(NCORES):
        xj = np.concatenate([x3T[b][:, 256 * j:256 * (j + 1)] for b in range(NCORES)],
                            axis=1)
        maps4.append({**base4, "xT": np.ascontiguousarray(xj)})
    r4 = run_bass_kernel_spmd(nc4, maps4, core_ids=cores)

    out = np.empty((8, T, D), np.float32)
    for j in range(NCORES):
        oj = r4.results[j]["xo"].T.reshape(8, 256, D)
        out[:, 256 * j:256 * (j + 1), :] = oj
    return out


def hostf_l0ffn_consts(inp):
    DI = 2730
    DIP = 2816
    W1 = np.asarray(inp["L0_ffn_W1"], np.float32)
    b1 = np.asarray(inp["L0_ffn_b1"], np.float32)
    W2 = np.asarray(inp["L0_ffn_W2"], np.float32)
    W1p = np.zeros((6, 1024, 2 * DIP), np.float32)
    W1p[:, :, 0:DI] = W1[:, :, 0:DI]
    W1p[:, :, DIP:DIP + DI] = W1[:, :, DI:]
    b1p = np.zeros((6, 2 * DIP), np.float32)
    b1p[:, 0:DI] = b1[:, 0:DI]
    b1p[:, DIP:DIP + DI] = b1[:, DI:]
    W2p = np.zeros((6, DIP, 1024), np.float32)
    W2p[:, 0:DI, :] = W2
    qs = np.asarray(inp["L0_ffn_q_scale"], np.float32)
    ks = np.asarray(inp["L0_ffn_k_scale"], np.float32)
    qscol = np.stack([np.tile(qs, (1, 2)), np.tile(ks, (1, 2))], axis=2)
    lng = np.asarray(inp["L0_ffn_ln_g"], np.float32).reshape(6, 8, 128).transpose(0, 2, 1).copy()
    lnb = np.asarray(inp["L0_ffn_ln_b"], np.float32).reshape(6, 8, 128).transpose(0, 2, 1).copy()
    b1col = b1p.reshape(6, 44, 128).transpose(0, 2, 1).copy()
    b2col = np.asarray(inp["L0_ffn_b2"], np.float32).reshape(6, 8, 128).transpose(0, 2, 1).copy()
    A, B, Ai, Bi = host_rotary_tables()
    return {
        "f_wqkv": np.ascontiguousarray(inp["L0_ffn_Wqkv"], np.float32),
        "f_wo": np.ascontiguousarray(inp["L0_ffn_Wo"], np.float32),
        "f_w1": W1p, "f_w2": W2p,
        "f_qscol": qscol, "f_lng": lng, "f_lnb": lnb,
        "f_b1col": b1col, "f_b2col": b2col,
        "f_tbl": np.stack([A, B, 8.0 * Ai, 8.0 * Bi]),
        "f_wmask": np.stack(list(host_window_masks()[0]) + list(host_window_masks()[1])),
        "f_ptc": host_rot_perm().astype(np.float32),
        "f_e2c": np.concatenate([np.repeat([[1.0, 0.0]], 64, 0),
                                 np.repeat([[0.0, 1.0]], 64, 0)]).astype(np.float32),
        "f_e2t": np.concatenate([np.repeat([[1.0, 0.0]], 64, 0),
                                 np.repeat([[0.0, 1.0]], 64, 0)]).astype(np.float32).T.copy(),
    }


def declare_l0ffn(nc):
    W = {}
    W["f_wqkv"] = nc.dram_tensor("f_wqkv", [6, D, 3072], F32, kind="ExternalInput").ap()
    W["f_wo"] = nc.dram_tensor("f_wo", [6, D, D], F32, kind="ExternalInput").ap()
    W["f_w1"] = nc.dram_tensor("f_w1", [6, D, 5632], F32, kind="ExternalInput").ap()
    W["f_w2"] = nc.dram_tensor("f_w2", [6, 2816, D], F32, kind="ExternalInput").ap()
    W["f_qscol"] = nc.dram_tensor("f_qscol", [6, 128, 2], F32, kind="ExternalInput").ap()
    W["f_lng"] = nc.dram_tensor("f_lng", [6, 128, 8], F32, kind="ExternalInput").ap()
    W["f_lnb"] = nc.dram_tensor("f_lnb", [6, 128, 8], F32, kind="ExternalInput").ap()
    W["f_b1col"] = nc.dram_tensor("f_b1col", [6, 128, 44], F32, kind="ExternalInput").ap()
    W["f_b2col"] = nc.dram_tensor("f_b2col", [6, 128, 8], F32, kind="ExternalInput").ap()
    W["f_tbl"] = nc.dram_tensor("f_tbl", [4, 128, T], F32, kind="ExternalInput").ap()
    W["f_wmask"] = nc.dram_tensor("f_wmask", [4, 128, 512], F32, kind="ExternalInput").ap()
    W["f_ptc"] = nc.dram_tensor("f_ptc", [128, 128], F32, kind="ExternalInput").ap()
    W["f_e2c"] = nc.dram_tensor("f_e2c", [128, 2], F32, kind="ExternalInput").ap()
    W["f_e2t"] = nc.dram_tensor("f_e2t", [2, 128], F32, kind="ExternalInput").ap()
    return W


def make_ctx_ffn(kb, cx):
    cx.tbl = kb.sbuf("tbl", [128, 2, 1024], BF16)
    cx.wmask = cx.f32scr[:, 4:6, :].bitcast(BF16).rearrange("p a t -> p (a t)").rearrange("p (a t) -> p a t", a=4)
    cx.ptc = kb.sbuf("ptc", [128, 128], BF16)
    cx.qtmp = cx.bigB[:, 4, :].rearrange("p (a t) -> p a t", t=1024)
    cx.khalo = cx.f32scr[:, 0:2, :].bitcast(BF16).rearrange("p a t -> p (a t)").rearrange("p (a t) -> p a t", a=8)
    cx.vhalo = cx.f32scr[:, 2:4, :].bitcast(BF16).rearrange("p a t -> p (a t)").rearrange("p (a t) -> p a t", a=2)
    cx.xsv = kb.sbuf("xsv", [128, 8, 4], F32)
    cx.hrows = cx.rows[0:2, 0:1024]
    cx.rotm = kb.sbuf("rotm", [128, 16], F32)
    cx.e2c = kb.sbuf("e2c", [128, 2], F32)
    cx.e2t = kb.sbuf("e2t", [2, 128], F32)
    cx.colc = kb.sbuf("colc", [128, 64], F32)
    cx.colc2 = kb.sbuf("colc2", [128, 8], F32)
    cx.xs = cx.bigA[:, :, 0:1024]
    cx.vfm = cx.bigA[:, :, 1024:2048]
    cx.att_f = cx.bigA[:, :, 1024:2048]
    cx.q3 = cx.bigC[:, :, 0:1024]
    cx.k3 = cx.bigC[:, :, 1024:2048]
    cx.vtm_f = cx.bigB[:, 0:4, :].rearrange("p a (b t) -> p (a b) t", t=1024)
    s4 = cx.bigB[:, 4, :]
    s5 = cx.bigB[:, 5, :]
    cx.w_sm = [s4[:, 0:512], s4[:, 512:1024]]
    cx.w_pe = [s4[:, 1024:1536], s4[:, 1536:2048]]
    cx.w_pn = [s5[:, 0:512], s5[:, 512:1024]]
    cx.w_pt = [s5[:, 1024:1536], s5[:, 1536:2048]]
    cx.xs2 = cx.bigC[:, 0:2, :].rearrange("p a (b t) -> p (a b) t", t=512)
    cx.xn = cx.bigC[:, 2:4, :].rearrange("p a (b t) -> p (a b) t", t=512)
    cx.ggel = cx.bigA[:, 0:6, :].rearrange("p a (b t) -> p (a b) t", t=512)
    cx.uge = cx.bigB[:, 0:6, :].rearrange("p a (b t) -> p (a b) t", t=512)
    return cx


def emit_ffn_init(kb, cx, W):
    kb.dma("gpsimd", cx.ptc[:, :], W["f_ptc"])
    kb.dma("gpsimd", cx.e2c[:, :], W["f_e2c"])
    kb.dma("gpsimd", cx.e2t[:, :], W["f_e2t"])
    kb.dma("gpsimd", cx.wmask[:, :, :], W["f_wmask"].rearrange("m p t -> p m t"))


def emit_shift(kb, cx, dst3, gl0, gl1, sv_col=0, eng="gpsimd"):
    w = gl1 - gl0
    m = kb.ms[eng]
    for c in range(4):
        kb.op(eng, lambda e, o=dst3[:, c, 0:w], i=cx.xbuf[:, c, gl0:gl1]:
              e.tensor_copy(o, i))
    for c in range(4, 8):
        kb.op(eng, lambda e, o=dst3[:, c, 1:w], i=cx.xbuf[:, c, gl0:gl1 - 1]:
              e.tensor_copy(o, i))
        if gl0 == 0:
            kb.op(eng, lambda e, o=dst3[:, c, 0:1]: e.memset(o, 0.0),
                  inc=m if c == 7 else None)
        else:
            kb.op(eng, lambda e, o=dst3[:, c, 0:1],
                  i=cx.xsv[:, c, sv_col:sv_col + 1]:
                  e.tensor_copy(o, i), inc=m if c == 7 else None)


def emit_rotary(kb, cx, x3, blk_col, tblA, tblB):
    sqw = cx.f32scr[:, 8:10, :].rearrange("p a t -> p (a t)")
    n2 = cx.hrows[:, 0:512]
    nr = cx.hrows[:, 512:1024]
    for c in range(NCH):
        kb.wait("scalar", kb.v)
        kb.wait("scalar", kb.g)
        kb.op("scalar", lambda e, o=sqw[:, 0:1024], i=x3[:, c, :]:
              e.activation(o, i, AF.Square), inc=kb.s)
        for n in range(2):
            mm = [lambda e, pt, n=n: e.matmul(pt[0:2, 0:512], blk_col,
                                              sqw[:, 512 * n:512 * n + 512],
                                              start=True, stop=True)]
            ev = lambda e, pt: e.tensor_copy(n2[:, 0:512], pt[0:2, 0:512])
            mm_evac(kb, cx.pp, mm, "vector", ev,
                    extra_mm_waits=[(kb.s, kb.s.n), (kb.s, kb.s.n)])
            kb.wait("scalar", kb.v)
            kb.op("scalar", lambda e, o=nr[:, 0:512], i=n2[:, 0:512]:
                  e.activation(o, i, AF.Sqrt), inc=kb.s)
            kb.wait("vector", kb.s)
            kb.op("vector", lambda e, o=n2[:, 0:512], i=nr[:, 0:512]:
                  e.tensor_scalar_max(o, i, 1e-12))
            kb.op("vector", lambda e, o=nr[:, 0:512], i=n2[:, 0:512]:
                  e.reciprocal(o, i), inc=kb.v)
            mm = [lambda e, pt: e.matmul(pt[:, 0:512], cx.e2t[:, :],
                                         nr[:, 0:512],
                                         start=True, stop=True)]
            ev = (lambda e, pt, n=n, c=c:
                  e.scalar_tensor_tensor(cx.qtmp[:, 0, 512 * n:512 * n + 512],
                                         x3[:, c, 512 * n:512 * n + 512], 1.0,
                                         pt[:, 0:512], ALU.mult, ALU.mult))
            mm_evac(kb, cx.pp, mm, "vector", ev,
                    extra_mm_waits=[(kb.v, kb.v.n), (kb.g, kb.g.n)])
        kb.op("vector", lambda e, o=cx.qtmp[:, 1, :], i=cx.qtmp[:, 0, :], t=tblA:
              e.tensor_tensor(o, i, t, ALU.mult), inc=kb.v)
        for n in range(2):
            mm = [lambda e, pt, n=n: e.matmul(pt[:, 0:512], cx.ptc[:, :],
                                              cx.qtmp[:, 0, 512 * n:512 * n + 512],
                                              start=True, stop=True)]
            ev = (lambda e, pt, n=n:
                  e.scalar_tensor_tensor(cx.qtmp[:, 0, 512 * n:512 * n + 512],
                                         pt[:, 0:512], 1.0,
                                         tblB[:, 512 * n:512 * n + 512],
                                         ALU.mult, ALU.mult))
            mm_evac(kb, cx.pp, mm, "vector", ev,
                    extra_mm_waits=[(kb.v, kb.v.n)])
        kb.wait("gpsimd", kb.v)
        kb.op("gpsimd", lambda e, o=x3[:, c, :], a=cx.qtmp[:, 1, :], b=cx.qtmp[:, 0, :]:
              e.tensor_tensor(o, a, b, ALU.add), inc=kb.g)


def emit_attn_win(kb, cx, tc):
    smro = Rot([0, 1])
    pero = Rot([0, 1])
    pnro = Rot([0, 1])
    ptro = Rot([0, 1])
    nmro = Rot(list(range(4)))
    suro = Rot(list(range(4, 8)))
    rsro = Rot(list(range(8, 12)))
    stc = lambda j: cx.stt[:, j:j + 1]
    kg0 = kb.g.n
    for r in (smro, pero, pnro, ptro):
        for j in range(len(r.views)):
            r.done(j, kb.g, kg0)
    for h in range(16):
        pb = (h % 2) * 64
        ch = h // 2
        for qt in range(8):
            w = qt // 2
            p = qt % 2
            glob0 = (w == 0 and tc == 0)
            smj0, smj, smw = smro.next()
            sm = cx.w_sm[smj0]
            qsl = cx.q3[pb:pb + 64, ch, 128 * qt:128 * qt + 128]
            mms = []
            if not glob0:
                if w == 0:
                    ksrc = cx.khalo[pb:pb + 64, ch, 0:256]
                else:
                    ksrc = cx.k3[pb:pb + 64, ch, 256 * (w - 1):256 * w]
                mms.append(lambda e, pt, qsl=qsl, ksrc=ksrc:
                           e.matmul(pt[:, 0:256], qsl, ksrc, start=True, stop=True))
            mms.append(lambda e, pt, qsl=qsl, a=cx.k3[pb:pb + 64, ch, 256 * w:256 * w + 256]:
                       e.matmul(pt[:, 256:512], qsl, a, start=True, stop=True))
            mi = (2 + p) if glob0 else p
            if glob0:
                for s_, v_ in smw:
                    kb.wait("vector", s_, v_)
                kb.op("vector", lambda e, o=sm[:, 0:256]: e.memset(o, NEG))
                ev = (lambda e, pt, sm=sm, mi=mi:
                      e.scalar_tensor_tensor(sm[:, 256:512], pt[:, 256:512], 1.0,
                                             cx.wmask[:, mi, 256:512], ALU.mult, ALU.add))
            else:
                if qt == 0 or True:
                    for s_, v_ in smw:
                        kb.wait("vector", s_, v_)
                ev = (lambda e, pt, sm=sm, mi=mi:
                      e.scalar_tensor_tensor(sm[:, 0:512], pt[:, 0:512], 1.0,
                                             cx.wmask[:, mi, 0:512], ALU.mult, ALU.add))
            mm_evac(kb, cx.pp, mms, "vector", ev, extra_mm_waits=[(kb.g, kg0)])
            nmj0, nmj, nmw = nmro.next()
            for s_, v_ in nmw:
                kb.wait("vector", s_, v_)
            kb.op("vector", lambda e, o=stc(nmj0), sm=sm:
                  e.reduce_max(o, sm[:, 0:512], axis=AX.X, negate=True), inc=kb.v)
            suj0, suj, suw = suro.next()
            pej0, pej, pew = pero.next()
            peb = cx.w_pe[pej0]
            kb.wait("scalar", kb.v)
            for s_, v_ in pew + suw:
                kb.wait("scalar", s_, v_)
            kb.op("scalar", lambda e, o=peb[:, 0:512], sm=sm, bi=stc(nmj0), ac=stc(suj0):
                  e.activation(o, sm[:, 0:512], AF.Exp, bias=bi, accum_out=ac),
                  inc=kb.s)
            smro.done(smj, kb.s, kb.s.n)
            nmro.done(nmj, kb.s, kb.s.n)
            rsj0, rsj, rsw = rsro.next()
            kb.wait("vector", kb.s)
            for s_, v_ in rsw:
                kb.wait("vector", s_, v_)
            kb.op("vector", lambda e, o=stc(rsj0), i=stc(suj0):
                  e.reciprocal(o, i), inc=kb.v)
            suro.done(suj, kb.v, kb.v.n)
            pnj0, pnj, pnw = pnro.next()
            pnb = cx.w_pn[pnj0]
            for s_, v_ in pnw:
                kb.wait("vector", s_, v_)
            kb.op("vector", lambda e, o=pnb[:, 0:512], i=peb[:, 0:512], r=stc(rsj0):
                  e.tensor_scalar_mul(o, i, r), inc=kb.v)
            pero.done(pej, kb.v, kb.v.n)
            rsro.done(rsj, kb.v, kb.v.n)
            ptj0, ptj, ptw = ptro.next()
            ptb = cx.w_pt[ptj0]
            kb.wait("sync", kb.v)
            for s_, v_ in ptw:
                kb.wait("sync", s_, v_)
            ndt = kb.dmaT("sync",
                          ptb[:, 0:512].rearrange("p (kb q) -> p kb q", q=128),
                          pnb[:, 0:512].rearrange("q (kb p) -> q kb p", p=128))
            pnro.done(pnj, kb.dt, ndt)
            sts = [2 * (w - 1) + j for j in range(4)]
            mms = []
            valid = [st for st in sts if not (st < 0 and tc == 0)]
            for idx, st in enumerate(valid):
                j = sts.index(st)
                if st < 0:
                    vsl = cx.vhalo[:, 2 + st, 64 * h:64 * h + 64]
                else:
                    vsl = cx.vtm_f[:, st, 64 * h:64 * h + 64]
                mms.append(lambda e, pt, j=j, vsl=vsl, s=(idx == 0), q=(idx == len(valid) - 1), ptb=ptb:
                           e.matmul(pt[0:64, 0:128], vsl,
                                    ptb[:, 128 * j:128 * j + 128],
                                    start=s, stop=q))
            ev = (lambda e, pt, pb=pb, ch=ch, qt=qt:
                  e.copy(cx.att_f[pb:pb + 64, ch, 128 * qt:128 * qt + 128],
                         pt[0:64, 0:128]))
            mm_evac(kb, cx.pp, mms, "scalar", ev, extra_mm_waits=[(kb.dt, ndt)])
            ptro.done(ptj, kb.pe, kb.pe.n)


def emit_l0_ffn_layer(kb, cx, W, li, do_attn=True, do_ffn=True, tcs=(0, 1), SW=None):
    n1 = kb.dma("gpsimd", cx.colc[:, 0:8], W["f_lng"][li])
    kb.dma("gpsimd", cx.colc[:, 8:16], W["f_lnb"][li])
    kb.dma("gpsimd", cx.colc[:, 16:60], W["f_b1col"][li])
    kb.dma("gpsimd", cx.colc2[:, 0:8], W["f_b2col"][li])
    nqs = kb.dma("gpsimd", cx.rotm[:, 0:2], W["f_qscol"][li])
    kb.wait("vector", kb.ld)
    kb.op("vector", lambda e: e.reciprocal(cx.rotm[:, 2:4], cx.rotm[:, 0:2]), inc=kb.v)
    kb.wait("scalar", kb.v)
    kb.op("scalar", lambda e: e.activation(cx.rotm[:, 4:6], cx.rotm[:, 2:4], AF.Square),
          inc=kb.s)
    kb.wait("vector", kb.s)
    kb.op("vector", lambda e: e.tensor_scalar(cx.rotm[:, 6:8], cx.e2c[:, 0:2],
                                              cx.rotm[:, 4:5], None, ALU.mult),
          )
    kb.op("vector", lambda e: e.tensor_scalar(cx.rotm[:, 8:10], cx.e2c[:, 0:2],
                                              cx.rotm[:, 5:6], None, ALU.mult),
          inc=kb.v)
    blkq = cx.rotm[:, 6:8]
    blkk = cx.rotm[:, 8:10]
    qcol = cx.rotm[:, 0:1]
    kcol = cx.rotm[:, 1:2]
    kb.wait("gpsimd", kb.v)
    kb.op("gpsimd", lambda e: e.tensor_copy(cx.xsv[:, 4:8, 0:1], cx.xbuf[:, 4:8, 1023:1024]),
          inc=kb.g)
    for tc in (tcs if do_attn else ()):
        gl0 = 1024 * tc
        ntb = 0
        for j in range(2):
            ntb = kb.dma("gpsimd", cx.tbl[:, j, :], W["f_tbl"][j][:, gl0:gl0 + 1024])
        kb.wait("gpsimd", kb.pe)
        kb.wait("gpsimd", kb.v)
        emit_shift(kb, cx, cx.xs, gl0, gl0 + 1024, sv_col=0)
        wq = W["f_wqkv"][li]
        kb.wait("tensor", kb.g)
        kb.wait("tensor", kb.ld, nqs)
        wqb = wbf(SW, "f_wqkv", li)
        emit_linear(kb, cx, wq, 8, cx.xs, out3=cx.q3, m0=0, nm=8, t_lo=0, t_hi=1024,
                    w_bf=wqb,
                    evac_fn=lambda e, pt, mi_, tl, w_, dst:
                    e.tensor_scalar(cx.q3[:, mi_, tl:tl + w_], pt[:, 0:w_],
                                    qcol, None, ALU.mult))
        emit_linear(kb, cx, wq, 8, cx.xs, out3=cx.k3, m0=8, nm=8, t_lo=0, t_hi=1024,
                    w_bf=wqb,
                    evac_fn=lambda e, pt, mi_, tl, w_, dst:
                    e.tensor_scalar(cx.k3[:, mi_, tl:tl + w_], pt[:, 0:w_],
                                    kcol, None, ALU.mult))
        if tc == 1:
            kb.wait("vector", kb.g)
        emit_linear(kb, cx, wq, 8, cx.xs, out3=cx.vfm, m0=16, nm=8, t_lo=0, t_hi=1024,
                    w_bf=wqb)
        kb.wait("sync", kb.v)
        kb.wait("sync", kb.g)
        nvt = 0
        for c in range(8):
            nvt = kb.dmaT("sync",
                          cx.vtm_f[:, :, 128 * c:128 * c + 128],
                          cx.vfm[:, c, :].rearrange("p (tt q) -> p tt q", q=128))
        kb.wait("vector", kb.ld, ntb)
        emit_rotary(kb, cx, cx.q3, blkq, cx.tbl[:, 0, :], cx.tbl[:, 1, :])
        kb.wait("gpsimd", kb.v)
        ntb2 = 0
        for j in range(2):
            ntb2 = kb.dma("gpsimd", cx.tbl[:, j, :], W["f_tbl"][2 + j][:, gl0:gl0 + 1024])
        kb.wait("vector", kb.ld, ntb2)
        emit_rotary(kb, cx, cx.k3, blkk, cx.tbl[:, 0, :], cx.tbl[:, 1, :])
        kb.wait("tensor", kb.dt, nvt)
        emit_attn_win(kb, cx, tc)
        if tc == 0:
            kb.wait("gpsimd", kb.pe)
            kb.wait("gpsimd", kb.s)
            kb.op("gpsimd", lambda e: e.tensor_copy(cx.khalo[:, :, :],
                                                    cx.k3[:, :, 768:1024]))
            kb.op("gpsimd", lambda e: e.tensor_copy(
                cx.vhalo[:, :, :].rearrange("p a t -> p (a t)"),
                cx.vtm_f[:, 6:8, :].rearrange("p a t -> p (a t)")), inc=kb.g)
        kb.wait("tensor", kb.s)
        emit_linear(kb, cx, W["f_wo"][li], 8, cx.att_f, nm=8, t_lo=0, t_hi=1024,
                    w_bf=wbf(SW, "f_wo", li), evac_eng="vector",
                    evac_fn=lambda e, pt, mi_, tl, w_, dst, gl0=gl0:
                    e.scalar_tensor_tensor(cx.xbuf[:, mi_, gl0 + tl:gl0 + tl + w_],
                                           pt[:, 0:w_], 1.0,
                                           cx.xbuf[:, mi_, gl0 + tl:gl0 + tl + w_],
                                           ALU.mult, ALU.add))
    if not do_ffn:
        return
    kb.wait("gpsimd", kb.v)
    for j, bc in enumerate((511, 1023, 1535)):
        kb.op("gpsimd", lambda e, j=j, bc=bc:
              e.tensor_copy(cx.xsv[:, 4:8, 1 + j:2 + j], cx.xbuf[:, 4:8, bc:bc + 1]),
              inc=kb.g if j == 2 else None)
    gcol = cx.colc[:, 0:8]
    bcol = cx.colc[:, 8:16]
    for tl in range(0, T, 512):
        kb.wait("gpsimd", kb.pe)
        kb.wait("gpsimd", kb.v)
        emit_shift(kb, cx, cx.xs2, tl, tl + 512, sv_col=tl // 512)
        emit_ln(kb, cx, cx.xs2, 0, 512, dst3=cx.xn, gamma_col=gcol, beta_col=bcol,
                src_bf=True)
        w1 = W["f_w1"][li]
        emit_linear(kb, cx, w1, 8, cx.xn, m0=22, nm=22, t_lo=0, t_hi=512,
                    w_bf=wbf(SW, "f_w1", li),
                    evac_eng="scalar", w_ready=(kb.v, kb.v.n),
                    evac_fn=lambda e, pt, mi_, tl_, w_, dst:
                    e.activation(cx.ggel[:, mi_, 0:w_], pt[:, 0:w_], AF.Gelu,
                                 bias=cx.colc[:, 38 + mi_:39 + mi_]))
        kb.wait("tensor", kb.s)
        emit_linear(kb, cx, w1, 8, cx.xn, m0=0, nm=22, t_lo=0, t_hi=512,
                    w_bf=wbf(SW, "f_w1", li),
                    evac_eng="vector", w_ready=(kb.s, kb.s.n),
                    evac_fn=lambda e, pt, mi_, tl_, w_, dst:
                    e.scalar_tensor_tensor(cx.uge[:, mi_, 0:w_], pt[:, 0:w_],
                                           cx.colc[:, 16 + mi_:17 + mi_],
                                           cx.ggel[:, mi_, 0:w_],
                                           ALU.add, ALU.mult))
        emit_linear(kb, cx, W["f_w2"][li], 22, cx.uge, nm=8, t_lo=0, t_hi=512,
                    w_bf=wbf(SW, "f_w2", li),
                    evac_eng="vector", w_ready=(kb.v, kb.v.n),
                    evac_fn=lambda e, pt, mi_, tl_, w_, dst, tl0=tl:
                    e.scalar_tensor_tensor(cx.xbuf[:, mi_, tl0:tl0 + w_],
                                           pt[:, 0:w_],
                                           cx.colc2[:, mi_:mi_ + 1],
                                           cx.xbuf[:, mi_, tl0:tl0 + w_],
                                           ALU.add, ALU.add))


def emit_l0_ffn(kb, cx, W, SW=None):
    for li in range(6):
        emit_l0_ffn_layer(kb, cx, W, li, SW=SW)


def emit_l0f_block(kb, cx, W, x1sv, n_layers=6, SW=None):
    kb.wait("sync", kb.g)
    kb.wait("sync", kb.v)
    nst = kb.dma("sync", x1sv.rearrange("(c p) t -> p c t", p=128), cx.xbuf[:, :, :])
    kb.wait("vector", kb.sx, nst)
    emit_ln(kb, cx, cx.xbuf, 0, T, dst3=cx.xbuf)
    for li in range(n_layers):
        emit_l0_ffn_layer(kb, cx, W, li, SW=SW)
    kb.wait("sync", kb.pe)
    kb.wait("sync", kb.s)
    for tl in range(0, T, 512):
        kb.wait("sync", kb.v)
        kb.wait("sync", kb.g)
        nld = kb.dma("sync", cx.f32scr[:, 0:8, :],
                     x1sv.rearrange("(c p) t -> p c t", p=128)[:, :, tl:tl + 512])
        kb.wait("vector", kb.sx, nld)
        emit_ln(kb, cx, cx.xbuf, tl, tl + 512, resid3=cx.xbuf,
                resid_src3=cx.f32scr, rs_tl0=0)


def build_l0f(n_layers=6, dump_stream=False):
    nc = bass.Bass("TRN2", target_bir_lowering=False, debug=False)
    xT = nc.dram_tensor("xT", [D, T], F32, kind="ExternalInput").ap()
    W = declare_l0ffn(nc)
    out = nc.dram_tensor("xo", [D, T], F32, kind="ExternalOutput").ap()
    x1sv = nc.dram_tensor("x1sv", [D, T], F32, kind="Internal").ap()
    with ExitStack() as stack:
        kb = KB(nc, stack)
        cx = make_ctx(kb)
        cx = make_ctx_ffn(kb, cx)
        emit_const_init(kb, cx)
        emit_ffn_init(kb, cx, W)
        kb.dma("sync", cx.xbuf[:, :, :], xT.rearrange("(c p) t -> p c t", p=128))
        for eng in ("scalar", "tensor", "vector", "gpsimd"):
            kb.wait(eng, kb.sx)
            kb.wait(eng, kb.ld)
        kb.wait("tensor", kb.v)
        if dump_stream:
            for li in range(n_layers):
                emit_l0_ffn_layer(kb, cx, W, li)
        else:
            emit_l0f_block(kb, cx, W, x1sv, n_layers)
        kb.wait("sync", kb.g)
        kb.wait("sync", kb.v)
        kb.dma("sync", out.rearrange("(c p) t -> p c t", p=128), cx.xbuf[:, :, :])
        kb.finalize([(kb.sx, kb.sx.n), (kb.ld, kb.ld.n), (kb.dt, kb.dt.n)])
    return nc


def l0f_inputs(x_T, inp):
    d = hostf_l0ffn_consts(inp)
    d["xT"] = np.ascontiguousarray(x_T, np.float32)
    return d


def host_l1a_consts(inp):
    glw = np.asarray(inp["L1_attn_gate_local_w"], np.float32)
    grw = np.asarray(inp["L1_attn_gate_remote_w"], np.float32)
    glwb = np.zeros((8, 128, 16), np.float32)
    for c in range(8):
        for j in range(2):
            glwb[c, 64 * j:64 * j + 64, 2 * c + j] = glw[2 * c + j]
    grwB = np.zeros((8, 128, 128), np.float32)
    for c in range(8):
        for j in range(2):
            grwB[c, 64 * j:64 * j + 64, 64 * j:64 * j + 64] = grw[2 * c + j][:, None]
    m = np.zeros((2, 128, 512), np.float32)
    m0 = np.zeros((2, 128, 512), np.float32)
    for p in range(2):
        for r in range(128):
            q = 128 * p + r
            c = np.arange(512)
            bad = (c <= q) | (c > q + 256)
            m[p, r, bad] = NEG
            bad0 = bad | (c < 256)
            m0[p, r, bad0] = NEG
    return {
        "a_wq": np.ascontiguousarray(inp["L1_attn_wq"], np.float32),
        "a_wk": np.ascontiguousarray(inp["L1_attn_wk"], np.float32),
        "a_wv": np.ascontiguousarray(inp["L1_attn_wv"], np.float32),
        "a_wgq": np.ascontiguousarray(inp["L1_attn_wgq"], np.float32),
        "a_wgk": np.ascontiguousarray(inp["L1_attn_wgk"], np.float32),
        "a_wgv": np.ascontiguousarray(inp["L1_attn_wgv"], np.float32),
        "a_wro": np.ascontiguousarray(inp["L1_attn_r_out_w"], np.float32),
        "a_wout": np.ascontiguousarray(inp["L1_attn_w_out"], np.float32),
        "a_glwb": glwb, "a_grwB": grwB,
        "a_wmask": np.stack(list(m) + list(m0)),
        "a_lmask": host_latent_masks(),
    }


def declare_l1a(nc):
    W = {}
    for nm, shp in (("a_wq", [D, D]), ("a_wk", [D, D]), ("a_wv", [D, D]),
                    ("a_wgq", [D, 512]), ("a_wgk", [D, 512]), ("a_wgv", [D, 512]),
                    ("a_wro", [512, D]), ("a_wout", [D, D]),
                    ("a_glwb", [8, 128, 16]), ("a_grwB", [8, 128, 128]),
                    ("a_wmask", [4, 128, 512]), ("a_lmask", [16, 128, 64])):
        W[nm] = nc.dram_tensor(nm, shp, F32, kind="ExternalInput").ap()
    return W


def make_ctx_l1a(kb, cx):
    tblf = cx.tbl[:, :, :].rearrange("p a t -> p (a t)")
    cx.grwB = tblf[:, 0:1024].rearrange("p (a t) -> p a t", t=128)
    cx.glwb = tblf[:, 1024:1152].rearrange("p (a t) -> p a t", t=16)
    cx.px = tblf[:, 1152:1664].rearrange("p (a t) -> p a t", t=64)
    cx.gk1 = tblf[:, 1664:1920].rearrange("p (a t) -> p a t", t=64)
    cx.gvf = cx.bigB[:, 4, 0:512].rearrange("p (a t) -> p a t", t=128)
    cx.gvt = cx.f32scr[:, 7, 0:256].bitcast(BF16).rearrange("p (a t) -> p a t", t=128)
    cx.glT = kb.sbuf("glT", [128, 4, 16], F32)
    cx.q1 = cx.bigC[:, :, 0:512]
    cx.k1 = cx.bigC[:, :, 512:1024]
    cx.vf1 = cx.bigC[:, :, 1024:1536]
    cx.kh1 = cx.bigC[:, :, 1536:1792]
    cx.vh1 = cx.bigC[:, :, 1792:2048]
    cx.vt1 = cx.bigB[:, 0:2, :].rearrange("p a (b t) -> p (a b) t", t=1024)
    cx.att1 = cx.bigB[:, 2:4, :].rearrange("p a (b t) -> p (a b) t", t=512)
    cx.gq1 = cx.bigB[:, 6, :].rearrange("p (a t) -> p a t", t=512)
    cx.lat1 = cx.bigB[:, 7, :].rearrange("p (a t) -> p a t", t=512)
    cx.lmask = cx.f32scr[:, 6, :].bitcast(BF16).rearrange("p (a t) -> p a t", t=64)
    return cx


def emit_l1a_init(kb, cx, W):
    kb.dma("gpsimd", cx.wmask[:, :, :], W["a_wmask"].rearrange("m p t -> p m t"))
    kb.dma("gpsimd", cx.lmask[:, :, :], W["a_lmask"].rearrange("m p t -> p m t"))
    kb.dma("gpsimd", cx.glwb[:, :, :], W["a_glwb"].rearrange("c p t -> p c t"))
    kb.dma("gpsimd", cx.grwB[:, :, :], W["a_grwB"].rearrange("c p t -> p c t"))


def emit_l1_attn(kb, cx, W, att_sv, SW=None):
    emit_ln(kb, cx, cx.xbuf, 0, T, dst3=cx.bigA)
    kb.wait("vector", kb.v)
    for c in range(NCH):
        pxf = cx.f32scr[:, 0, 0:64]
        kb.op("vector", lambda e, c=c, pxf=pxf:
              e.reduce_sum(pxf.unsqueeze(2),
                           cx.bigA[:, c, :].rearrange("p (g w) -> p g w", w=32),
                           axis=AX.X))
        kb.op("vector", lambda e, c=c, pxf=pxf:
              e.tensor_scalar_mul(cx.px[:, c, :], pxf, 1.0 / 32.0),
              inc=kb.v if c == NCH - 1 else None)
    kb.wait("tensor", kb.v)
    emit_linear(kb, cx, W["a_wgk"], 8, cx.px, out3=cx.gk1, nm=4, t_lo=0, t_hi=64, w_bf=wbf(SW, "a_wgk"))
    emit_linear(kb, cx, W["a_wgv"], 8, cx.px, out3=cx.gvf, nm=4, t_lo=0, t_hi=64, w_bf=wbf(SW, "a_wgv"))
    kb.wait("sync", kb.v)
    ngv = kb.dmaT("sync", cx.gvt[:, :, :], cx.gvf[:, :, :])
    smro = Rot([0, 1]); pero = Rot([0, 1]); pnro = Rot([0, 1]); ptro = Rot([0, 1])
    nmro = Rot(list(range(4))); suro = Rot(list(range(4, 8))); rsro = Rot(list(range(8, 12)))
    stc = lambda j: cx.stt[:, j:j + 1]
    for tc in range(4):
        gl0 = 512 * tc
        kb.wait("tensor", kb.g)
        kb.wait("tensor", kb.s)
        kb.wait("tensor", kb.sx)
        emit_linear(kb, cx, W["a_wq"], 8, cx.bigA, out3=cx.q1, nm=8,
                    t_lo=gl0, t_hi=gl0 + 512, out_tl0=0, w_bf=wbf(SW, "a_wq"))
        emit_linear(kb, cx, W["a_wk"], 8, cx.bigA, out3=cx.k1, nm=8,
                    t_lo=gl0, t_hi=gl0 + 512, out_tl0=0, evac_scale=0.125,
                    w_bf=wbf(SW, "a_wk"))
        emit_linear(kb, cx, W["a_wv"], 8, cx.bigA, out3=cx.vf1, nm=8,
                    t_lo=gl0, t_hi=gl0 + 512, out_tl0=0, w_bf=wbf(SW, "a_wv"))
        emit_linear(kb, cx, W["a_wgq"], 8, cx.bigA, out3=cx.gq1, nm=4,
                    t_lo=gl0, t_hi=gl0 + 512, out_tl0=0, w_bf=wbf(SW, "a_wgq"))
        kb.wait("sync", kb.v)
        kb.wait("sync", kb.pe)
        nvt = 0
        for c in range(8):
            nvt = kb.dmaT("sync",
                          cx.vt1[:, :, 128 * c:128 * c + 128],
                          cx.vf1[:, c, :].rearrange("p (tt q) -> p tt q", q=128))
        for tt in range(4):
            mms = []
            for c in range(8):
                mms.append(lambda e, pt, c=c, tt=tt:
                           e.matmul(pt[:, 0:16], cx.q1[:, c, 128 * tt:128 * tt + 128],
                                    cx.glwb[:, c, :], start=(c == 0), stop=(c == 7)))
            ev = (lambda e, pt, tt=tt:
                  e.activation(cx.glT[:, tt, :], pt[:, 0:16], AF.Sigmoid))
            mm_evac(kb, cx.pp, mms, "scalar", ev, extra_mm_waits=[(kb.v, kb.v.n)])
        kg0 = kb.g.n
        for r in (smro, pero, pnro, ptro):
            for j in range(len(r.views)):
                r.done(j, kb.g, kg0)
        for h in range(16):
            pb = (h % 2) * 64
            ch = h // 2
            for qt in range(4):
                w = qt // 2
                p = qt % 2
                glob0 = (w == 0 and tc == 0)
                smj0, smj, smw = smro.next()
                sm = cx.w_sm[smj0]
                qsl = cx.q1[pb:pb + 64, ch, 128 * qt:128 * qt + 128]
                mms = []
                if not glob0:
                    if w == 0:
                        ksrc = cx.kh1[pb:pb + 64, ch, 0:256]
                    else:
                        ksrc = cx.k1[pb:pb + 64, ch, 0:256]
                    mms.append(lambda e, pt, qsl=qsl, ksrc=ksrc:
                               e.matmul(pt[:, 0:256], qsl, ksrc, start=True, stop=True))
                mms.append(lambda e, pt, qsl=qsl,
                           a=cx.k1[pb:pb + 64, ch, 256 * w:256 * w + 256]:
                           e.matmul(pt[:, 256:512], qsl, a, start=True, stop=True))
                mi = (2 + p) if glob0 else p
                for s_, v_ in smw:
                    kb.wait("vector", s_, v_)
                if glob0:
                    kb.op("vector", lambda e, o=sm[:, 0:256]: e.memset(o, NEG))
                    ev = (lambda e, pt, sm=sm, mi=mi:
                          e.scalar_tensor_tensor(sm[:, 256:512], pt[:, 256:512], 1.0,
                                                 cx.wmask[:, mi, 256:512],
                                                 ALU.mult, ALU.add))
                else:
                    ev = (lambda e, pt, sm=sm, mi=mi:
                          e.scalar_tensor_tensor(sm[:, 0:512], pt[:, 0:512], 1.0,
                                                 cx.wmask[:, mi, 0:512],
                                                 ALU.mult, ALU.add))
                mm_evac(kb, cx.pp, mms, "vector", ev, extra_mm_waits=[(kb.g, kg0)])
                nmj0, nmj, nmw = nmro.next()
                for s_, v_ in nmw:
                    kb.wait("vector", s_, v_)
                kb.op("vector", lambda e, o=stc(nmj0), sm=sm:
                      e.reduce_max(o, sm[:, 0:512], axis=AX.X, negate=True), inc=kb.v)
                suj0, suj, suw = suro.next()
                pej0, pej, pew = pero.next()
                peb = cx.w_pe[pej0]
                kb.wait("scalar", kb.v)
                for s_, v_ in pew + suw:
                    kb.wait("scalar", s_, v_)
                kb.op("scalar", lambda e, o=peb[:, 0:512], sm=sm, bi=stc(nmj0),
                      ac=stc(suj0):
                      e.activation(o, sm[:, 0:512], AF.Exp, bias=bi, accum_out=ac),
                      inc=kb.s)
                smro.done(smj, kb.s, kb.s.n)
                nmro.done(nmj, kb.s, kb.s.n)
                rsj0, rsj, rsw = rsro.next()
                kb.wait("vector", kb.s)
                for s_, v_ in rsw:
                    kb.wait("vector", s_, v_)
                kb.op("vector", lambda e, o=stc(rsj0), i=stc(suj0):
                      e.reciprocal(o, i), inc=kb.v)
                suro.done(suj, kb.v, kb.v.n)
                pnj0, pnj, pnw = pnro.next()
                pnb = cx.w_pn[pnj0]
                for s_, v_ in pnw:
                    kb.wait("vector", s_, v_)
                kb.op("vector", lambda e, o=pnb[:, 0:512], i=peb[:, 0:512],
                      r=stc(rsj0), g=cx.glT[:, qt, h:h + 1]:
                      e.tensor_scalar(o, i, r, g, ALU.mult, ALU.mult), inc=kb.v)
                pero.done(pej, kb.v, kb.v.n)
                rsro.done(rsj, kb.v, kb.v.n)
                ptj0, ptj, ptw = ptro.next()
                ptb = cx.w_pt[ptj0]
                kb.wait("sync", kb.v)
                for s_, v_ in ptw:
                    kb.wait("sync", s_, v_)
                ndt = kb.dmaT("sync",
                              ptb[:, 0:512].rearrange("p (kb q) -> p kb q", q=128),
                              pnb[:, 0:512].rearrange("q (kb p) -> q kb p", p=128))
                pnro.done(pnj, kb.dt, ndt)
                sts = [2 * (w - 1) + j for j in range(4)]
                mms = []
                valid = [st for st in sts if not (st < 0 and tc == 0)]
                for idx, st in enumerate(valid):
                    j = sts.index(st)
                    if st < 0:
                        sth = 2 + st
                        flat = sth * 1024 + 64 * h
                        vsl = cx.vh1[:, flat // 256, flat % 256:flat % 256 + 64]
                    else:
                        vsl = cx.vt1[:, st, 64 * h:64 * h + 64]
                    mms.append(lambda e, pt, j=j, vsl=vsl, s=(idx == 0),
                               q=(idx == len(valid) - 1), ptb=ptb:
                               e.matmul(pt[0:64, 0:128], vsl,
                                        ptb[:, 128 * j:128 * j + 128],
                                        start=s, stop=q))
                ev = (lambda e, pt, pb=pb, ch=ch, qt=qt:
                      e.copy(cx.att1[pb:pb + 64, ch, 128 * qt:128 * qt + 128],
                             pt[0:64, 0:128]))
                mm_evac(kb, cx.pp, mms, "scalar", ev, extra_mm_waits=[(kb.dt, ndt)])
                ptro.done(ptj, kb.pe, kb.pe.n)
        kb.wait("gpsimd", kb.pe)
        kb.wait("gpsimd", kb.s)
        kb.op("gpsimd", lambda e: e.tensor_copy(cx.kh1[:, :, :], cx.k1[:, :, 256:512]))
        kb.op("gpsimd", lambda e: e.tensor_copy(
            cx.vh1[:, :, :],
            cx.vt1[:, 2:4, :].rearrange("p a (b t) -> p (a b) t", t=256)), inc=kb.g)
        for lh in range(8):
            pb = (lh % 2) * 64
            ch = lh // 2
            for qt in range(4):
                gqt = 4 * tc + qt
                smj0, smj, smw = smro.next()
                sm = cx.w_sm[smj0]
                mms = [lambda e, pt, pb=pb, ch=ch, qt=qt:
                       e.matmul(pt[:, 0:64], cx.gq1[pb:pb + 64, ch, 128 * qt:128 * qt + 128],
                                cx.gk1[pb:pb + 64, ch, :], start=True, stop=True)]
                for s_, v_ in smw:
                    kb.wait("vector", s_, v_)
                ev = (lambda e, pt, sm=sm, gqt=gqt:
                      e.scalar_tensor_tensor(sm[:, 0:64], pt[:, 0:64], 0.125,
                                             cx.lmask[:, gqt, :], ALU.mult, ALU.add))
                mm_evac(kb, cx.pp, mms, "vector", ev, extra_mm_waits=[(kb.g, kb.g.n)])
                nmj0, nmj, nmw = nmro.next()
                for s_, v_ in nmw:
                    kb.wait("vector", s_, v_)
                kb.op("vector", lambda e, o=stc(nmj0), sm=sm:
                      e.reduce_max(o, sm[:, 0:64], axis=AX.X, negate=True), inc=kb.v)
                suj0, suj, suw = suro.next()
                pej0, pej, pew = pero.next()
                peb = cx.w_pe[pej0]
                kb.wait("scalar", kb.v)
                for s_, v_ in pew + suw:
                    kb.wait("scalar", s_, v_)
                kb.op("scalar", lambda e, o=peb[:, 0:64], sm=sm, bi=stc(nmj0),
                      ac=stc(suj0):
                      e.activation(o, sm[:, 0:64], AF.Exp, bias=bi, accum_out=ac),
                      inc=kb.s)
                smro.done(smj, kb.s, kb.s.n)
                nmro.done(nmj, kb.s, kb.s.n)
                rsj0, rsj, rsw = rsro.next()
                kb.wait("vector", kb.s)
                for s_, v_ in rsw:
                    kb.wait("vector", s_, v_)
                kb.op("vector", lambda e, o=stc(rsj0), i=stc(suj0):
                      e.reciprocal(o, i), inc=kb.v)
                suro.done(suj, kb.v, kb.v.n)
                pnj0, pnj, pnw = pnro.next()
                pnb = cx.w_pn[pnj0]
                for s_, v_ in pnw:
                    kb.wait("vector", s_, v_)
                kb.op("vector", lambda e, o=pnb[:, 0:64], i=peb[:, 0:64], r=stc(rsj0):
                      e.tensor_scalar_mul(o, i, r), inc=kb.v)
                pero.done(pej, kb.v, kb.v.n)
                rsro.done(rsj, kb.v, kb.v.n)
                ptj0, ptj, ptw = ptro.next()
                ptb = cx.w_pt[ptj0]
                kb.wait("sync", kb.v)
                for s_, v_ in ptw:
                    kb.wait("sync", s_, v_)
                ndt = kb.dmaT("sync", ptb[:, 0:128], pnb[:, 0:128])
                pnro.done(pnj, kb.dt, ndt)
                mms = [lambda e, pt, lh=lh, ptb=ptb:
                       e.matmul(pt[0:64, 0:128],
                                cx.gvt[0:64, lh // 2, (lh % 2) * 64:(lh % 2) * 64 + 64],
                                ptb[0:64, 0:128], start=True, stop=True)]
                ev = (lambda e, pt, pb=pb, ch=ch, qt=qt:
                      e.copy(cx.lat1[pb:pb + 64, ch, 128 * qt:128 * qt + 128],
                             pt[0:64, 0:128]))
                mm_evac(kb, cx.pp, mms, "scalar", ev,
                        extra_mm_waits=[(kb.dt, ndt), (kb.dt, ngv)])
                ptro.done(ptj, kb.pe, kb.pe.n)
        kb.wait("tensor", kb.s)
        for c in range(8):
            mms = [lambda e, pt, c=c:
                   e.matmul(pt[:, 0:512], cx.grwB[:, c, :], cx.q1[:, c, :],
                            start=True, stop=True)]
            grb = cx.w_sm[0]
            ev = (lambda e, pt, grb=grb:
                  e.activation(grb[:, 0:512], pt[:, 0:512], AF.Sigmoid))
            mm_evac(kb, cx.pp, mms, "scalar", ev,
                    extra_mm_waits=[(kb.g, kb.g.n), (kb.s, kb.s.n)])
            def remev(e, pt, mi_, tl_, w_, dst, c=c, grb=grb):
                return e.tensor_tensor(cx.w_pe[0][:, 0:512], pt[:, 0:512],
                                       grb[:, 0:512], ALU.mult)
            emit_linear(kb, cx, W["a_wro"], 4, cx.lat1, m0=c, nm=1,
                        t_lo=0, t_hi=512, evac_eng="vector", w_bf=wbf(SW, "a_wro"),
                        w_ready=[(kb.s, kb.s.n), (kb.g, kb.g.n)], evac_fn=remev)
            kb.wait("gpsimd", kb.v)
            kb.op("gpsimd", lambda e, c=c:
                  e.tensor_tensor(cx.att1[:, c, :], cx.att1[:, c, :],
                                  cx.w_pe[0][:, 0:512], ALU.add), inc=kb.g)
        kb.wait("sync", kb.g)
        kb.dma("sync", att_sv.rearrange("(c p) t -> p c t", p=128)[:, :, gl0:gl0 + 512],
               cx.att1[:, :, :])
    natt = kb.dma("sync", cx.bigA[:, :, :], att_sv.rearrange("(c p) t -> p c t", p=128))
    kb.wait("tensor", kb.sx, natt)
    for tl in range(0, T, 512):
        kb.wait("vector", kb.g)
        emit_linear(kb, cx, W["a_wout"], 8, cx.bigA, nm=8, t_lo=tl, t_hi=tl + 512,
                    w_bf=wbf(SW, "a_wout"),
                    evac_fn=lambda e, pt, mi, tl_, w_, dst:
                    e.tensor_copy(cx.f32scr[:, mi, 0:w_], pt[:, 0:w_]))
        kb.wait("scalar", kb.v)
        kb.wait("gpsimd", kb.v)
        emit_ln(kb, cx, cx.f32scr, tl, tl + 512, resid3=cx.xbuf, src_tl0=0)


def host_l1f_consts(inp):
    d = {}
    for nm in ("wf", "uf", "wi", "ui", "wc", "uc", "wo", "uo"):
        d["g_" + nm] = np.ascontiguousarray(np.asarray(inp["L1_ffn_" + nm], np.float32).T)
    for nm in ("bf", "bi", "bc", "bo", "in_c"):
        d["g_" + nm] = np.ascontiguousarray(
            np.asarray(inp["L1_ffn_" + nm], np.float32).reshape(8, 128).T)
    return d


def declare_l1f(nc):
    W = {}
    for nm in ("g_wf", "g_uf", "g_wi", "g_ui", "g_wc", "g_uc", "g_wo", "g_uo"):
        W[nm] = nc.dram_tensor(nm, [D, D], F32, kind="ExternalInput").ap()
    for nm in ("g_bf", "g_bi", "g_bc", "g_bo", "g_in_c"):
        W[nm] = nc.dram_tensor(nm, [128, 8], F32, kind="ExternalInput").ap()
    return W


def make_ctx_l1f(kb, cx):
    cx.c_st = cx.f32scr[:, 0:4, :].rearrange("p a (b t) -> p (a b) t", t=256)
    cx.t1 = cx.f32scr[:, 4:8, :].rearrange("p a (b t) -> p (a b) t", t=256)
    cx.t2 = cx.f32scr[:, 8:12, :].rearrange("p a (b t) -> p (a b) t", t=256)
    cx.uf_s = cx.bigB[:, 0:4, :].rearrange("p a (b t) -> p (a b) t", t=1024)
    cx.uf_r = cx.bigB[:, 0:4, :].rearrange("p a (b m) -> p (a b) m", m=1024)
    cx.ui_r = cx.bigB[:, 4:8, :].rearrange("p a (b m) -> p (a b) m", m=1024)
    cx.uc_r = cx.bigC[:, 0:4, :].rearrange("p a (b m) -> p (a b) m", m=1024)
    cx.g_f = cx.bigC[:, 4, :].rearrange("p (a t) -> p a t", t=256)
    cx.g_i = cx.bigC[:, 5, :].rearrange("p (a t) -> p a t", t=256)
    cx.g_g = cx.bigC[:, 6, :].rearrange("p (a t) -> p a t", t=256)
    cx.c_bf = cx.bigC[:, 7, :].rearrange("p (a t) -> p a t", t=256)
    cx.bcols = kb.sbuf("bcols", [128, 5, 8], F32)
    return cx


def emit_resident_u(kb, cx, W, SW=None):
    n = 0
    for key, dst in (("g_uf", cx.uf_r), ("g_ui", cx.ui_r), ("g_uc", cx.uc_r)):
        bf = wbf(SW, key)
        if bf is not None:
            kb.wait("sync", *bf[1])
            wv = bf[0].rearrange("(c p) m -> p c m", p=128)
            n = kb.dma("sync", dst, wv, sem=kb.ld)
        else:
            wv = W[key].rearrange("(c p) m -> p c m", p=128)
            n = kb.dma("gpsimd", dst, wv)
    return n


def emit_l1_ffn(kb, cx, W, SW=None):
    for j, nm in enumerate(("g_bf", "g_bi", "g_bc", "g_bo", "g_in_c")):
        kb.dma("gpsimd", cx.bcols[:, j, :], W[nm])
    emit_ln(kb, cx, cx.xbuf, 0, T, dst3=cx.bigA)
    nur = emit_resident_u(kb, cx, W, SW)
    kb.wait("gpsimd", kb.ld)
    for c in range(NCH):
        kb.op("gpsimd", lambda e, c=c:
              e.tensor_copy(cx.c_st[:, c, :],
                            cx.bcols[:, 4, c:c + 1].broadcast_to([128, 256])), inc=kb.g)
    for step in range(8):
        s0 = 256 * step
        kb.wait("gpsimd", kb.v)
        for c in range(NCH):
            kb.op("gpsimd", lambda e, c=c:
                  e.tensor_copy(cx.c_bf[:, c, :], cx.c_st[:, c, :]),
                  inc=kb.g if c == NCH - 1 else None)
        for gate, wname, uname, bj, act in (
                (cx.g_f, "g_wf", cx.uf_r, 0, AF.Sigmoid),
                (cx.g_i, "g_wi", cx.ui_r, 1, AF.Sigmoid),
                (cx.g_g, "g_wc", cx.uc_r, 2, AF.Tanh)):
            bf = wbf(SW, wname)
            weng = "gpsimd" if bf is None else "sync"
            wv = (W[wname] if bf is None else bf[0]).rearrange("(c p) m -> p c m", p=128)
            if bf is not None and step == 0:
                kb.wait("sync", *bf[1])
            for mi in range(8):
                wsl, wj, wpend = cx.wrot.next()
                for s_, v_ in wpend:
                    kb.wait(weng, s_, v_)
                nld = kb.dma(weng,
                             wsl[:, 0:1024].rearrange("p (c m) -> p c m", m=128),
                             wv[:, :, 128 * mi:128 * mi + 128], sem=kb.ld)
                wslv = wsl[:, 0:1024].rearrange("p (c m) -> p c m", m=128)
                mms = []
                for kc in range(8):
                    mms.append(lambda e, pt, kc=kc, wslv=wslv, mi=mi, s0=s0:
                               e.matmul(pt[:, 0:256], wslv[:, kc, :],
                                        cx.bigA[:, kc, s0:s0 + 256],
                                        start=(kc == 0), stop=False))
                for kc in range(8):
                    mms.append(lambda e, pt, kc=kc, uname=uname, mi=mi:
                               e.matmul(pt[:, 0:256], uname[:, kc, 128 * mi:128 * mi + 128],
                                        cx.c_bf[:, kc, :],
                                        start=False, stop=(kc == 7)))
                ev = (lambda e, pt, gate=gate, mi=mi, bj=bj, act=act:
                      e.activation(gate[:, mi, :], pt[:, 0:256], act,
                                   bias=cx.bcols[:, bj, mi:mi + 1]))
                mm_evac(kb, cx.pp, mms, "scalar", ev,
                        extra_mm_waits=[(kb.ld, nld), (kb.g, kb.g.n),
                                        (kb.ld, nur)])
                cx.wrot.done(wj, kb.pe, kb.pe.n)
        kb.wait("vector", kb.s)
        for c in range(NCH):
            kb.op("vector", lambda e, c=c:
                  e.tensor_tensor(cx.t1[:, c, :], cx.g_i[:, c, :], cx.g_g[:, c, :],
                                  ALU.mult))
            kb.op("vector", lambda e, c=c:
                  e.tensor_tensor(cx.t2[:, c, :], cx.g_f[:, c, :], cx.c_st[:, c, :],
                                  ALU.mult))
            kb.op("vector", lambda e, c=c:
                  e.tensor_tensor(cx.c_st[:, c, :], cx.t1[:, c, :], cx.t2[:, c, :],
                                  ALU.add), inc=kb.v if c == NCH - 1 else None)
    hT = cx.bigB[:, :, :]
    bfo = wbf(SW, "g_wo")
    bfu = wbf(SW, "g_uo")
    weng_o = "gpsimd" if bfo is None else "sync"
    wvo = (W["g_wo"] if bfo is None else bfo[0]).rearrange("(c p) m -> p c m", p=128)
    uvo = (W["g_uo"] if bfu is None else bfu[0]).rearrange("(c p) m -> p c m", p=128)
    if bfo is not None:
        kb.wait("sync", *bfo[1])
        kb.wait("sync", *bfu[1])
    kb.wait("gpsimd", kb.v)
    for c in range(NCH):
        kb.op("gpsimd", lambda e, c=c:
              e.tensor_copy(cx.c_bf[:, c, :], cx.c_st[:, c, :]),
              inc=kb.g if c == NCH - 1 else None)
    for mi in range(8):
        usl, uj, upend = cx.wbig.next()
        for s_, v_ in upend:
            kb.wait(weng_o, s_, v_)
        nld2 = kb.dma(weng_o, usl[:, 0:1024].rearrange("p (c m) -> p c m", m=128),
                      uvo[:, :, 128 * mi:128 * mi + 128], sem=kb.ld)
        uslv = usl[:, 0:1024].rearrange("p (c m) -> p c m", m=128)
        mms = []
        for kc in range(8):
            mms.append(lambda e, pt, kc=kc, uslv=uslv:
                       e.matmul(pt[:, 0:256], uslv[:, kc, :], cx.c_bf[:, kc, :],
                                start=(kc == 0), stop=(kc == 7)))
        zuo = cx.t1[:, 0:1, :]
        ev = (lambda e, pt: e.tensor_copy(cx.t1[:, 0, :], pt[:, 0:256]))
        mm_evac(kb, cx.pp, mms, "vector", ev,
                extra_mm_waits=[(kb.ld, nld2), (kb.g, kb.g.n)])
        cx.wbig.done(uj, kb.pe, kb.pe.n)
        wsl, wj, wpend = cx.wrot.next()
        for s_, v_ in wpend:
            kb.wait(weng_o, s_, v_)
        nld = kb.dma(weng_o, wsl[:, 0:1024].rearrange("p (c m) -> p c m", m=128),
                     wvo[:, :, 128 * mi:128 * mi + 128], sem=kb.ld)
        wslv = wsl[:, 0:1024].rearrange("p (c m) -> p c m", m=128)
        for tl in range(0, T, 512):
            mms = []
            for kc in range(8):
                mms.append(lambda e, pt, kc=kc, wslv=wslv, tl=tl:
                           e.matmul(pt[:, 0:512], wslv[:, kc, :],
                                    cx.bigA[:, kc, tl:tl + 512],
                                    start=(kc == 0), stop=(kc == 7)))
            def hev(e, pt, mi=mi, tl=tl):
                zb = cx.t1[:, 0, :].unsqueeze(1).broadcast_to([128, 2, 256])
                return e.tensor_tensor(
                    cx.t2[:, 0:2, :], pt[:, 0:512].rearrange("p (a t) -> p a t", t=256),
                    zb, ALU.add)
            mm_evac(kb, cx.pp, mms, "vector", hev,
                    extra_mm_waits=[(kb.ld, nld), (kb.v, kb.v.n), (kb.s, kb.s.n)])
            kb.wait("scalar", kb.v)
            kb.op("scalar", lambda e, mi=mi, tl=tl:
                  e.activation(hT[:, mi, tl:tl + 512],
                               cx.t2[:, 0:2, :].rearrange("p a t -> p (a t)"),
                               AF.Tanh, bias=cx.bcols[:, 3, mi:mi + 1]), inc=kb.s)
        cx.wrot.done(wj, kb.pe, kb.pe.n)
    for tl in range(0, T, 512):
        kb.wait("scalar", kb.s)
        emit_ln(kb, cx, hT, tl, tl + 512, resid3=cx.xbuf, src_bf=True)


def build_l1a():
    nc = bass.Bass("TRN2", target_bir_lowering=False, debug=False)
    xT = nc.dram_tensor("xT", [D, T], F32, kind="ExternalInput").ap()
    W = declare_l1a(nc)
    out = nc.dram_tensor("xo", [D, T], F32, kind="ExternalOutput").ap()
    att_sv = nc.dram_tensor("att_sv", [D, T], BF16, kind="Internal").ap()
    with ExitStack() as stack:
        kb = KB(nc, stack)
        cx = make_ctx(kb)
        cx = make_ctx_ffn(kb, cx)
        cx = make_ctx_l1a(kb, cx)
        emit_const_init(kb, cx)
        emit_l1a_init(kb, cx, W)
        kb.dma("sync", cx.xbuf[:, :, :], xT.rearrange("(c p) t -> p c t", p=128))
        for eng in ("scalar", "tensor", "vector", "gpsimd"):
            kb.wait(eng, kb.sx)
            kb.wait(eng, kb.ld)
        kb.wait("tensor", kb.v)
        emit_l1_attn(kb, cx, W, att_sv)
        kb.wait("sync", kb.g)
        kb.wait("sync", kb.v)
        kb.dma("sync", out.rearrange("(c p) t -> p c t", p=128), cx.xbuf[:, :, :])
        kb.finalize([(kb.sx, kb.sx.n), (kb.ld, kb.ld.n), (kb.dt, kb.dt.n)])
    return nc


def build_l1f():
    nc = bass.Bass("TRN2", target_bir_lowering=False, debug=False)
    xT = nc.dram_tensor("xT", [D, T], F32, kind="ExternalInput").ap()
    W = declare_l1f(nc)
    out = nc.dram_tensor("xo", [D, T], F32, kind="ExternalOutput").ap()
    with ExitStack() as stack:
        kb = KB(nc, stack)
        cx = make_ctx(kb)
        cx = make_ctx_ffn(kb, cx)
        cx = make_ctx_l1f(kb, cx)
        emit_const_init(kb, cx)
        kb.dma("sync", cx.xbuf[:, :, :], xT.rearrange("(c p) t -> p c t", p=128))
        for eng in ("scalar", "tensor", "vector", "gpsimd"):
            kb.wait(eng, kb.sx)
            kb.wait(eng, kb.ld)
        kb.wait("tensor", kb.v)
        emit_l1_ffn(kb, cx, W)
        kb.wait("sync", kb.g)
        kb.wait("sync", kb.v)
        kb.dma("sync", out.rearrange("(c p) t -> p c t", p=128), cx.xbuf[:, :, :])
        kb.finalize([(kb.sx, kb.sx.n), (kb.ld, kb.ld.n), (kb.dt, kb.dt.n)])
    return nc


def _phase(kb, eng):
    for s in (kb.pe, kb.v, kb.s, kb.g, kb.ld, kb.sx, kb.dt):
        if kb.ms.get(eng) is s:
            continue
        kb.wait(eng, s)


def emit_l0a_init(kb, cx, W):
    kb.dma("gpsimd", cx.cmask[:, :], W["cm"])


def emit_reshard(kb, cx, cin, cout):
    kb.wait("sync", kb.g)
    kb.wait("sync", kb.v)
    n1 = kb.dma("sync", cin.rearrange("(c p) t -> p c t", p=128), cx.xbuf[:, :, :])
    kb.wait("gpsimd", kb.sx, n1)
    cc = kb.sem("ccol")
    cc.n += 1
    kb.rec("gpsimd", lambda e, h=cc.h: e.collective_compute(
        "AllGather", ALU.bypass, replica_groups=[list(range(NCORES))],
        ins=[cin.opt()], outs=[cout.opt()]).then_inc(h, 1))
    kb.wait("gpsimd", cc, cc.n)
    nc = kb.nc
    for b in range(NCORES):
        def ld(e, b=b):
            pid = nc.partition_id(engines=[mybir.EngineType.Pool])
            view = cout.rearrange("(b c p) t -> b p c t", b=NCORES, p=128)[b]
            return e.dma_start(out=cx.xbuf[:, :, 256 * b:256 * b + 256],
                               in_=view[:, :, bass.ts(pid, 256)])
        kb.ld.n += 16
        kb.rec("gpsimd", lambda e, ld=ld: ld(e).then_inc(kb.ld.h, 16))
    kb.wait("gpsimd", kb.ld)


def build_fused():
    nc = bass.Bass("TRN2", target_bir_lowering=False, debug=False)
    xT = nc.dram_tensor("xT", [D, T], F32, kind="ExternalInput").ap()
    Wa = {}
    Wa["wcd"] = nc.dram_tensor("wcd", [D, 512], F32, kind="ExternalInput").ap()
    Wa["wqu"] = nc.dram_tensor("wqu", [512, D], F32, kind="ExternalInput").ap()
    Wa["wkk"] = nc.dram_tensor("wkk", [512, 128], F32, kind="ExternalInput").ap()
    Wa["wv"] = nc.dram_tensor("wv", [512, 64], F32, kind="ExternalInput").ap()
    Wa["wcp"] = nc.dram_tensor("wcp", [D, D], F32, kind="ExternalInput").ap()
    Wa["cm"] = nc.dram_tensor("cm", [128, 640], F32, kind="ExternalInput").ap()
    Wf = declare_l0ffn(nc)
    W1a = declare_l1a(nc)
    W1f = declare_l1f(nc)
    out = nc.dram_tensor("xo", [D, T], F32, kind="ExternalOutput").ap()
    x1sv = nc.dram_tensor("x1sv", [D, T], F32, kind="Internal").ap()
    att_sv = nc.dram_tensor("att_sv", [D, T], BF16, kind="Internal").ap()
    cin = nc.dram_tensor("cin", [D, T], F32, kind="Internal")
    cout = nc.dram_tensor("cout", [NCORES * D, T], F32, kind="Internal")
    with ExitStack() as stack:
        kb = KB(nc, stack)
        cx = make_ctx(kb)
        cx = make_ctx_ffn(kb, cx)
        cx = make_ctx_l1a(kb, cx)
        cx = make_ctx_l1f(kb, cx)
        emit_const_init(kb, cx)
        emit_l0a_init(kb, cx, Wa)
        SW = None
        kb.dma("sync", cx.xbuf[:, :, :], xT.rearrange("(c p) t -> p c t", p=128))
        for eng in ("scalar", "tensor", "vector", "gpsimd"):
            kb.wait(eng, kb.sx)
        kb.wait("tensor", kb.v)
        kb.wait("vector", kb.ld)
        kb.wait("scalar", kb.ld)
        kb.wait("tensor", kb.ld)
        emit_l0_attn(kb, cx, Wa, SW)
        _phase(kb, "gpsimd")
        emit_ffn_init(kb, cx, Wf)
        kb.wait("vector", kb.ld)
        kb.wait("tensor", kb.ld)
        emit_l0f_block(kb, cx, Wf, x1sv, 6, SW=SW)
        _phase(kb, "gpsimd")
        emit_l1a_init(kb, cx, W1a)
        kb.wait("vector", kb.ld)
        kb.wait("tensor", kb.ld)
        emit_l1_attn(kb, cx, W1a, att_sv, SW=SW)
        _phase(kb, "sync")
        emit_reshard(kb, cx, cin.ap(), cout.ap())
        _phase(kb, "gpsimd")
        _phase(kb, "vector")
        _phase(kb, "scalar")
        _phase(kb, "tensor")
        emit_l1_ffn(kb, cx, W1f, SW=SW)
        kb.wait("sync", kb.g)
        kb.wait("sync", kb.v)
        kb.dma("sync", out.rearrange("(c p) t -> p c t", p=128), cx.xbuf[:, :, :])
        kb.finalize([(kb.sx, kb.sx.n), (kb.ld, kb.ld.n), (kb.dt, kb.dt.n),
                     (kb.stg, kb.stg.n)])
    return nc


def kernel(**inputs):
    x = _to_f32(inputs["L0_attn_x"])
    nc = build_fused()
    base = {}
    base.update(hostf_l0ffn_consts(inputs))
    base.update(host_l1a_consts(inputs))
    base.update(host_l1f_consts(inputs))
    maps = []
    for b in range(NCORES):
        m = dict(base)
        m.update(l0a_inputs(x[b].T, inputs))
        maps.append(m)
    res = run_bass_kernel_spmd(nc, maps, core_ids=list(range(NCORES)))
    out = np.empty((8, T, D), np.float32)
    for j in range(NCORES):
        oj = res.results[j]["xo"].T.reshape(8, 256, D)
        out[:, 256 * j:256 * (j + 1), :] = oj
    return out
